# version 05
# baseline (speedup 1.0000x reference)
import math
import os
import numpy as np
import ml_dtypes
import concourse.bass as bass
import concourse.mybir as mybir
import concourse.bass_utils as _bu
from concourse.bass_utils import run_bass_kernel_spmd

if not getattr(_bu, "_ldwopt_patched", False):
    _bu._ldwopt_patched = True
    _orig_run_command = _bu.run_command

    def _run_command_ldwopt(cmd, **kw):
        cmd = ["--enable-ldw-opt=true" if c == "--enable-ldw-opt=false" else c
               for c in cmd]
        return _orig_run_command(cmd, **kw)

    _bu.run_command = _run_command_ldwopt

F32 = mybir.dt.float32
BF16 = mybir.dt.bfloat16
U8 = mybir.dt.uint8
AX = mybir.AxisListType.X
AF = mybir.ActivationFunctionType
ALU = mybir.AluOpType
bfnp = ml_dtypes.bfloat16

S, D, NH, NKV, HD, WIN, HF = 4096, 2048, 16, 4, 128, 1024, 5120
NT, SS, NSS = 32, 512, 8
NKC = D // 128
NHC = HF // 128
ISQ = 1.0 / math.sqrt(HD)
GQ = NH // NKV

ENGS = ("sync", "vector", "scalar", "tensor", "gpsimd")


class Ctr:
    __slots__ = ("h", "n")

    def __init__(self, h):
        self.h = h
        self.n = 0

    def inc(self, k=1):
        self.n += k
        return self.n


def build_nc():
    nc = bass.Bass("TRN2", target_bir_lowering=False,
                   detect_race_conditions=False)
    din = lambda n, s, d=F32: nc.dram_tensor(n, s, d, kind="ExternalInput").ap()
    _dbg = os.environ.get("KDBG") == "1"
    dint = lambda n, s, d: nc.dram_tensor(
        n, s, d, kind="ExternalOutput" if _dbg else "Internal").ap()

    xin = din("xin", [S, D])
    wq = din("wq", [D, D])
    wk = din("wk", [D, NKV * HD])
    wv = din("wv", [D, NKV * HD])
    wo = din("wo", [D, D])
    wfc = din("wfc", [HF, D])
    wpj = din("wpj", [D, HF])
    cosT_c = din("cosT_c", [128, S], BF16)
    sinsT_c = din("sinsT_c", [128, S], BF16)
    mcT_c = din("mcT_c", [128, 128], BF16)
    mfT_c = din("mfT_c", [128, 128], BF16)
    ident_c = din("ident_c", [128, 128], BF16)
    onesrb_c = din("onesrb_c", [1, 128], BF16)
    onesrf_c = din("onesrf_c", [1, 128], F32)
    onescf_c = din("onescf_c", [128, 1], F32)
    epsc_c = din("epsc_c", [128, 1], F32)
    y = nc.dram_tensor("y", [S, D], F32, kind="ExternalOutput").ap()

    xnT_hbm = dint("xnT_hbm", [NSS, 128, NKC * SS], BF16)
    ctx_hbm = dint("ctx_hbm", [S, D], BF16)
    x1_hbm = dint("x1_hbm", [S, D], F32)
    h_hbm = dint("h_hbm", [NSS, 128, NHC * SS], BF16)
    sfc_hbm = dint("sfc_hbm", [HF, D], BF16)
    sp_hbm = dint("sp_hbm", [D, HF], BF16)

    from contextlib import ExitStack
    es = ExitStack()

    ARENA = 211968
    arena = es.enter_context(nc.sbuf_tensor("arena", [128, ARENA], U8))
    arena_addr = nc.lookup_mloc(arena).addr

    def place(name, shape, dtype, off):
        assert off % 32 == 0, name
        per = int(np.prod(shape[1:])) * mybir.dt.size(dtype)
        assert off + per <= ARENA, (name, off, per)
        h = nc.virtual_sbuf_tensor(name, list(shape), dtype)
        nc._tensor(h.name, h.shape, h.dtype, type="SB")
        mloc = nc.lookup_mloc(h)
        mloc.allocated = True
        mloc.addr = arena_addr + off
        return h

    w0t = [place(f"w0t{i}", [128, 2048], F32, i * 8192) for i in range(2)]
    s0t = [place(f"s0t{i}", [128, 2048], BF16, 16384 + i * 4096) for i in range(2)]
    acc0 = place("acc0", [128, 160], F32, 24576)
    s_xf = [place(f"s_xf{i}", [128, D], F32, i * 8192) for i in range(2)]
    s_mrow = place("s_mrow", [1, 2048], BF16, 0)
    s_P2 = place("s_P2", [128, 1152], BF16, 4096)
    s_mcols = [place(f"s_mcols{i}", [128, 32], BF16, 8192 + i * 64) for i in range(2)]
    s_negrow32 = place("s_negrow32", [1, 32], BF16, 8320)
    s_negrow = place("s_negrow", [1, 16], BF16, 8384)
    s_mbias = [place(f"s_mbias{i}", [128, 16], F32, 8448 + i * 64) for i in range(2)]
    s_lnr = [place(f"s_lnr{i}", [128, 1], F32, 8576 + i * 32) for i in range(2)]
    s_r = [place(f"s_r{i}", [128, 1], F32, 8640 + i * 32) for i in range(2)]
    s_qtmp = [place(f"s_qtmp{i}", [128, SS], BF16, 9216 + i * 1024) for i in range(2)]
    s_qshf = [place(f"s_qshf{i}", [128, SS], BF16, 11264 + i * 1024) for i in range(2)]
    s_t1 = place("s_t1", [128, SS], BF16, 13312)
    s_t2 = place("s_t2", [128, SS], BF16, 14336)
    s_wk = place("s_wk", [128, NKC * 512], BF16, 16384)
    s_wv = place("s_wv", [128, NKC * 512], BF16, 32768)
    s_QT = place("s_QT", [128, NH * SS], BF16, 16384)
    s_xnT = place("s_xnT", [128, NKC * SS], BF16, 32768)
    s_ctxT = place("s_ctxT", [128, NH * SS], BF16, 16384)
    s_K = place("s_K", [128, NKV * S], BF16, 49152)
    s_V = place("s_V", [128, NT * NKV * 129], BF16, 81920)
    s_stage = [place(f"s_stage{i}", [128, D], F32, 49152 + i * 8192) for i in range(2)]
    s_ht = [place(f"s_ht{i}", [128, NHC * 128], BF16, 180736 + i * 10240) for i in range(2)]
    s_yst = place("s_yst", [128, D], F32, 201216)
    s_wq = place("s_wq", [128, NKC * D], BF16, 115200)
    s_wo = place("s_wo", [128, NKC * D], BF16, 115200)
    s_xnsq1 = place("s_xnsq1", [128, D], BF16, 115200)
    s_st1 = place("s_st1", [128, 8], F32, 119296)
    s_ktmp = [place(f"s_ktmp{i}", [128, SS], BF16, 119424 + i * 1024) for i in range(2)]
    s_kshf = [place(f"s_kshf{i}", [128, SS], BF16, 121472 + i * 1024) for i in range(2)]
    s_kt1 = place("s_kt1", [128, SS], BF16, 123520)
    s_kt2 = place("s_kt2", [128, SS], BF16, 124544)
    s_xnT1 = place("s_xnT1", [128, NKC * SS], BF16, 126976)
    s_SfcT = place("s_SfcT", [128, NKC * HF], BF16, 16384)
    s_SpT = place("s_SpT", [128, NHC * D], BF16, 16384)
    s_cosT = place("s_cosT", [128, S], BF16, 180736)
    s_sinsT = place("s_sinsT", [128, S], BF16, 188928)
    s_xn2T = place("s_xn2T", [128, NKC * SS], BF16, 180736)
    s_id = place("s_id", [128, 128], BF16, 197120)
    s_mcT = place("s_mcT", [128, 128], BF16, 197376)
    s_mfT = place("s_mfT", [128, 128], BF16, 197632)
    s_onesrb = place("s_onesrb", [1, 128], BF16, 197888)
    s_onesrf = place("s_onesrf", [1, 128], F32, 198144)
    s_onescf = place("s_onescf", [128, 1], F32, 198656)
    s_eps = place("s_eps", [128, 1], F32, 198688)
    sc0 = place("sc0", [1, 16], F32, 198720)
    scB = place("scB", [128, 8], F32, 198784)
    s_P = [place(f"s_P{i}", [128, 1152], BF16, 198912 + i * 2304) for i in range(2)]
    s_P.append(s_P2)
    s_ctx = [place(f"s_ctx{i}", [128, D], BF16, 203520 + i * 4096) for i in range(2)]
    s_hst = [place(f"s_hst{i}", [128, 2048], BF16, 198912 + i * 4096) for i in range(2)]
    s_xnsq3 = place("s_xnsq3", [128, D], BF16, 207104)
    s_st3 = place("s_st3", [128, 8], F32, 211232)

    pSC = [es.enter_context(nc.psum_tensor(f"pSC{i}", [128, 1024], F32)) for i in range(2)]
    pCX = [es.enter_context(nc.psum_tensor(f"pCX{i}", [128, 512], F32)) for i in range(2)]
    pLNB = es.enter_context(nc.psum_tensor("pLNB", [128, 1024], BF16))
    pAUX = es.enter_context(nc.psum_tensor("pAUX", [128, 512], F32))

    sems = {}
    ops = {e: [] for e in ENGS}

    def C(name):
        if name not in sems:
            sems[name] = Ctr(es.enter_context(nc.semaphore(name)))
        return sems[name]

    def E(eng, fn):
        ops[eng].append(fn)

    def W(eng, ctr, val):
        if val <= 0:
            return
        E(eng, lambda e, s=ctr.h, v=val: e.wait_ge(s, v))

    def EI(eng, ctr, fn, k=1):
        ctr.inc(k)
        E(eng, lambda e, f=fn, s=ctr.h, kk=k: f(e).then_inc(s, kk))

    class Res:

        def __init__(self):
            self.reqs = []

        def acquire(self, eng):
            for c, v in self.reqs:
                W(eng, c, v)
            self.reqs = []

        def release(self, ctr):
            self.reqs.append((ctr, ctr.n))

    rSC = [Res(), Res()]
    rCX = [Res(), Res()]
    rLNB = Res()
    rAUX = Res()

    FEN = C("FEN")
    fence_round = [0]

    def fence():
        fence_round[0] += 1
        tgt = 5 * fence_round[0]
        for eng in ENGS:
            E(eng, lambda e, s=FEN.h: e.nop().then_inc(s, 1))
        FEN.inc(5)
        for eng in ENGS:
            W(eng, FEN, tgt)

    _PH = int(os.environ.get("KPH", "9"))

    def _finish():
        with nc.Block() as block:
            @block.sync
            def _(e):
                for f in ops["sync"]:
                    f(e)

            @block.vector
            def _(e):
                for f in ops["vector"]:
                    f(e)

            @block.scalar
            def _(e):
                for f in ops["scalar"]:
                    f(e)

            @block.tensor
            def _(e):
                for f in ops["tensor"]:
                    f(e)

            @block.gpsimd
            def _(e):
                for f in ops["gpsimd"]:
                    f(e)
        es.close()
        return nc

    dW = C("dW")
    for dst, src in [(s_cosT, cosT_c), (s_sinsT, sinsT_c), (s_mcT, mcT_c),
                     (s_mfT, mfT_c), (s_id, ident_c), (s_onesrb, onesrb_c),
                     (s_onesrf, onesrf_c), (s_onescf, onescf_c), (s_eps, epsc_c)]:
        EI("sync", dW, lambda e, d=dst, s=src: e.dma_start(out=d[:], in_=s), 16)
    NCONSTW = dW.n

    dW0 = [C("dW0a"), C("dW0b")]
    vAB = C("vAB")
    tPS = C("tPS")
    gTN = C("gTN")
    dS = [C("dSa"), C("dSb")]
    aC = C("aC")

    mats = [("fc", wfc, sfc_hbm, HF // 128, 1, 2048, HF * D, 0, 6),
            ("pj", wpj, sp_hbm, D // 128, 4, 1280, HF * D, 2, 7)]
    for (mn, wsrc, sdst, nrt, ncc, cw, nel, cthr, cabs) in mats:
        ntile = nrt * ncc
        base_vab = vAB.n
        ti = 0
        for rt in range(nrt):
            for cc in range(ncc):
                j = ti % 2
                lq = "sync" if j == 0 else "scalar"
                if ti >= 2:
                    W(lq, vAB, base_vab + ti - 1)
                else:
                    W(lq, gTN, gTN.n)
                EI(lq, dW0[j], lambda e, j=j, rt=rt, cc=cc, w=wsrc, cw=cw:
                   e.dma_start(out=w0t[j][:, 0:cw], in_=w[rt * 128:(rt + 1) * 128,
                                                         cc * cw:(cc + 1) * cw]), 16)
                W("vector", dW0[j], dW0[j].n)
                EI("vector", vAB, lambda e, j=j, ti=ti, cw=cw:
                   e.tensor_reduce(out=acc0[:, ti:ti + 1], in_=w0t[j][:, 0:cw],
                                   axis=AX, op=ALU.add, apply_absolute_value=True))
                ti += 1
        W("vector", vAB, vAB.n)
        EI("vector", vAB, lambda e, nt=ntile:
           e.tensor_reduce(out=acc0[:, 159:160], in_=acc0[:, 0:nt], axis=AX,
                           op=ALU.add))
        W("tensor", vAB, vAB.n)
        W("tensor", dW, NCONSTW)
        rAUX.acquire("tensor")
        EI("tensor", tPS, lambda e:
           e.matmul(out=pAUX[0:1, 20:21], lhsT=acc0[:, 159:160], rhs=s_onescf[:],
                    start=True, stop=True))
        W("vector", tPS, tPS.n)

        steps = [
            lambda e: e.tensor_copy(out=sc0[0:1, 0:1], in_=pAUX[0:1, 20:21]),
            lambda e, nel=nel: e.tensor_scalar(out=sc0[0:1, 1:2], in0=sc0[0:1, 0:1],
                                               scalar1=1.0 / nel, scalar2=None,
                                               op0=ALU.mult),
            lambda e: e.tensor_scalar(out=sc0[0:1, 2:3], in0=sc0[0:1, 1:2],
                                      scalar1=1e-5, scalar2=None, op0=ALU.max),
            lambda e: e.tensor_scalar(out=sc0[0:1, 3:4], in0=sc0[0:1, 2:3],
                                      scalar1=0.7, scalar2=None, op0=ALU.mult),
            lambda e: e.tensor_scalar(out=sc0[0:1, 4:5], in0=sc0[0:1, 3:4],
                                      scalar1=-1.0, scalar2=None, op0=ALU.mult),
            lambda e, cabs=cabs: e.tensor_copy(out=sc0[0:1, cabs:cabs + 1],
                                               in_=sc0[0:1, 2:3]),
        ]
        for stp in steps:
            EI("vector", vAB, stp)
            W("vector", vAB, vAB.n)
        W("tensor", vAB, vAB.n)
        rAUX.acquire("tensor")
        EI("tensor", tPS, lambda e:
           e.matmul(out=pAUX[:, 21:22], lhsT=s_onesrf[:], rhs=sc0[0:1, 3:4],
                    start=True, stop=True))
        EI("tensor", tPS, lambda e:
           e.matmul(out=pAUX[:, 22:23], lhsT=s_onesrf[:], rhs=sc0[0:1, 4:5],
                    start=True, stop=True))
        W("vector", tPS, tPS.n)

        def evac_thr(e, cthr=cthr):
            e.tensor_copy(out=scB[:, cthr:cthr + 1], in_=pAUX[:, 21:22])
            return e.tensor_copy(out=scB[:, cthr + 1:cthr + 2], in_=pAUX[:, 22:23])
        EI("vector", vAB, evac_thr)
        rAUX.release(vAB)
        base_gtn = gTN.n
        base_ds = [dS[0].n, dS[1].n]
        vab_thr = vAB.n
        ti = 0
        for rt in range(nrt):
            for cc in range(ncc):
                j = ti % 2
                lq = "sync" if j == 0 else "scalar"
                if ti >= 2:
                    W(lq, gTN, base_gtn + 3 * (ti - 1))
                else:
                    W(lq, vAB, vab_thr)
                EI(lq, dW0[j], lambda e, j=j, rt=rt, cc=cc, w=wsrc, cw=cw:
                   e.dma_start(out=w0t[j][:, 0:cw], in_=w[rt * 128:(rt + 1) * 128,
                                                         cc * cw:(cc + 1) * cw]), 16)
                W("vector", dW0[j], dW0[j].n)
                if ti >= 2:
                    W("vector", dS[j], base_ds[j] + 16 * (ti // 2))

                EI("vector", gTN, lambda e, j=j, c=cthr, cw=cw:
                   e.tensor_scalar(out=s0t[j][:, 0:cw], in0=w0t[j][:, 0:cw],
                                   scalar1=scB[:, c:c + 1], scalar2=None,
                                   op0=ALU.is_gt))
                EI("vector", gTN, lambda e, j=j, c=cthr, cw=cw:
                   e.tensor_scalar(out=w0t[j][:, 0:cw], in0=w0t[j][:, 0:cw],
                                   scalar1=scB[:, c + 1:c + 2], scalar2=None,
                                   op0=ALU.is_lt))
                W("vector", gTN, gTN.n)
                EI("vector", gTN, lambda e, j=j, cw=cw:
                   e.tensor_sub(out=s0t[j][:, 0:cw], in0=s0t[j][:, 0:cw],
                                in1=w0t[j][:, 0:cw]))
                W("gpsimd", gTN, gTN.n)
                EI("gpsimd", dS[j], lambda e, j=j, rt=rt, cc=cc, sd=sdst, cw=cw:
                   e.dma_start(out=sd[rt * 128:(rt + 1) * 128,
                                      cc * cw:(cc + 1) * cw], in_=s0t[j][:, 0:cw]), 16)
                ti += 1

    EI("vector", vAB, lambda e: e.tensor_mul(out=sc0[0:1, 8:9], in0=sc0[0:1, 6:7],
                                             in1=sc0[0:1, 6:7]))
    W("vector", vAB, vAB.n)
    EI("vector", vAB, lambda e: e.tensor_mul(out=sc0[0:1, 8:9], in0=sc0[0:1, 8:9],
                                             in1=sc0[0:1, 7:8]))
    W("scalar", vAB, vAB.n)

    EI("scalar", aC, lambda e: e.activation(out=sc0[0:1, 9:10], in_=sc0[0:1, 8:9],
                                            func=AF.Ln))
    W("scalar", aC, aC.n)
    EI("scalar", aC, lambda e: e.activation(out=sc0[0:1, 10:11],
                                            in_=sc0[0:1, 9:10], func=AF.Exp,
                                            scale=0.5))
    W("tensor", aC, aC.n)
    rAUX.acquire("tensor")
    EI("tensor", tPS, lambda e:
       e.matmul(out=pAUX[:, 23:24], lhsT=s_onesrf[:], rhs=sc0[0:1, 10:11],
                start=True, stop=True))
    W("vector", tPS, tPS.n)
    EI("vector", vAB, lambda e: e.tensor_copy(out=scB[:, 4:5], in_=pAUX[:, 23:24]))
    rAUX.release(vAB)

    if _dbg:
        o_scB = nc.dram_tensor("o_scB", [128, 8], F32, kind="ExternalOutput").ap()
        o_sc0 = nc.dram_tensor("o_sc0", [1, 16], F32, kind="ExternalOutput").ap()
        dDBG = C("dDBG")
        W("sync", vAB, vAB.n)
        EI("sync", dDBG, lambda e: e.dma_start(out=o_scB, in_=scB[:]), 16)
        EI("sync", dDBG, lambda e: e.dma_start(out=o_sc0, in_=sc0[:]), 16)
        W("sync", dDBG, 32)

    fence()
    if _PH <= 1:
        return _finish()

    dX = [C("dXa"), C("dXb")]
    aSQ = C("aSQ")
    vRED = C("vRED")
    vST = C("vST")
    aRS = C("aRS")
    vXN = C("vXN")
    tTR = C("tTR")
    vTE = C("vTE")

    def emit_ln_tile(src_hbm, tt, xnsq, stats, xnT_dst, dve_prewait=None):
        j = tt % 2
        if vXN.n >= 2:
            W("sync", vXN, vXN.n - 1)
        EI("sync", dX[j], lambda e, j=j, tt=tt, src=src_hbm:
           e.dma_start(out=s_xf[j][:], in_=src[tt * 128:(tt + 1) * 128, :]), 16)
        ldx = dX[j].n
        W("scalar", dX[j], ldx)
        W("scalar", tTR, tTR.n)
        EI("scalar", aSQ, lambda e, j=j, xnsq=xnsq, stats=stats:
           e.activation(out=xnsq[:], in_=s_xf[j][:], func=AF.Square,
                        accum_out=stats[:, 1:2]))
        W("vector", dX[j], ldx)
        EI("vector", vRED, lambda e, j=j, stats=stats:
           e.tensor_reduce(out=stats[:, 0:1], in_=s_xf[j][:], axis=AX, op=ALU.add))
        W("vector", vRED, vRED.n)
        W("vector", aSQ, aSQ.n)

        EI("vector", vST, lambda e, stats=stats:
           e.tensor_scalar(out=stats[:, 2:3], in0=stats[:, 0:1],
                           scalar1=-1.0 / D, scalar2=None, op0=ALU.mult))
        EI("vector", vST, lambda e, stats=stats:
           e.tensor_scalar(out=stats[:, 3:4], in0=stats[:, 1:2],
                           scalar1=1.0 / D, scalar2=None, op0=ALU.mult))
        W("vector", vST, vST.n)
        EI("vector", vST, lambda e, stats=stats:
           e.scalar_tensor_tensor(out=stats[:, 4:5], in0=stats[:, 2:3],
                                  scalar=stats[:, 2:3], in1=stats[:, 3:4],
                                  op0=ALU.mult, op1=ALU.subtract))
        W("scalar", vST, vST.n)

        def rstd_chain(e, stats=stats):
            e.activation(out=stats[:, 5:6], in_=stats[:, 4:5], func=AF.Ln,
                         bias=s_eps[:], scale=-1.0)
            return e.activation(out=stats[:, 6:7], in_=stats[:, 5:6], func=AF.Exp,
                                scale=-0.5)
        EI("scalar", aRS, rstd_chain)
        W("vector", aRS, aRS.n)
        if dve_prewait is not None:
            dve_prewait()
        EI("vector", vXN, lambda e, j=j, xnsq=xnsq, stats=stats:
           e.tensor_scalar(out=xnsq[:], in0=s_xf[j][:], scalar1=stats[:, 2:3],
                           scalar2=stats[:, 6:7], op0=ALU.add, op1=ALU.mult))
        r = tt % 4
        W("tensor", vXN, vXN.n)
        for f in range(2):
            rLNB.acquire("tensor")
            for kk in range(8):
                k = 8 * f + kk
                fn = lambda e, k=k, kk=kk, xnsq=xnsq: e.transpose(
                    out=pLNB[:, kk * 128:(kk + 1) * 128],
                    in_=xnsq[:, k * 128:(k + 1) * 128], identity=s_id[:])
                if kk == 7:
                    EI("tensor", tTR, fn)
                else:
                    E("tensor", fn)
            W("vector", tTR, tTR.n)

            def evac(e, f=f, r=r, dst=xnT_dst):
                o = dst[:, :].rearrange("p (k q) -> p k q", k=NKC, q=SS)
                o = o[:, 8 * f:8 * f + 8, r * 128:(r + 1) * 128]
                return e.tensor_copy(
                    out=o, in_=pLNB[:, 0:1024].rearrange("p (k q) -> p k q",
                                                         k=8, q=128))
            EI("vector", vTE, evac)
            rLNB.release(vTE)

    dWG = C("dWG")
    W("gpsimd", dS[0], dS[0].n)
    W("gpsimd", dS[1], dS[1].n)
    for k in range(NKC):
        EI("gpsimd", dWG, lambda e, k=k:
           e.dma_start(out=s_wk[:, k * 512:(k + 1) * 512],
                       in_=wk[k * 128:(k + 1) * 128, :]), 16)
    for k in range(NKC):
        EI("gpsimd", dWG, lambda e, k=k:
           e.dma_start(out=s_wv[:, k * 512:(k + 1) * 512],
                       in_=wv[k * 128:(k + 1) * 128, :]), 16)
    NWKV = dWG.n
    gVO = C("gVO")
    EI("gpsimd", gVO, lambda e: e.memset(
        s_V[:, :].rearrange("p (c h o) -> p c h o", c=NT, h=NKV, o=129)[:, :, :, 128:129],
        1.0))

    dXN_ST = C("dXN_ST")
    tKV = C("tKV")
    aKE = C("aKE")
    dKS = C("dKS")
    vKR = C("vKR")
    aVE = C("aVE")

    for s in range(NSS):
        for r in range(4):
            tt = 4 * s + r

            def dve_pre():
                if s >= 1:
                    W("vector", dXN_ST, 16 * s)
                    W("vector", tKV, 8 * s)
            emit_ln_tile(xin, tt, s_xnsq1, s_st1, s_xnT1,
                         dve_prewait=dve_pre if r == 0 else None)
        W("sync", vTE, vTE.n)
        EI("sync", dXN_ST, lambda e, s=s:
           e.dma_start(out=xnT_hbm[s], in_=s_xnT1[:]), 16)
        W("tensor", vTE, vTE.n)
        W("tensor", dWG, NWKV)
        for kvh in range(NKV):
            p = kvh % 2
            rSC[p].acquire("tensor")
            for k in range(NKC):
                fn = lambda e, k=k, kvh=kvh, p=p: e.matmul(
                    out=pSC[p][:, 0:512],
                    lhsT=s_wk[:, k * 512 + kvh * 128:k * 512 + (kvh + 1) * 128],
                    rhs=s_xnT1[:, k * SS:(k + 1) * SS],
                    start=(k == 0), stop=(k == NKC - 1))
                if k == NKC - 1:
                    EI("tensor", tKV, fn)
                else:
                    E("tensor", fn)
            j = kvh % 2
            W("scalar", tKV, tKV.n)
            if vKR.n >= 2:
                W("scalar", vKR, vKR.n - 1)
            EI("scalar", aKE, lambda e, p=p, j=j:
               e.copy(out=s_ktmp[j][:], in_=pSC[p][:, 0:512]))
            rSC[p].release(aKE)
            W("sync", aKE, aKE.n)
            if vKR.n >= 2:
                W("sync", vKR, vKR.n - 1)
            E("sync", lambda e, j=j, sh=dKS.h: (
                e.dma_start(out=s_kshf[j][0:64, :],
                            in_=s_ktmp[j][64:128, :]).then_inc(sh, 16),
                e.dma_start(out=s_kshf[j][64:128, :],
                            in_=s_ktmp[j][0:64, :]).then_inc(sh, 16)))
            dKS.inc(32)
            W("vector", aKE, aKE.n)
            W("vector", dKS, dKS.n)

            def rope_k(e, j=j, s=s, kvh=kvh):
                e.tensor_mul(out=s_kt1[:], in0=s_ktmp[j][:],
                             in1=s_cosT[:, s * SS:(s + 1) * SS])
                e.tensor_mul(out=s_kt2[:], in0=s_kshf[j][:],
                             in1=s_sinsT[:, s * SS:(s + 1) * SS])
                return e.tensor_add(
                    out=s_K[:, kvh * S + s * SS:kvh * S + (s + 1) * SS],
                    in0=s_kt1[:], in1=s_kt2[:])
            EI("vector", vKR, rope_k)
        for tb in range(4):
            p = tb % 2
            cpos = 4 * s + tb
            rSC[p].acquire("tensor")
            if s == 0 and tb == 0:
                W("tensor", gVO, 1)
            for k in range(NKC):
                fn = lambda e, k=k, tb=tb, p=p: e.matmul(
                    out=pSC[p][:, 0:512],
                    lhsT=s_xnT1[:, k * SS + tb * 128:k * SS + (tb + 1) * 128],
                    rhs=s_wv[:, k * 512:(k + 1) * 512],
                    start=(k == 0), stop=(k == NKC - 1))
                if k == NKC - 1:
                    EI("tensor", tKV, fn)
                else:
                    E("tensor", fn)
            W("scalar", tKV, tKV.n)

            def evac_v(e, p=p, cpos=cpos):
                o = s_V[:, cpos * NKV * 129:(cpos + 1) * NKV * 129]
                o = o.rearrange("p (h o) -> p h o", h=NKV, o=129)[:, :, 0:128]
                return e.copy(out=o, in_=pSC[p][:, 0:512].rearrange(
                    "p (h d) -> p h d", h=NKV, d=128))
            EI("scalar", aVE, evac_v)
            rSC[p].release(aVE)

    fence()
    if _PH <= 2:
        return _finish()

    for k in range(NKC):
        EI("gpsimd", dWG, lambda e, k=k:
           e.dma_start(out=s_wq[:, k * D:(k + 1) * D],
                       in_=wq[k * 128:(k + 1) * 128, :]), 16)
    NWQ = dWG.n
    vMS = C("vMS")
    EI("vector", vMS, lambda e: (e.memset(s_mbias[0][:], 0.0),
                                 e.memset(s_mbias[1][:], 0.0),
                                 e.memset(s_P[0][:], 0.0),
                                 e.memset(s_P[1][:], 0.0),
                                 e.memset(s_P[2][:], 0.0))[-1])

    dXN_LD = C("dXN_LD")
    tQP = C("tQP")
    aQE = C("aQE")
    dQS = C("dQS")
    vQR = C("vQR")
    tSC = [C("tSC0"), C("tSC1")]
    vMX = [C("vMX0"), C("vMX1")]
    aEX = [C("aEX0"), C("aEX1")]
    gMK = [C("gMK0"), C("gMK1")]
    tPV = [C("tPV0"), C("tPV1")]
    aRC = [C("aRC0"), C("aRC1")]
    vCX = [C("vCX0"), C("vCX1")]
    dCTX = C("dCTX")
    dMB = C("dMB")
    vMB = C("vMB")
    tMB = C("tMB")
    aMB = C("aMB")
    vMBE = C("vMBE")

    nA = [0, 0]
    uglob = [0]
    pend_pv = []
    pvrec = {}
    _P2S = int(os.environ.get("KP2S", "8"))
    _KMSK = int(os.environ.get("KMSK", "1"))
    _KPV = int(os.environ.get("KPV", "1"))
    _KEXP = int(os.environ.get("KEXP", "1"))
    _KMAX = int(os.environ.get("KMAX", "1"))
    _KSCR = int(os.environ.get("KSCR", "1"))
    _P2T = int(os.environ.get("KP2T", "4"))
    _P2H = int(os.environ.get("KP2H", "16"))

    def emit_scores(t, h):
        u = uglob[0]
        p = u % 2
        kv = h // GQ
        s = t // 4
        r = t % 4
        gs = max(0, t - 7)
        ln = t - gs + 1
        far = t - 8 if t >= 8 else None
        rSC[p].acquire("tensor")
        rCX[p].acquire("tensor")
        W("tensor", vQR, 16 * s + h + 1)
        if not _KSCR:
            EI("tensor", tSC[p], lambda e: e.nop())
        else:
            nsc = ln + (1 if far is not None else 0)
            for i in range(ln):
                jc = gs + i
                fn = lambda e, i=i, jc=jc, p=p, kv=kv, h=h, r=r: e.matmul(
                    out=pSC[p][:, i * 128:(i + 1) * 128],
                    lhsT=s_K[:, kv * S + jc * 128:kv * S + (jc + 1) * 128],
                    rhs=s_QT[:, h * SS + r * 128:h * SS + (r + 1) * 128],
                    start=True, stop=True)
                if i == ln - 1 and far is None:
                    EI("tensor", tSC[p], fn)
                else:
                    E("tensor", fn)
            if far is not None:
                EI("tensor", tSC[p], lambda e, jc=far, p=p, kv=kv, h=h, r=r: e.matmul(
                    out=pCX[p][:, 256:384],
                    lhsT=s_K[:, kv * S + jc * 128:kv * S + (jc + 1) * 128],
                    rhs=s_QT[:, h * SS + r * 128:h * SS + (r + 1) * 128],
                    start=True, stop=True))
        sp = u % 3
        W("scalar", tSC[p], tSC[p].n)
        if t >= 2:
            W("scalar", vMBE, t - 1)
        else:
            W("scalar", vMS, 1)
        if u - 3 in pvrec:
            c3, v3 = pvrec.pop(u - 3)
            W("scalar", c3, v3)

        def expf(e, p=p, sp=sp, t=t, h=h, ln=ln, far=far):
            if not _KEXP:
                return e.nop()
            i1 = e.activation(out=s_P[sp][:, 0:ln * 128], in_=pSC[p][:, 0:ln * 128],
                              func=AF.Exp, bias=s_mbias[t % 2][:, h:h + 1],
                              scale=ISQ)
            if far is not None:
                return e.activation(out=s_P[sp][:, 1024:1152],
                                    in_=pCX[p][:, 256:384], func=AF.Exp,
                                    bias=s_mbias[t % 2][:, h:h + 1], scale=ISQ)
            return i1
        EI("scalar", aEX[p], expf)
        W("vector", aEX[p], aEX[p].n)
        if t >= 2:
            W("vector", dMB, 16 * (t - 1))
        EI("vector", vMX[p], lambda e, sp=sp, t=t, h=h:
           e.reduce_max(out=s_mcols[t % 2][:, h:h + 1], in_=s_P[sp][:, 0:1152],
                        axis=AX) if _KMAX else e.nop())
        W("gpsimd", aEX[p], aEX[p].n)

        def masks(e, sp=sp, ln=ln, far=far):
            if not _KMSK:
                return e.nop()
            i1 = e.tensor_mul(out=s_P[sp][:, (ln - 1) * 128:ln * 128],
                              in0=s_P[sp][:, (ln - 1) * 128:ln * 128], in1=s_mcT[:])
            if far is not None:
                return e.tensor_mul(out=s_P[sp][:, 1024:1152],
                                    in0=s_P[sp][:, 1024:1152], in1=s_mfT[:])
            return i1
        EI("gpsimd", gMK[p], masks)
        pend_pv.append((u, p, t, h, gs, ln, far))
        nA[p] += 1
        uglob[0] += 1

    def emit_pv(item):
        u, p, t, h, gs, ln, far = item
        sp = u % 3
        kv = h // GQ
        W("tensor", gMK[p], (u // 2) + 1)
        plist = [(i * 128, gs + i) for i in range(ln)]
        if far is not None:
            plist.append((1024, far))
        npc = len(plist)
        if not _KPV:
            EI("tensor", tPV[p], lambda e: e.nop())
            pvrec[u] = (tPV[p], tPV[p].n)
        for i, (poff, jc) in enumerate(plist):
            if not _KPV:
                break
            fn = lambda e, poff=poff, jc=jc, p=p, sp=sp, kv=kv, i=i, npc=npc: e.matmul(
                out=pCX[p][:, 0:129],
                lhsT=s_P[sp][:, poff:poff + 128],
                rhs=s_V[:, jc * NKV * 129 + kv * 129:jc * NKV * 129 + (kv + 1) * 129],
                start=(i == 0), stop=(i == npc - 1))
            if i == npc - 1:
                EI("tensor", tPV[p], fn)
            else:
                E("tensor", fn)
        pvrec[u] = (tPV[p], tPV[p].n)
        W("scalar", tPV[p], tPV[p].n)

        def recip(e, p=p):
            e.activation(out=s_lnr[p][:], in_=pCX[p][:, 128:129], func=AF.Ln)
            return e.activation(out=s_r[p][:], in_=s_lnr[p][:], func=AF.Exp,
                                scale=-1.0)
        EI("scalar", aRC[p], recip)
        W("vector", aRC[p], aRC[p].n)
        if t >= 2:
            W("vector", dCTX, 16 * (t - 1))
        EI("vector", vCX[p], lambda e, p=p, t=t, h=h:
           e.tensor_scalar(out=s_ctx[t % 2][:, h * 128:(h + 1) * 128],
                           in0=pCX[p][:, 0:128], scalar1=s_r[p][:], scalar2=None,
                           op0=ALU.mult))
        rCX[p].release(vCX[p])
        rCX[p].release(aEX[p])
        rSC[p].release(aEX[p])

    _KMB = int(os.environ.get("KMB", "4"))

    def emit_mbias(t):
        if _KMB < 1:
            return
        W("sync", vMX[0], vMX[0].n)
        W("sync", vMX[1], vMX[1].n)
        EI("sync", dMB, lambda e, t=t: e.dma_start(
            out=s_mrow[0:1, :].rearrange("o (p c) -> o p c", p=128, c=16),
            in_=s_mcols[t % 2][:, 0:16]), 16)
        if _KMB < 2:
            return
        W("vector", dMB, dMB.n)
        EI("vector", vRED, lambda e: e.tensor_reduce(
            out=s_negrow[:], in_=s_mrow[0:1, :].rearrange(
                "o (p c) -> o c p", p=128, c=16), axis=AX, op=ALU.max))
        W("scalar", vRED, vRED.n)
        EI("scalar", aMB, lambda e:
           e.activation(out=s_negrow[:], in_=s_negrow[:], func=AF.Ln))
        if _KMB < 3:
            return
        W("tensor", aMB, aMB.n)
        rAUX.acquire("tensor")
        EI("tensor", tMB, lambda e: e.matmul(
            out=pAUX[:, 0:16], lhsT=s_onesrb[:], rhs=s_negrow[0:1, 0:16],
            start=True, stop=True))
        W("vector", tMB, tMB.n)
        W("vector", aEX[0], 8 * (t + 1))
        W("vector", aEX[1], 8 * (t + 1))
        EI("vector", vMBE, lambda e, t=t:
           e.tensor_sub(out=s_mbias[t % 2][:], in0=s_mbias[t % 2][:],
                        in1=pAUX[:, 0:16]))
        rAUX.release(vMBE)

    for s in range(min(NSS, _P2S)):
        if s >= 1:
            W("sync", tQP, 16 * s)
        W("sync", dXN_ST, 16 * (s + 1))
        EI("sync", dXN_LD, lambda e, s=s:
           e.dma_start(out=s_xnT[:], in_=xnT_hbm[s]), 16)
        while pend_pv:
            emit_pv(pend_pv.pop(0))
        W("tensor", dXN_LD, dXN_LD.n)
        W("tensor", dWG, NWQ)
        for h in range(NH):
            p = h % 2
            rSC[p].acquire("tensor")
            for k in range(NKC):
                fn = lambda e, k=k, h=h, p=p: e.matmul(
                    out=pSC[p][:, 0:512],
                    lhsT=s_wq[:, k * D + h * 128:k * D + (h + 1) * 128],
                    rhs=s_xnT[:, k * SS:(k + 1) * SS],
                    start=(k == 0), stop=(k == NKC - 1))
                if k == NKC - 1:
                    EI("tensor", tQP, fn)
                else:
                    E("tensor", fn)
            j = h % 2
            W("scalar", tQP, tQP.n)
            if vQR.n >= 2:
                W("scalar", vQR, vQR.n - 1)
            EI("scalar", aQE, lambda e, p=p, j=j:
               e.copy(out=s_qtmp[j][:], in_=pSC[p][:, 0:512]))
            rSC[p].release(aQE)
            W("sync", aQE, aQE.n)
            if vQR.n >= 2:
                W("sync", vQR, vQR.n - 1)
            E("sync", lambda e, j=j, sh=dQS.h: (
                e.dma_start(out=s_qshf[j][0:64, :],
                            in_=s_qtmp[j][64:128, :]).then_inc(sh, 16),
                e.dma_start(out=s_qshf[j][64:128, :],
                            in_=s_qtmp[j][0:64, :]).then_inc(sh, 16)))
            dQS.inc(32)
            W("vector", aQE, aQE.n)
            W("vector", dQS, dQS.n)
            if h == 0 and s > 0:
                W("vector", tSC[0], 32 * s)
                W("vector", tSC[1], 32 * s)

            def rope_q(e, j=j, s=s, h=h):
                e.tensor_mul(out=s_t1[:], in0=s_qtmp[j][:],
                             in1=s_cosT[:, s * SS:(s + 1) * SS])
                e.tensor_mul(out=s_t2[:], in0=s_qshf[j][:],
                             in1=s_sinsT[:, s * SS:(s + 1) * SS])
                return e.tensor_add(out=s_QT[:, h * SS:(h + 1) * SS],
                                    in0=s_t1[:], in1=s_t2[:])
            EI("vector", vQR, rope_q)
        W("tensor", aQE, aQE.n)
        for r in range(min(4, _P2T)):
            t = 4 * s + r
            for h in range(min(NH, _P2H)):
                emit_scores(t, h)
                if h == 8 and t >= 1:
                    emit_mbias(t - 1)
                if len(pend_pv) >= 2:
                    emit_pv(pend_pv.pop(0))
            W("sync", vCX[0], vCX[0].n + len([u for u in pend_pv if u[1] == 0]))
            W("sync", vCX[1], vCX[1].n + len([u for u in pend_pv if u[1] == 1]))
            EI("sync", dCTX, lambda e, t=t:
               e.dma_start(out=ctx_hbm[t * 128:(t + 1) * 128, :],
                           in_=s_ctx[t % 2][:]), 16)
    while pend_pv:
        emit_pv(pend_pv.pop(0))

    fence()
    if _PH <= 3:
        return _finish()

    for k in range(NKC):
        EI("gpsimd", dWG, lambda e, k=k:
           e.dma_start(out=s_wo[:, k * D:(k + 1) * D],
                       in_=wo[k * 128:(k + 1) * 128, :]), 16)
    NWO = dWG.n
    dCT_LD = C("dCT_LD")
    dX2 = C("dX2")
    dX1_ST = C("dX1_ST")
    tOU = [C("tOU0"), C("tOU1")]
    vOU = [C("vOU0"), C("vOU1")]
    pc2 = [0]
    base_tou = tOU[0].n
    for s in range(NSS):
        if s >= 1:
            W("sync", tOU[0], base_tou + 4 * s)
        for h in range(NH):
            EI("sync", dCT_LD, lambda e, s=s, h=h: e.dma_start(
                out=s_ctxT[:, h * SS:(h + 1) * SS],
                in_=ctx_hbm[s * SS:(s + 1) * SS, h * 128:(h + 1) * 128],
                transpose=True), 16)
        for tb in range(4):
            t = 4 * s + tb
            if t >= 2:
                W("sync", vOU[0], vOU[0].n)
            EI("sync", dX2, lambda e, t=t:
               e.dma_start(out=s_xf[t % 2][:], in_=xin[t * 128:(t + 1) * 128, :]), 16)
            W("tensor", dCT_LD, 16 * 16 * (s + 1))
            W("tensor", dWG, NWO)
            W("tensor", vOU[0], vOU[0].n)
            pb = [(pSC[0], 0), (pSC[0], 512), (pSC[1], 0), (pSC[1], 512)]
            for h in range(NH):
                for n in range(4):
                    bt, bo = pb[n]
                    fn = lambda e, h=h, tb=tb, n=n, bt=bt, bo=bo: e.matmul(
                        out=bt[:, bo:bo + 512],
                        lhsT=s_ctxT[:, h * SS + tb * 128:h * SS + (tb + 1) * 128],
                        rhs=s_wo[:, h * D + n * 512:h * D + (n + 1) * 512],
                        start=(h == 0), stop=(h == NH - 1))
                    if h == NH - 1 and n == 3:
                        EI("tensor", tOU[0], fn)
                    else:
                        E("tensor", fn)
            W("vector", tOU[0], tOU[0].n)
            W("vector", dX2, 16 * (t + 1))
            if t >= 2:
                W("vector", dX1_ST, 16 * (t - 1))

            def evac2b(e, t=t, pb=pb):
                for n in range(4):
                    bt, bo = pb[n]
                    i = e.tensor_add(out=s_stage[t % 2][:, n * 512:(n + 1) * 512],
                                     in0=bt[:, bo:bo + 512],
                                     in1=s_xf[t % 2][:, n * 512:(n + 1) * 512])
                return i
            EI("vector", vOU[0], evac2b)
            W("sync", vOU[0], vOU[0].n)
            EI("sync", dX1_ST, lambda e, t=t:
               e.dma_start(out=x1_hbm[t * 128:(t + 1) * 128, :],
                           in_=s_stage[t % 2][:]), 16)

    fence()
    if _PH <= 4:
        return _finish()

    dSF = C("dSF")
    for k in range(NKC):
        EI("sync", dSF, lambda e, k=k: e.dma_start(
            out=s_SfcT[:, k * HF:(k + 1) * HF],
            in_=sfc_hbm[:, k * 128:(k + 1) * 128], transpose=True), 16)
    tH = [C("tH0"), C("tH1")]
    aH = [C("aH0"), C("aH1")]
    dH_ST = C("dH_ST")
    for s in range(NSS):
        for r in range(4):
            tt = 4 * s + r

            def dve_pre3():
                if s >= 1:
                    W("vector", tH[0], (NHC * s) // 2)
                    W("vector", tH[1], (NHC * s) // 2)
            emit_ln_tile(x1_hbm, tt, s_xnsq3, s_st3, s_xn2T,
                         dve_prewait=dve_pre3 if r == 0 else None)
        W("tensor", vTE, vTE.n)
        W("tensor", dSF, 16 * NKC)
        for hh in range(NHC):
            p = hh % 2
            rSC[p].acquire("tensor")
            for k in range(NKC):
                fn = lambda e, k=k, hh=hh, p=p: e.matmul(
                    out=pSC[p][:, 0:512],
                    lhsT=s_SfcT[:, k * HF + hh * 128:k * HF + (hh + 1) * 128],
                    rhs=s_xn2T[:, k * SS:(k + 1) * SS],
                    start=(k == 0), stop=(k == NKC - 1))
                if k == NKC - 1:
                    EI("tensor", tH[p], fn)
                else:
                    E("tensor", fn)
            jb = (hh // 4) % 2
            W("scalar", tH[p], tH[p].n)
            bidx = (NHC // 4) * s + hh // 4
            if hh % 4 == 0 and bidx >= 2:
                W("scalar", dH_ST, 16 * (bidx - 1))

            def relu_sq(e, p=p, jb=jb, hh=hh):
                e.activation(out=pAUX[:, 0:512], in_=pSC[p][:, 0:512],
                             func=AF.Relu, scale=scB[:, 4:5])
                return e.activation(
                    out=s_hst[jb][:, (hh % 4) * 512:(hh % 4 + 1) * 512],
                    in_=pAUX[:, 0:512], func=AF.Square)
            EI("scalar", aH[p], relu_sq)
            rSC[p].release(aH[p])
            if hh % 4 == 3:
                W("sync", aH[0], aH[0].n)
                W("sync", aH[1], aH[1].n)
                EI("sync", dH_ST, lambda e, s=s, hh=hh, jb=jb: e.dma_start(
                    out=h_hbm[s][:, (hh - 3) * 512:(hh + 1) * 512],
                    in_=s_hst[jb][:]), 16)

    fence()
    if _PH <= 5:
        return _finish()

    dSP = C("dSP")
    for jc in range(NHC):
        EI("sync", dSP, lambda e, jc=jc: e.dma_start(
            out=s_SpT[:, jc * D:(jc + 1) * D],
            in_=sp_hbm[:, jc * 128:(jc + 1) * 128], transpose=True), 16)
    dH_LD = C("dH_LD")
    dX1_LD = C("dX1_LD")
    dY = C("dY")
    pc3 = [0]
    for t in range(NT):
        s, tb = t // 4, t % 4
        if t >= 2:
            W("sync", vOU[0], vOU[0].n)

        def ht_in(e, s=s, tb=tb, j=t % 2):
            src = h_hbm[s].rearrange("p (c q) -> p c q", c=NHC, q=SS)
            src = src[:, :, tb * 128:(tb + 1) * 128]
            return e.dma_start(out=s_ht[j][:, :].rearrange(
                "p (c q) -> p c q", c=NHC, q=128), in_=src)
        EI("sync", dH_LD, ht_in, 16)
        EI("sync", dX1_LD, lambda e, t=t: e.dma_start(
            out=s_xf[t % 2][:], in_=x1_hbm[t * 128:(t + 1) * 128, :]), 16)
        W("tensor", dSP, 16 * NHC)
        W("tensor", dH_LD, 16 * (t + 1))
        W("tensor", vOU[0], vOU[0].n)
        pb = [(pSC[0], 0), (pSC[0], 512), (pSC[1], 0), (pSC[1], 512)]
        for jc in range(NHC):
            for n in range(4):
                bt, bo = pb[n]
                fn = lambda e, jc=jc, n=n, bt=bt, bo=bo, jj=t % 2: e.matmul(
                    out=bt[:, bo:bo + 512],
                    lhsT=s_ht[jj][:, jc * 128:(jc + 1) * 128],
                    rhs=s_SpT[:, jc * D + n * 512:jc * D + (n + 1) * 512],
                    start=(jc == 0), stop=(jc == NHC - 1))
                if jc == NHC - 1 and n == 3:
                    EI("tensor", tOU[0], fn)
                else:
                    E("tensor", fn)
        W("vector", tOU[0], tOU[0].n)
        W("vector", dX1_LD, 16 * (t + 1))
        if t >= 1:
            W("vector", dY, 16 * t)

        def evac3b(e, t=t, pb=pb):
            for n in range(4):
                bt, bo = pb[n]
                i = e.tensor_add(out=s_yst[:, n * 512:(n + 1) * 512],
                                 in0=bt[:, bo:bo + 512],
                                 in1=s_xf[t % 2][:, n * 512:(n + 1) * 512])
            return i
        EI("vector", vOU[0], evac3b)
        W("sync", vOU[0], vOU[0].n)
        EI("sync", dY, lambda e, t=t: e.dma_start(
            out=y[t * 128:(t + 1) * 128, :], in_=s_yst[:]), 16)
    W("sync", dY, 16 * NT)
    return _finish()


def _consts():
    inv = 1.0 / (10000.0 ** (np.arange(0, HD, 2, dtype=np.float32) / HD))
    t = np.arange(S, dtype=np.float32)
    f = np.concatenate([inv, inv])
    ang = f[:, None] * t[None, :]
    cosT = np.cos(ang).astype(bfnp)
    sins = np.sin(ang).astype(np.float32)
    sins[:HD // 2] *= -1.0
    sinsT = sins.astype(bfnp)
    jj = np.arange(128)
    mcT = (jj[:, None] <= jj[None, :]).astype(bfnp)
    mfT = (jj[:, None] >= jj[None, :]).astype(bfnp)
    return dict(
        cosT_c=cosT, sinsT_c=sinsT, mcT_c=mcT, mfT_c=mfT,
        ident_c=np.eye(128, dtype=bfnp),
        onesrb_c=np.ones((1, 128), dtype=bfnp),
        onesrf_c=np.ones((1, 128), dtype=np.float32),
        onescf_c=np.ones((128, 1), dtype=np.float32),
        epsc_c=np.full((128, 1), 1e-5, dtype=np.float32),
    )


def kernel(L0_attn_hidden_states, L0_attn_w_q, L0_attn_w_k, L0_attn_w_v,
           L0_attn_w_o, L0_ffn_w_fc, L0_ffn_w_proj):
    x = np.ascontiguousarray(np.asarray(L0_attn_hidden_states, dtype=np.float32))
    base = dict(
        wq=np.ascontiguousarray(L0_attn_w_q, dtype=np.float32),
        wk=np.ascontiguousarray(L0_attn_w_k, dtype=np.float32),
        wv=np.ascontiguousarray(L0_attn_w_v, dtype=np.float32),
        wo=np.ascontiguousarray(L0_attn_w_o, dtype=np.float32),
        wfc=np.ascontiguousarray(L0_ffn_w_fc, dtype=np.float32),
        wpj=np.ascontiguousarray(L0_ffn_w_proj, dtype=np.float32),
        **_consts(),
    )
    nc = build_nc()
    in_maps = [dict(xin=np.ascontiguousarray(x[b]), **base) for b in range(8)]
    res = run_bass_kernel_spmd(nc, in_maps, core_ids=list(range(8)))
    out = np.stack([np.asarray(r["y"], dtype=np.float32) for r in res.results])
    return out


# version 07
# speedup vs baseline: 1.0415x; 1.0415x over previous
import math
import os
import numpy as np
import ml_dtypes
import concourse.bass as bass
import concourse.mybir as mybir
import concourse.bass_utils as _bu
from concourse.bass_utils import run_bass_kernel_spmd

if not getattr(_bu, "_ldwopt_patched", False):
    _bu._ldwopt_patched = True
    _orig_run_command = _bu.run_command

    def _run_command_ldwopt(cmd, **kw):
        cmd = ["--enable-ldw-opt=true" if c == "--enable-ldw-opt=false" else c
               for c in cmd]
        return _orig_run_command(cmd, **kw)

    _bu.run_command = _run_command_ldwopt

F32 = mybir.dt.float32
BF16 = mybir.dt.bfloat16
U8 = mybir.dt.uint8
AX = mybir.AxisListType.X
AF = mybir.ActivationFunctionType
ALU = mybir.AluOpType
bfnp = ml_dtypes.bfloat16

S, D, NH, NKV, HD, WIN, HF = 4096, 2048, 16, 4, 128, 1024, 5120
NT, SS, NSS = 32, 512, 8
NKC = D // 128
NHC = HF // 128
ISQ = 1.0 / math.sqrt(HD)
GQ = NH // NKV

ENGS = ("sync", "vector", "scalar", "tensor", "gpsimd")


class Ctr:
    __slots__ = ("h", "n")

    def __init__(self, h):
        self.h = h
        self.n = 0

    def inc(self, k=1):
        self.n += k
        return self.n


def build_nc():
    nc = bass.Bass("TRN2", target_bir_lowering=False,
                   detect_race_conditions=False)
    din = lambda n, s, d=F32: nc.dram_tensor(n, s, d, kind="ExternalInput").ap()
    _dbg = os.environ.get("KDBG") == "1"
    dint = lambda n, s, d: nc.dram_tensor(
        n, s, d, kind="ExternalOutput" if _dbg else "Internal").ap()

    xin = din("xin", [S, D])
    wq = din("wq", [D, D])
    wk = din("wk", [D, NKV * HD])
    wv = din("wv", [D, NKV * HD])
    wo = din("wo", [D, D])
    wfc = din("wfc", [HF, D])
    wpj = din("wpj", [D, HF])
    cosT_c = din("cosT_c", [128, S], BF16)
    sinsT_c = din("sinsT_c", [128, S], BF16)
    mcT_c = din("mcT_c", [128, 128], BF16)
    mfT_c = din("mfT_c", [128, 128], BF16)
    ident_c = din("ident_c", [128, 128], BF16)
    onesrb_c = din("onesrb_c", [1, 128], BF16)
    onesrf_c = din("onesrf_c", [1, 128], F32)
    onescf_c = din("onescf_c", [128, 1], F32)
    onescb_c = din("onescb_c", [128, 1], BF16)
    epsc_c = din("epsc_c", [128, 1], F32)
    y = nc.dram_tensor("y", [S, D], F32, kind="ExternalOutput").ap()

    xnT_hbm = dint("xnT_hbm", [NSS, 128, NKC * SS], BF16)
    ctxT_hbm = dint("ctxT_hbm", [NSS, 128, NH * SS], BF16)
    x1_hbm = dint("x1_hbm", [S, D], F32)
    h_hbm = dint("h_hbm", [NSS, 128, NHC * SS], BF16)
    sfc_hbm = dint("sfc_hbm", [HF, D], BF16)
    sp_hbm = dint("sp_hbm", [D, HF], BF16)

    from contextlib import ExitStack
    es = ExitStack()

    ARENA = 211968
    arena = es.enter_context(nc.sbuf_tensor("arena", [128, ARENA], U8))
    arena_addr = nc.lookup_mloc(arena).addr

    def place(name, shape, dtype, off):
        assert off % 32 == 0, name
        per = int(np.prod(shape[1:])) * mybir.dt.size(dtype)
        assert off + per <= ARENA, (name, off, per)
        h = nc.virtual_sbuf_tensor(name, list(shape), dtype)
        nc._tensor(h.name, h.shape, h.dtype, type="SB")
        mloc = nc.lookup_mloc(h)
        mloc.allocated = True
        mloc.addr = arena_addr + off
        return h

    w0t = [place(f"w0t{i}", [128, 2048], F32, i * 8192) for i in range(2)]
    s0t = [place(f"s0t{i}", [128, 2048], BF16, 16384 + i * 4096) for i in range(2)]
    acc0 = place("acc0", [128, 160], F32, 24576)
    s_xf = [place(f"s_xf{i}", [128, D], F32, i * 8192) for i in range(2)]
    s_mrow = place("s_mrow", [1, 2048], BF16, 0)
    s_Pc = [place(f"s_Pc{i}", [128, 512], BF16, 4096 + i * 1024) for i in range(3)]
    s_mcols = [place(f"s_mcols{i}", [128, 32], BF16, 8192 + i * 64) for i in range(2)]
    s_negrow32 = place("s_negrow32", [1, 32], BF16, 8320)
    s_negrow = place("s_negrow", [1, 16], BF16, 8384)
    s_mbias = [place(f"s_mbias{i}", [128, 16], F32, 8448 + i * 64) for i in range(2)]
    s_lnr = [place(f"s_lnr{i}", [128, 1], F32, 8576 + i * 32) for i in range(2)]
    s_r = [place(f"s_r{i}", [128, 1], F32, 8640 + i * 32) for i in range(2)]
    s_qtmp = [place(f"s_qtmp{i}", [128, SS], BF16, 9216 + i * 1024) for i in range(2)]
    s_qshf = [place(f"s_qshf{i}", [128, SS], BF16, 11264 + i * 1024) for i in range(2)]
    s_t1 = place("s_t1", [128, SS], BF16, 13312)
    s_t2 = place("s_t2", [128, SS], BF16, 14336)
    s_wk = place("s_wk", [128, NKC * 512], BF16, 16384)
    s_wv = place("s_wv", [128, NKC * 512], BF16, 32768)
    s_QT = place("s_QT", [128, NH * SS], BF16, 16384)
    s_xnT = place("s_xnT", [128, NKC * SS], BF16, 32768)
    s_ctxT = place("s_ctxT", [128, NH * SS], BF16, 16384)
    s_K = place("s_K", [128, NKV * S], BF16, 49152)
    s_V = place("s_V", [128, NT * NKV * 128], BF16, 81920)
    s_stage = [place(f"s_stage{i}", [128, D], F32, 49152 + i * 8192) for i in range(2)]
    s_ht = [place(f"s_ht{i}", [128, NHC * 128], BF16, 180736 + i * 10240) for i in range(2)]
    s_yst = place("s_yst", [128, D], F32, 201216)
    s_wq = place("s_wq", [128, NKC * D], BF16, 115200)
    s_wo = place("s_wo", [128, NKC * D], BF16, 115200)
    s_xnsq1 = place("s_xnsq1", [128, D], BF16, 115200)
    s_st1 = place("s_st1", [128, 8], F32, 119296)
    s_ktmp = [place(f"s_ktmp{i}", [128, SS], BF16, 119424 + i * 1024) for i in range(2)]
    s_kshf = [place(f"s_kshf{i}", [128, SS], BF16, 121472 + i * 1024) for i in range(2)]
    s_kt1 = place("s_kt1", [128, SS], BF16, 123520)
    s_kt2 = place("s_kt2", [128, SS], BF16, 124544)
    s_xnT1 = place("s_xnT1", [128, NKC * SS], BF16, 126976)
    s_SfcT = place("s_SfcT", [128, NKC * HF], BF16, 16384)
    s_SpT = place("s_SpT", [128, NHC * D], BF16, 16384)
    s_cosT = place("s_cosT", [128, S], BF16, 180736)
    s_sinsT = place("s_sinsT", [128, S], BF16, 188928)
    s_xn2T = place("s_xn2T", [128, NKC * SS], BF16, 180736)
    s_id = place("s_id", [128, 128], BF16, 197120)
    s_mcT = place("s_mcT", [128, 128], BF16, 197376)
    s_mfT = place("s_mfT", [128, 128], BF16, 197632)
    s_onesrb = place("s_onesrb", [1, 128], BF16, 197888)
    s_onesrf = place("s_onesrf", [1, 128], F32, 198144)
    s_onescf = place("s_onescf", [128, 1], F32, 198656)
    s_eps = place("s_eps", [128, 1], F32, 198688)
    sc0 = place("sc0", [1, 16], F32, 198720)
    scB = place("scB", [128, 8], F32, 198784)
    s_ctx1 = [place(f"s_ctx1{i}", [128, 512], BF16, 198912 + i * 1024) for i in range(2)]
    s_rb = [place(f"s_rb{i}", [128, 512], BF16, 201216 + i * 1024) for i in range(2)]
    s_rrow = place("s_rrow", [1, 512], BF16, 203264)
    s_onescb = place("s_onescb", [128, 1], BF16, 204288)
    s_hst = [place(f"s_hst{i}", [128, 2048], BF16, 198912 + i * 4096) for i in range(2)]
    s_xnsq3 = place("s_xnsq3", [128, D], BF16, 207104)
    s_st3 = place("s_st3", [128, 8], F32, 211232)

    pSC = [es.enter_context(nc.psum_tensor(f"pSC{i}", [128, 1024], F32)) for i in range(2)]
    pCX = [es.enter_context(nc.psum_tensor(f"pCX{i}", [128, 512], F32)) for i in range(2)]
    pLNB = es.enter_context(nc.psum_tensor("pLNB", [128, 1024], BF16))
    pAUX = es.enter_context(nc.psum_tensor("pAUX", [128, 512], F32))

    sems = {}
    ops = {e: [] for e in ENGS}

    def C(name):
        if name not in sems:
            sems[name] = Ctr(es.enter_context(nc.semaphore(name)))
        return sems[name]

    def E(eng, fn):
        ops[eng].append(fn)

    def W(eng, ctr, val):
        if val <= 0:
            return
        E(eng, lambda e, s=ctr.h, v=val: e.wait_ge(s, v))

    def EI(eng, ctr, fn, k=1):
        ctr.inc(k)
        E(eng, lambda e, f=fn, s=ctr.h, kk=k: f(e).then_inc(s, kk))

    class Res:

        def __init__(self):
            self.reqs = []

        def acquire(self, eng):
            for c, v in self.reqs:
                W(eng, c, v)
            self.reqs = []

        def release(self, ctr):
            self.reqs.append((ctr, ctr.n))

    rSC = [Res(), Res()]
    rCX = [Res(), Res()]
    rLNB = Res()
    rAUX = Res()

    FEN = C("FEN")
    fence_round = [0]

    def fence():
        fence_round[0] += 1
        tgt = 5 * fence_round[0]
        for eng in ENGS:
            E(eng, lambda e, s=FEN.h: e.nop().then_inc(s, 1))
        FEN.inc(5)
        for eng in ENGS:
            W(eng, FEN, tgt)

    _PH = int(os.environ.get("KPH", "9"))

    def _finish():
        with nc.Block() as block:
            @block.sync
            def _(e):
                for f in ops["sync"]:
                    f(e)

            @block.vector
            def _(e):
                for f in ops["vector"]:
                    f(e)

            @block.scalar
            def _(e):
                for f in ops["scalar"]:
                    f(e)

            @block.tensor
            def _(e):
                for f in ops["tensor"]:
                    f(e)

            @block.gpsimd
            def _(e):
                for f in ops["gpsimd"]:
                    f(e)
        es.close()
        return nc

    dW = C("dW")
    for dst, src in [(s_cosT, cosT_c), (s_sinsT, sinsT_c), (s_mcT, mcT_c),
                     (s_mfT, mfT_c), (s_id, ident_c), (s_onesrb, onesrb_c),
                     (s_onesrf, onesrf_c), (s_onescf, onescf_c), (s_eps, epsc_c),
                     (s_onescb, onescb_c)]:
        EI("sync", dW, lambda e, d=dst, s=src: e.dma_start(out=d[:], in_=s), 16)
    NCONSTW = dW.n

    dW0 = [C("dW0a"), C("dW0b")]
    vAB = C("vAB")
    tPS = C("tPS")
    gTN = C("gTN")
    dS = [C("dSa"), C("dSb")]
    aC = C("aC")

    mats = [("fc", wfc, sfc_hbm, HF // 128, 1, 2048, HF * D, 0, 6),
            ("pj", wpj, sp_hbm, D // 128, 4, 1280, HF * D, 2, 7)]
    for (mn, wsrc, sdst, nrt, ncc, cw, nel, cthr, cabs) in mats:
        ntile = nrt * ncc
        base_vab = vAB.n
        ti = 0
        for rt in range(nrt):
            for cc in range(ncc):
                j = ti % 2
                lq = "sync" if j == 0 else "scalar"
                if ti >= 2:
                    W(lq, vAB, base_vab + ti - 1)
                else:
                    W(lq, gTN, gTN.n)
                EI(lq, dW0[j], lambda e, j=j, rt=rt, cc=cc, w=wsrc, cw=cw:
                   e.dma_start(out=w0t[j][:, 0:cw], in_=w[rt * 128:(rt + 1) * 128,
                                                         cc * cw:(cc + 1) * cw]), 16)
                W("vector", dW0[j], dW0[j].n)
                EI("vector", vAB, lambda e, j=j, ti=ti, cw=cw:
                   e.tensor_reduce(out=acc0[:, ti:ti + 1], in_=w0t[j][:, 0:cw],
                                   axis=AX, op=ALU.add, apply_absolute_value=True))
                ti += 1
        W("vector", vAB, vAB.n)
        EI("vector", vAB, lambda e, nt=ntile:
           e.tensor_reduce(out=acc0[:, 159:160], in_=acc0[:, 0:nt], axis=AX,
                           op=ALU.add))
        W("tensor", vAB, vAB.n)
        W("tensor", dW, NCONSTW)
        rAUX.acquire("tensor")
        EI("tensor", tPS, lambda e:
           e.matmul(out=pAUX[0:1, 20:21], lhsT=acc0[:, 159:160], rhs=s_onescf[:],
                    start=True, stop=True))
        W("vector", tPS, tPS.n)

        steps = [
            lambda e: e.tensor_copy(out=sc0[0:1, 0:1], in_=pAUX[0:1, 20:21]),
            lambda e, nel=nel: e.tensor_scalar(out=sc0[0:1, 1:2], in0=sc0[0:1, 0:1],
                                               scalar1=1.0 / nel, scalar2=None,
                                               op0=ALU.mult),
            lambda e: e.tensor_scalar(out=sc0[0:1, 2:3], in0=sc0[0:1, 1:2],
                                      scalar1=1e-5, scalar2=None, op0=ALU.max),
            lambda e: e.tensor_scalar(out=sc0[0:1, 3:4], in0=sc0[0:1, 2:3],
                                      scalar1=0.7, scalar2=None, op0=ALU.mult),
            lambda e: e.tensor_scalar(out=sc0[0:1, 4:5], in0=sc0[0:1, 3:4],
                                      scalar1=-1.0, scalar2=None, op0=ALU.mult),
            lambda e, cabs=cabs: e.tensor_copy(out=sc0[0:1, cabs:cabs + 1],
                                               in_=sc0[0:1, 2:3]),
        ]
        for stp in steps:
            EI("vector", vAB, stp)
            W("vector", vAB, vAB.n)
        W("tensor", vAB, vAB.n)
        rAUX.acquire("tensor")
        EI("tensor", tPS, lambda e:
           e.matmul(out=pAUX[:, 21:22], lhsT=s_onesrf[:], rhs=sc0[0:1, 3:4],
                    start=True, stop=True))
        EI("tensor", tPS, lambda e:
           e.matmul(out=pAUX[:, 22:23], lhsT=s_onesrf[:], rhs=sc0[0:1, 4:5],
                    start=True, stop=True))
        W("vector", tPS, tPS.n)

        def evac_thr(e, cthr=cthr):
            e.tensor_copy(out=scB[:, cthr:cthr + 1], in_=pAUX[:, 21:22])
            return e.tensor_copy(out=scB[:, cthr + 1:cthr + 2], in_=pAUX[:, 22:23])
        EI("vector", vAB, evac_thr)
        rAUX.release(vAB)
        base_gtn = gTN.n
        base_ds = [dS[0].n, dS[1].n]
        vab_thr = vAB.n
        ti = 0
        for rt in range(nrt):
            for cc in range(ncc):
                j = ti % 2
                lq = "sync" if j == 0 else "scalar"
                if ti >= 2:
                    W(lq, gTN, base_gtn + 3 * (ti - 1))
                else:
                    W(lq, vAB, vab_thr)
                EI(lq, dW0[j], lambda e, j=j, rt=rt, cc=cc, w=wsrc, cw=cw:
                   e.dma_start(out=w0t[j][:, 0:cw], in_=w[rt * 128:(rt + 1) * 128,
                                                         cc * cw:(cc + 1) * cw]), 16)
                W("vector", dW0[j], dW0[j].n)
                if ti >= 2:
                    W("vector", dS[j], base_ds[j] + 16 * (ti // 2))

                EI("vector", gTN, lambda e, j=j, c=cthr, cw=cw:
                   e.tensor_scalar(out=s0t[j][:, 0:cw], in0=w0t[j][:, 0:cw],
                                   scalar1=scB[:, c:c + 1], scalar2=None,
                                   op0=ALU.is_gt))
                EI("vector", gTN, lambda e, j=j, c=cthr, cw=cw:
                   e.tensor_scalar(out=w0t[j][:, 0:cw], in0=w0t[j][:, 0:cw],
                                   scalar1=scB[:, c + 1:c + 2], scalar2=None,
                                   op0=ALU.is_lt))
                W("vector", gTN, gTN.n)
                EI("vector", gTN, lambda e, j=j, cw=cw:
                   e.tensor_sub(out=s0t[j][:, 0:cw], in0=s0t[j][:, 0:cw],
                                in1=w0t[j][:, 0:cw]))
                W("gpsimd", gTN, gTN.n)
                EI("gpsimd", dS[j], lambda e, j=j, rt=rt, cc=cc, sd=sdst, cw=cw:
                   e.dma_start(out=sd[rt * 128:(rt + 1) * 128,
                                      cc * cw:(cc + 1) * cw], in_=s0t[j][:, 0:cw]), 16)
                ti += 1

    EI("vector", vAB, lambda e: e.tensor_mul(out=sc0[0:1, 8:9], in0=sc0[0:1, 6:7],
                                             in1=sc0[0:1, 6:7]))
    W("vector", vAB, vAB.n)
    EI("vector", vAB, lambda e: e.tensor_mul(out=sc0[0:1, 8:9], in0=sc0[0:1, 8:9],
                                             in1=sc0[0:1, 7:8]))
    W("scalar", vAB, vAB.n)

    EI("scalar", aC, lambda e: e.activation(out=sc0[0:1, 9:10], in_=sc0[0:1, 8:9],
                                            func=AF.Ln))
    W("scalar", aC, aC.n)
    EI("scalar", aC, lambda e: e.activation(out=sc0[0:1, 10:11],
                                            in_=sc0[0:1, 9:10], func=AF.Exp,
                                            scale=0.5))
    W("tensor", aC, aC.n)
    rAUX.acquire("tensor")
    EI("tensor", tPS, lambda e:
       e.matmul(out=pAUX[:, 23:24], lhsT=s_onesrf[:], rhs=sc0[0:1, 10:11],
                start=True, stop=True))
    W("vector", tPS, tPS.n)
    EI("vector", vAB, lambda e: e.tensor_copy(out=scB[:, 4:5], in_=pAUX[:, 23:24]))
    rAUX.release(vAB)

    if _dbg:
        o_scB = nc.dram_tensor("o_scB", [128, 8], F32, kind="ExternalOutput").ap()
        o_sc0 = nc.dram_tensor("o_sc0", [1, 16], F32, kind="ExternalOutput").ap()
        dDBG = C("dDBG")
        W("sync", vAB, vAB.n)
        EI("sync", dDBG, lambda e: e.dma_start(out=o_scB, in_=scB[:]), 16)
        EI("sync", dDBG, lambda e: e.dma_start(out=o_sc0, in_=sc0[:]), 16)
        W("sync", dDBG, 32)

    fence()
    if _PH <= 1:
        return _finish()

    dX = [C("dXa"), C("dXb")]
    aSQ = C("aSQ")
    vRED = C("vRED")
    vST = C("vST")
    aRS = C("aRS")
    vXN = C("vXN")
    tTR = C("tTR")
    vTE = C("vTE")

    def emit_ln_tile(src_hbm, tt, xnsq, stats, xnT_dst, dve_prewait=None):
        j = tt % 2
        if vXN.n >= 2:
            W("sync", vXN, vXN.n - 1)
        EI("sync", dX[j], lambda e, j=j, tt=tt, src=src_hbm:
           e.dma_start(out=s_xf[j][:], in_=src[tt * 128:(tt + 1) * 128, :]), 16)
        ldx = dX[j].n
        W("scalar", dX[j], ldx)
        W("scalar", tTR, tTR.n)
        EI("scalar", aSQ, lambda e, j=j, xnsq=xnsq, stats=stats:
           e.activation(out=xnsq[:], in_=s_xf[j][:], func=AF.Square,
                        accum_out=stats[:, 1:2]))
        W("vector", dX[j], ldx)
        EI("vector", vRED, lambda e, j=j, stats=stats:
           e.tensor_reduce(out=stats[:, 0:1], in_=s_xf[j][:], axis=AX, op=ALU.add))
        W("vector", vRED, vRED.n)
        W("vector", aSQ, aSQ.n)

        EI("vector", vST, lambda e, stats=stats:
           e.tensor_scalar(out=stats[:, 2:3], in0=stats[:, 0:1],
                           scalar1=-1.0 / D, scalar2=None, op0=ALU.mult))
        EI("vector", vST, lambda e, stats=stats:
           e.tensor_scalar(out=stats[:, 3:4], in0=stats[:, 1:2],
                           scalar1=1.0 / D, scalar2=None, op0=ALU.mult))
        W("vector", vST, vST.n)
        EI("vector", vST, lambda e, stats=stats:
           e.scalar_tensor_tensor(out=stats[:, 4:5], in0=stats[:, 2:3],
                                  scalar=stats[:, 2:3], in1=stats[:, 3:4],
                                  op0=ALU.mult, op1=ALU.subtract))
        W("scalar", vST, vST.n)

        def rstd_chain(e, stats=stats):
            e.activation(out=stats[:, 5:6], in_=stats[:, 4:5], func=AF.Ln,
                         bias=s_eps[:], scale=-1.0)
            return e.activation(out=stats[:, 6:7], in_=stats[:, 5:6], func=AF.Exp,
                                scale=-0.5)
        EI("scalar", aRS, rstd_chain)
        W("vector", aRS, aRS.n)
        if dve_prewait is not None:
            dve_prewait()
        EI("vector", vXN, lambda e, j=j, xnsq=xnsq, stats=stats:
           e.tensor_scalar(out=xnsq[:], in0=s_xf[j][:], scalar1=stats[:, 2:3],
                           scalar2=stats[:, 6:7], op0=ALU.add, op1=ALU.mult))
        r = tt % 4
        W("tensor", vXN, vXN.n)
        for f in range(2):
            rLNB.acquire("tensor")
            for kk in range(8):
                k = 8 * f + kk
                fn = lambda e, k=k, kk=kk, xnsq=xnsq: e.transpose(
                    out=pLNB[:, kk * 128:(kk + 1) * 128],
                    in_=xnsq[:, k * 128:(k + 1) * 128], identity=s_id[:])
                if kk == 7:
                    EI("tensor", tTR, fn)
                else:
                    E("tensor", fn)
            W("vector", tTR, tTR.n)

            def evac(e, f=f, r=r, dst=xnT_dst):
                o = dst[:, :].rearrange("p (k q) -> p k q", k=NKC, q=SS)
                o = o[:, 8 * f:8 * f + 8, r * 128:(r + 1) * 128]
                return e.tensor_copy(
                    out=o, in_=pLNB[:, 0:1024].rearrange("p (k q) -> p k q",
                                                         k=8, q=128))
            EI("vector", vTE, evac)
            rLNB.release(vTE)

    dWG = C("dWG")
    W("gpsimd", dS[0], dS[0].n)
    W("gpsimd", dS[1], dS[1].n)
    for k in range(NKC):
        EI("gpsimd", dWG, lambda e, k=k:
           e.dma_start(out=s_wk[:, k * 512:(k + 1) * 512],
                       in_=wk[k * 128:(k + 1) * 128, :]), 16)
    for k in range(NKC):
        EI("gpsimd", dWG, lambda e, k=k:
           e.dma_start(out=s_wv[:, k * 512:(k + 1) * 512],
                       in_=wv[k * 128:(k + 1) * 128, :]), 16)
    NWKV = dWG.n

    dXN_ST = C("dXN_ST")
    tKV = C("tKV")
    aKE = C("aKE")
    dKS = C("dKS")
    vKR = C("vKR")
    aVE = C("aVE")

    for s in range(NSS):
        for r in range(4):
            tt = 4 * s + r

            def dve_pre():
                if s >= 1:
                    W("vector", dXN_ST, 16 * s)
                    W("vector", tKV, 8 * s)
            emit_ln_tile(xin, tt, s_xnsq1, s_st1, s_xnT1,
                         dve_prewait=dve_pre if r == 0 else None)
        W("sync", vTE, vTE.n)
        EI("sync", dXN_ST, lambda e, s=s:
           e.dma_start(out=xnT_hbm[s], in_=s_xnT1[:]), 16)
        W("tensor", vTE, vTE.n)
        W("tensor", dWG, NWKV)
        for kvh in range(NKV):
            p = kvh % 2
            rSC[p].acquire("tensor")
            for k in range(NKC):
                fn = lambda e, k=k, kvh=kvh, p=p: e.matmul(
                    out=pSC[p][:, 0:512],
                    lhsT=s_wk[:, k * 512 + kvh * 128:k * 512 + (kvh + 1) * 128],
                    rhs=s_xnT1[:, k * SS:(k + 1) * SS],
                    start=(k == 0), stop=(k == NKC - 1))
                if k == NKC - 1:
                    EI("tensor", tKV, fn)
                else:
                    E("tensor", fn)
            j = kvh % 2
            W("scalar", tKV, tKV.n)
            if vKR.n >= 2:
                W("scalar", vKR, vKR.n - 1)
            EI("scalar", aKE, lambda e, p=p, j=j:
               e.copy(out=s_ktmp[j][:], in_=pSC[p][:, 0:512]))
            rSC[p].release(aKE)
            W("sync", aKE, aKE.n)
            if vKR.n >= 2:
                W("sync", vKR, vKR.n - 1)
            E("sync", lambda e, j=j, sh=dKS.h: (
                e.dma_start(out=s_kshf[j][0:64, :],
                            in_=s_ktmp[j][64:128, :]).then_inc(sh, 16),
                e.dma_start(out=s_kshf[j][64:128, :],
                            in_=s_ktmp[j][0:64, :]).then_inc(sh, 16)))
            dKS.inc(32)
            W("vector", aKE, aKE.n)
            W("vector", dKS, dKS.n)

            def rope_k(e, j=j, s=s, kvh=kvh):
                e.tensor_mul(out=s_kt1[:], in0=s_ktmp[j][:],
                             in1=s_cosT[:, s * SS:(s + 1) * SS])
                e.tensor_mul(out=s_kt2[:], in0=s_kshf[j][:],
                             in1=s_sinsT[:, s * SS:(s + 1) * SS])
                return e.tensor_add(
                    out=s_K[:, kvh * S + s * SS:kvh * S + (s + 1) * SS],
                    in0=s_kt1[:], in1=s_kt2[:])
            EI("vector", vKR, rope_k)
        for tb in range(4):
            p = tb % 2
            cpos = 4 * s + tb
            rSC[p].acquire("tensor")
            for k in range(NKC):
                fn = lambda e, k=k, tb=tb, p=p: e.matmul(
                    out=pSC[p][:, 0:512],
                    lhsT=s_xnT1[:, k * SS + tb * 128:k * SS + (tb + 1) * 128],
                    rhs=s_wv[:, k * 512:(k + 1) * 512],
                    start=(k == 0), stop=(k == NKC - 1))
                if k == NKC - 1:
                    EI("tensor", tKV, fn)
                else:
                    E("tensor", fn)
            W("scalar", tKV, tKV.n)

            def evac_v(e, p=p, cpos=cpos):
                return e.copy(out=s_V[:, cpos * 512:(cpos + 1) * 512],
                              in_=pSC[p][:, 0:512])
            EI("scalar", aVE, evac_v)
            rSC[p].release(aVE)

    fence()
    if _PH <= 2:
        return _finish()

    for k in range(NKC):
        EI("gpsimd", dWG, lambda e, k=k:
           e.dma_start(out=s_wq[:, k * D:(k + 1) * D],
                       in_=wq[k * 128:(k + 1) * 128, :]), 16)
    NWQ = dWG.n
    vMS = C("vMS")
    EI("vector", vMS, lambda e: (e.memset(s_mbias[0][:], 0.0),
                                 e.memset(s_mbias[1][:], 0.0),
                                 e.memset(s_Pc[0][:], 0.0),
                                 e.memset(s_Pc[1][:], 0.0),
                                 e.memset(s_Pc[2][:], 0.0))[-1])

    dXN_LD = C("dXN_LD")
    tQP = C("tQP")
    aQE = C("aQE")
    dQS = C("dQS")
    vQR = C("vQR")
    tSC = [C("tSC0"), C("tSC1")]
    vMX = [C("vMX0"), C("vMX1")]
    aEX = [C("aEX0"), C("aEX1")]
    gMK = [C("gMK0"), C("gMK1")]
    tPV = [C("tPV0"), C("tPV1")]
    aRC = [C("aRC0"), C("aRC1")]
    vCX = [C("vCX0"), C("vCX1")]
    dCTX = C("dCTX")
    dMB = C("dMB")
    vMB = C("vMB")
    tMB = C("tMB")
    aMB = C("aMB")
    vMBE = C("vMBE")

    tSC2 = C("tSC2")
    aEX2 = C("aEX2")
    gMK2 = C("gMK2")
    tPV2 = [C("tPV2a"), C("tPV2b")]
    aRC2 = [C("aRC2a"), C("aRC2b")]
    tRB2 = [C("tRB2a"), C("tRB2b")]
    vRB2 = [C("vRB2a"), C("vRB2b")]
    vCX2 = [C("vCX2a"), C("vCX2b")]
    vMX2 = C("vMX2")
    dCTX2 = C("dCTX2")
    dMB2 = C("dMB2")
    vMB2 = C("vMB2")
    aMB2 = C("aMB2")
    tMB2 = C("tMB2")
    vMBE2 = C("vMBE2")
    rSCB = [Res(), Res()]
    rCXB = [Res(), Res()]
    rSMB = [Res(), Res()]
    sp_guard = {0: [], 1: [], 2: []}
    scb = [lambda a, b: pSC[0][:, a:b], lambda a, b: pSC[0][:, 512 + a:512 + b]]
    ctxb = [lambda a, b: pSC[1][:, a:b], lambda a, b: pSC[1][:, 512 + a:512 + b]]
    sumb = [lambda a, b: pCX[0][0:1, a:b], lambda a, b: pCX[1][0:1, a:b]]
    rbb = [lambda a, b: pCX[0][:, a:b], lambda a, b: pCX[1][:, a:b]]

    def emit_group(s, h):
        g = 16 * s + h
        pg = g % 2
        kv = h // GQ
        c0 = max(0, 4 * s - 8)
        chunks = list(range(c0, 4 * s + 4))
        geo = []
        for c in chunks:
            rel = c - 4 * s
            q0 = max(0, rel * 128)
            q1 = min(512, rel * 128 + 1152)
            tri = None
            if 0 <= rel <= 3:
                tri = ("c", rel * 128)
            elif -8 <= rel <= -5:
                tri = ("f", rel * 128 + 1024)
            geo.append((c, q0, q1, tri))
        geo.sort(key=lambda x: 0 if x[0] == 4 * s else 1)
        nch = len(geo)
        exp_marks = []

        def emit_chunk_scores(ci):
            c, q0, q1, tri = geo[ci]
            bk = ci % 2
            rSCB[bk].acquire("tensor")
            if ci == 0:
                W("tensor", vQR, 16 * s + h + 1)
            EI("tensor", tSC2, lambda e, c=c, q0=q0, q1=q1, bk=bk:
               e.matmul(out=scb[bk](q0, q1),
                        lhsT=s_K[:, kv * S + c * 128:kv * S + (c + 1) * 128],
                        rhs=s_QT[:, h * SS + q0:h * SS + q1],
                        start=True, stop=True))
            W("scalar", tSC2, tSC2.n)
            if s >= 1:
                W("scalar", vMBE2, 2 * s)
            else:
                W("scalar", vMS, 1)
            sp = ci % 3
            for cc, vv in sp_guard[sp]:
                W("scalar", cc, vv)
            sp_guard[sp] = []
            EI("scalar", aEX2, lambda e, sp=sp, q0=q0, q1=q1, bk=bk:
               e.activation(out=s_Pc[sp][:, q0:q1], in_=scb[bk](q0, q1),
                            func=AF.Exp, bias=s_mbias[s % 2][:, h:h + 1],
                            scale=ISQ))
            rSCB[bk].release(aEX2)
            if tri is not None:
                W("gpsimd", aEX2, aEX2.n)
                kind, t0c = tri
                mk = s_mcT if kind == "c" else s_mfT
                EI("gpsimd", gMK2, lambda e, sp=sp, t0c=t0c, mk=mk:
                   e.tensor_mul(out=s_Pc[sp][:, t0c:t0c + 128],
                                in0=s_Pc[sp][:, t0c:t0c + 128], in1=mk[:]))
                exp_marks.append((ci, (gMK2, gMK2.n)))
            else:
                exp_marks.append((ci, (aEX2, aEX2.n)))

        def emit_chunk_pv(ci):
            c, q0, q1, tri = geo[ci]
            sp = ci % 3
            cc, vv = exp_marks[ci][1]
            W("tensor", cc, vv)
            if ci == 0:
                rCXB[pg].acquire("tensor")
                rSMB[pg].acquire("tensor")
            vb = (c * NKV + kv) * 128
            E("tensor", lambda e, sp=sp, q0=q0, q1=q1, vb=vb, ci=ci:
              e.matmul(out=ctxb[pg](q0, q1),
                       lhsT=s_V[:, vb:vb + 128],
                       rhs=s_Pc[sp][:, q0:q1],
                       start=(ci == 0), stop=(ci == nch - 1)))
            fn = lambda e, sp=sp, q0=q0, q1=q1, ci=ci: e.matmul(
                out=sumb[pg](q0, q1),
                lhsT=s_onescb[:],
                rhs=s_Pc[sp][:, q0:q1],
                start=(ci == 0), stop=(ci == nch - 1))
            EI("tensor", tPV2[pg], fn)
            sp_guard[sp].append((tPV2[pg], tPV2[pg].n))

        emit_chunk_scores(0)
        for ci in range(1, nch):
            emit_chunk_scores(ci)
            emit_chunk_pv(ci - 1)
        emit_chunk_pv(nch - 1)
        W("vector", exp_marks[-1][1][0], exp_marks[-1][1][1])
        if s >= 2:
            W("vector", dMB2, 16 * (s - 1))
        EI("vector", vMX2, lambda e, sp=(nch - 1) % 3:
           e.reduce_max(out=s_mcols[s % 2][:, h:h + 1], in_=s_Pc[sp][:, 0:512],
                        axis=AX))
        sp_guard[(nch - 1) % 3].append((vMX2, vMX2.n))
        W("scalar", tPV2[pg], tPV2[pg].n)
        W("scalar", tRB2[0], tRB2[0].n)
        W("scalar", tRB2[1], tRB2[1].n)
        EI("scalar", aRC2[pg], lambda e:
           e.activation(out=s_rrow[:], in_=sumb[pg](0, 512), func=AF.Ln))
        W("scalar", aRC2[pg], aRC2[pg].n)
        EI("scalar", aRC2[pg], lambda e:
           e.activation(out=s_rrow[:], in_=s_rrow[:], func=AF.Exp, scale=-1.0))
        W("tensor", aRC2[pg], aRC2[pg].n)
        EI("tensor", tRB2[pg], lambda e: e.matmul(
            out=rbb[pg](0, 512), lhsT=s_onesrb[:], rhs=s_rrow[0:1, 0:512],
            start=True, stop=True))
        W("vector", tRB2[pg], tRB2[pg].n)
        if g >= 2:
            W("vector", dCTX2, 16 * (g - 1))
        EI("vector", vRB2[pg], lambda e:
           e.tensor_copy(out=s_rb[pg][:], in_=rbb[pg](0, 512)))
        rSMB[pg].release(vRB2[pg])
        EI("vector", vCX2[pg], lambda e:
           e.tensor_mul(out=s_ctx1[g % 2][:], in0=ctxb[pg](0, 512),
                        in1=s_rb[pg][:]))
        rCXB[pg].release(vCX2[pg])
        W("sync", vCX2[pg], vCX2[pg].n)
        EI("sync", dCTX2, lambda e, s=s, h=h, g=g: e.dma_start(
            out=ctxT_hbm[s][:, h * SS:(h + 1) * SS], in_=s_ctx1[g % 2][:]), 16)

    def emit_mbias2(s):
        W("sync", vMX2, vMX2.n)
        EI("sync", dMB2, lambda e, s=s: e.dma_start(
            out=s_mrow[0:1, :].rearrange("o (p c) -> o p c", p=128, c=16),
            in_=s_mcols[s % 2][:, 0:16]), 16)
        W("vector", dMB2, dMB2.n)
        EI("vector", vMB2, lambda e: e.tensor_reduce(
            out=s_negrow[:], in_=s_mrow[0:1, :].rearrange(
                "o (p c) -> o c p", p=128, c=16), axis=AX, op=ALU.max))
        W("scalar", vMB2, vMB2.n)
        EI("scalar", aMB2, lambda e:
           e.activation(out=s_negrow[:], in_=s_negrow[:], func=AF.Ln))
        W("tensor", aMB2, aMB2.n)
        rAUX.acquire("tensor")
        EI("tensor", tMB2, lambda e: e.matmul(
            out=pAUX[:, 0:16], lhsT=s_onesrb[:], rhs=s_negrow[0:1, 0:16],
            start=True, stop=True))
        W("vector", tMB2, tMB2.n)
        W("vector", aEX2, aEX2.n)

        EI("vector", vMBE2, lambda e, s=s:
           e.tensor_copy(out=s_mbias[(s + 1) % 2][:], in_=s_mbias[s % 2][:]))
        W("vector", vMBE2, vMBE2.n)
        EI("vector", vMBE2, lambda e, s=s:
           e.tensor_sub(out=s_mbias[(s + 1) % 2][:],
                        in0=s_mbias[(s + 1) % 2][:], in1=pAUX[:, 0:16]))
        rAUX.release(vMBE2)

    for s in range(NSS):
        if s >= 1:
            W("sync", tQP, 16 * s)
        W("sync", dXN_ST, 16 * (s + 1))
        EI("sync", dXN_LD, lambda e, s=s:
           e.dma_start(out=s_xnT[:], in_=xnT_hbm[s]), 16)
        W("tensor", dXN_LD, dXN_LD.n)
        W("tensor", dWG, NWQ)
        for h in range(NH):
            p = h % 2
            (rSCB[0] if p == 0 else rCXB[0]).acquire("tensor")
            if s > 0 or h >= 2:
                W("tensor", aQE, aQE.n - 1)
            for k in range(NKC):
                fn = lambda e, k=k, h=h, p=p: e.matmul(
                    out=pSC[p][:, 0:512],
                    lhsT=s_wq[:, k * D + h * 128:k * D + (h + 1) * 128],
                    rhs=s_xnT[:, k * SS:(k + 1) * SS],
                    start=(k == 0), stop=(k == NKC - 1))
                if k == NKC - 1:
                    EI("tensor", tQP, fn)
                else:
                    E("tensor", fn)
            j = h % 2
            W("scalar", tQP, tQP.n)
            if vQR.n >= 2:
                W("scalar", vQR, vQR.n - 1)
            EI("scalar", aQE, lambda e, p=p, j=j:
               e.copy(out=s_qtmp[j][:], in_=pSC[p][:, 0:512]))
            (rSCB[0] if p == 0 else rCXB[0]).release(aQE)
            W("sync", aQE, aQE.n)
            if vQR.n >= 2:
                W("sync", vQR, vQR.n - 1)
            E("sync", lambda e, j=j, sh=dQS.h: (
                e.dma_start(out=s_qshf[j][0:64, :],
                            in_=s_qtmp[j][64:128, :]).then_inc(sh, 16),
                e.dma_start(out=s_qshf[j][64:128, :],
                            in_=s_qtmp[j][0:64, :]).then_inc(sh, 16)))
            dQS.inc(32)
            W("vector", aQE, aQE.n)
            W("vector", dQS, dQS.n)
            if h == 0 and s > 0:
                W("vector", tSC2, tSC2.n)

            def rope_q(e, j=j, s=s, h=h):
                e.tensor_mul(out=s_t1[:], in0=s_qtmp[j][:],
                             in1=s_cosT[:, s * SS:(s + 1) * SS])
                e.tensor_mul(out=s_t2[:], in0=s_qshf[j][:],
                             in1=s_sinsT[:, s * SS:(s + 1) * SS])
                return e.tensor_add(out=s_QT[:, h * SS:(h + 1) * SS],
                                    in0=s_t1[:], in1=s_t2[:])
            EI("vector", vQR, rope_q)
        W("tensor", aQE, aQE.n)
        for bk in range(2):
            rSCB[bk].acquire("tensor")
        for h in range(NH):
            emit_group(s, h)
        emit_mbias2(s)

    fence()
    if _PH <= 3:
        return _finish()

    for k in range(NKC):
        EI("gpsimd", dWG, lambda e, k=k:
           e.dma_start(out=s_wo[:, k * D:(k + 1) * D],
                       in_=wo[k * 128:(k + 1) * 128, :]), 16)
    NWO = dWG.n
    dCT_LD = C("dCT_LD")
    dX2 = C("dX2")
    dX1_ST = C("dX1_ST")
    tOU = [C("tOU0"), C("tOU1")]
    vOU = [C("vOU0"), C("vOU1")]
    pc2 = [0]
    base_tou = tOU[0].n
    for s in range(NSS):
        if s >= 1:
            W("sync", tOU[0], base_tou + 4 * s)
        EI("sync", dCT_LD, lambda e, s=s: e.dma_start(
            out=s_ctxT[:], in_=ctxT_hbm[s]), 16)
        for tb in range(4):
            t = 4 * s + tb
            if t >= 2:
                W("sync", vOU[0], vOU[0].n)
            EI("sync", dX2, lambda e, t=t:
               e.dma_start(out=s_xf[t % 2][:], in_=xin[t * 128:(t + 1) * 128, :]), 16)
            W("tensor", dCT_LD, 16 * (s + 1))
            W("tensor", dWG, NWO)
            W("tensor", vOU[0], vOU[0].n)
            pb = [(pSC[0], 0), (pSC[0], 512), (pSC[1], 0), (pSC[1], 512)]
            for h in range(NH):
                for n in range(4):
                    bt, bo = pb[n]
                    fn = lambda e, h=h, tb=tb, n=n, bt=bt, bo=bo: e.matmul(
                        out=bt[:, bo:bo + 512],
                        lhsT=s_ctxT[:, h * SS + tb * 128:h * SS + (tb + 1) * 128],
                        rhs=s_wo[:, h * D + n * 512:h * D + (n + 1) * 512],
                        start=(h == 0), stop=(h == NH - 1))
                    if h == NH - 1 and n == 3:
                        EI("tensor", tOU[0], fn)
                    else:
                        E("tensor", fn)
            W("vector", tOU[0], tOU[0].n)
            W("vector", dX2, 16 * (t + 1))
            if t >= 2:
                W("vector", dX1_ST, 16 * (t - 1))

            def evac2b(e, t=t, pb=pb):
                for n in range(4):
                    bt, bo = pb[n]
                    i = e.tensor_add(out=s_stage[t % 2][:, n * 512:(n + 1) * 512],
                                     in0=bt[:, bo:bo + 512],
                                     in1=s_xf[t % 2][:, n * 512:(n + 1) * 512])
                return i
            EI("vector", vOU[0], evac2b)
            W("sync", vOU[0], vOU[0].n)
            EI("sync", dX1_ST, lambda e, t=t:
               e.dma_start(out=x1_hbm[t * 128:(t + 1) * 128, :],
                           in_=s_stage[t % 2][:]), 16)

    fence()
    if _PH <= 4:
        return _finish()

    dSF = C("dSF")
    for k in range(NKC):
        EI("sync", dSF, lambda e, k=k: e.dma_start(
            out=s_SfcT[:, k * HF:(k + 1) * HF],
            in_=sfc_hbm[:, k * 128:(k + 1) * 128], transpose=True), 16)
    tH = [C("tH0"), C("tH1")]
    aH = [C("aH0"), C("aH1")]
    dH_ST = C("dH_ST")
    for s in range(NSS):
        for r in range(4):
            tt = 4 * s + r

            def dve_pre3():
                if s >= 1:
                    W("vector", tH[0], (NHC * s) // 2)
                    W("vector", tH[1], (NHC * s) // 2)
            emit_ln_tile(x1_hbm, tt, s_xnsq3, s_st3, s_xn2T,
                         dve_prewait=dve_pre3 if r == 0 else None)
        W("tensor", vTE, vTE.n)
        W("tensor", dSF, 16 * NKC)
        for hh in range(NHC):
            p = hh % 2
            rSC[p].acquire("tensor")
            for k in range(NKC):
                fn = lambda e, k=k, hh=hh, p=p: e.matmul(
                    out=pSC[p][:, 0:512],
                    lhsT=s_SfcT[:, k * HF + hh * 128:k * HF + (hh + 1) * 128],
                    rhs=s_xn2T[:, k * SS:(k + 1) * SS],
                    start=(k == 0), stop=(k == NKC - 1))
                if k == NKC - 1:
                    EI("tensor", tH[p], fn)
                else:
                    E("tensor", fn)
            jb = (hh // 4) % 2
            W("scalar", tH[p], tH[p].n)
            bidx = (NHC // 4) * s + hh // 4
            if hh % 4 == 0 and bidx >= 2:
                W("scalar", dH_ST, 16 * (bidx - 1))

            def relu_sq(e, p=p, jb=jb, hh=hh):
                e.activation(out=pAUX[:, 0:512], in_=pSC[p][:, 0:512],
                             func=AF.Relu, scale=scB[:, 4:5])
                return e.activation(
                    out=s_hst[jb][:, (hh % 4) * 512:(hh % 4 + 1) * 512],
                    in_=pAUX[:, 0:512], func=AF.Square)
            EI("scalar", aH[p], relu_sq)
            rSC[p].release(aH[p])
            if hh % 4 == 3:
                W("sync", aH[0], aH[0].n)
                W("sync", aH[1], aH[1].n)
                EI("sync", dH_ST, lambda e, s=s, hh=hh, jb=jb: e.dma_start(
                    out=h_hbm[s][:, (hh - 3) * 512:(hh + 1) * 512],
                    in_=s_hst[jb][:]), 16)

    fence()
    if _PH <= 5:
        return _finish()

    dSP = C("dSP")
    for jc in range(NHC):
        EI("sync", dSP, lambda e, jc=jc: e.dma_start(
            out=s_SpT[:, jc * D:(jc + 1) * D],
            in_=sp_hbm[:, jc * 128:(jc + 1) * 128], transpose=True), 16)
    dH_LD = C("dH_LD")
    dX1_LD = C("dX1_LD")
    dY = C("dY")
    pc3 = [0]
    for t in range(NT):
        s, tb = t // 4, t % 4
        if t >= 2:
            W("sync", vOU[0], vOU[0].n)

        def ht_in(e, s=s, tb=tb, j=t % 2):
            src = h_hbm[s].rearrange("p (c q) -> p c q", c=NHC, q=SS)
            src = src[:, :, tb * 128:(tb + 1) * 128]
            return e.dma_start(out=s_ht[j][:, :].rearrange(
                "p (c q) -> p c q", c=NHC, q=128), in_=src)
        EI("sync", dH_LD, ht_in, 16)
        EI("sync", dX1_LD, lambda e, t=t: e.dma_start(
            out=s_xf[t % 2][:], in_=x1_hbm[t * 128:(t + 1) * 128, :]), 16)
        W("tensor", dSP, 16 * NHC)
        W("tensor", dH_LD, 16 * (t + 1))
        W("tensor", vOU[0], vOU[0].n)
        pb = [(pSC[0], 0), (pSC[0], 512), (pSC[1], 0), (pSC[1], 512)]
        for jc in range(NHC):
            for n in range(4):
                bt, bo = pb[n]
                fn = lambda e, jc=jc, n=n, bt=bt, bo=bo, jj=t % 2: e.matmul(
                    out=bt[:, bo:bo + 512],
                    lhsT=s_ht[jj][:, jc * 128:(jc + 1) * 128],
                    rhs=s_SpT[:, jc * D + n * 512:jc * D + (n + 1) * 512],
                    start=(jc == 0), stop=(jc == NHC - 1))
                if jc == NHC - 1 and n == 3:
                    EI("tensor", tOU[0], fn)
                else:
                    E("tensor", fn)
        W("vector", tOU[0], tOU[0].n)
        W("vector", dX1_LD, 16 * (t + 1))
        if t >= 1:
            W("vector", dY, 16 * t)

        def evac3b(e, t=t, pb=pb):
            for n in range(4):
                bt, bo = pb[n]
                i = e.tensor_add(out=s_yst[:, n * 512:(n + 1) * 512],
                                 in0=bt[:, bo:bo + 512],
                                 in1=s_xf[t % 2][:, n * 512:(n + 1) * 512])
            return i
        EI("vector", vOU[0], evac3b)
        W("sync", vOU[0], vOU[0].n)
        EI("sync", dY, lambda e, t=t: e.dma_start(
            out=y[t * 128:(t + 1) * 128, :], in_=s_yst[:]), 16)
    W("sync", dY, 16 * NT)
    return _finish()


def _consts():
    inv = 1.0 / (10000.0 ** (np.arange(0, HD, 2, dtype=np.float32) / HD))
    t = np.arange(S, dtype=np.float32)
    f = np.concatenate([inv, inv])
    ang = f[:, None] * t[None, :]
    cosT = np.cos(ang).astype(bfnp)
    sins = np.sin(ang).astype(np.float32)
    sins[:HD // 2] *= -1.0
    sinsT = sins.astype(bfnp)
    jj = np.arange(128)
    mcT = (jj[:, None] <= jj[None, :]).astype(bfnp)
    mfT = (jj[:, None] >= jj[None, :]).astype(bfnp)
    return dict(
        cosT_c=cosT, sinsT_c=sinsT, mcT_c=mcT, mfT_c=mfT,
        ident_c=np.eye(128, dtype=bfnp),
        onescb_c=np.ones((128, 1), dtype=bfnp),
        onesrb_c=np.ones((1, 128), dtype=bfnp),
        onesrf_c=np.ones((1, 128), dtype=np.float32),
        onescf_c=np.ones((128, 1), dtype=np.float32),
        epsc_c=np.full((128, 1), 1e-5, dtype=np.float32),
    )


def kernel(L0_attn_hidden_states, L0_attn_w_q, L0_attn_w_k, L0_attn_w_v,
           L0_attn_w_o, L0_ffn_w_fc, L0_ffn_w_proj):
    x = np.ascontiguousarray(np.asarray(L0_attn_hidden_states, dtype=np.float32))
    base = dict(
        wq=np.ascontiguousarray(L0_attn_w_q, dtype=np.float32),
        wk=np.ascontiguousarray(L0_attn_w_k, dtype=np.float32),
        wv=np.ascontiguousarray(L0_attn_w_v, dtype=np.float32),
        wo=np.ascontiguousarray(L0_attn_w_o, dtype=np.float32),
        wfc=np.ascontiguousarray(L0_ffn_w_fc, dtype=np.float32),
        wpj=np.ascontiguousarray(L0_ffn_w_proj, dtype=np.float32),
        **_consts(),
    )
    nc = build_nc()
    in_maps = [dict(xin=np.ascontiguousarray(x[b]), **base) for b in range(8)]
    res = run_bass_kernel_spmd(nc, in_maps, core_ids=list(range(8)))
    out = np.stack([np.asarray(r["y"], dtype=np.float32) for r in res.results])
    return out


# version 08
# speedup vs baseline: 1.0662x; 1.0237x over previous
import math
import os
import numpy as np
import ml_dtypes
import concourse.bass as bass
import concourse.mybir as mybir
import concourse.bass_utils as _bu
from concourse.bass_utils import run_bass_kernel_spmd

if not getattr(_bu, "_ldwopt_patched", False):
    _bu._ldwopt_patched = True
    _orig_run_command = _bu.run_command

    def _run_command_ldwopt(cmd, **kw):
        cmd = ["--enable-ldw-opt=true" if c == "--enable-ldw-opt=false" else c
               for c in cmd]
        return _orig_run_command(cmd, **kw)

    _bu.run_command = _run_command_ldwopt

F32 = mybir.dt.float32
BF16 = mybir.dt.bfloat16
U8 = mybir.dt.uint8
AX = mybir.AxisListType.X
AF = mybir.ActivationFunctionType
ALU = mybir.AluOpType
bfnp = ml_dtypes.bfloat16

S, D, NH, NKV, HD, WIN, HF = 4096, 2048, 16, 4, 128, 1024, 5120
NT, SS, NSS = 32, 512, 8
NKC = D // 128
NHC = HF // 128
ISQ = 1.0 / math.sqrt(HD)
GQ = NH // NKV

ENGS = ("sync", "vector", "scalar", "tensor", "gpsimd")


class Ctr:
    __slots__ = ("h", "n")

    def __init__(self, h):
        self.h = h
        self.n = 0

    def inc(self, k=1):
        self.n += k
        return self.n


def build_nc():
    nc = bass.Bass("TRN2", target_bir_lowering=False,
                   detect_race_conditions=False)
    din = lambda n, s, d=F32: nc.dram_tensor(n, s, d, kind="ExternalInput").ap()
    _dbg = os.environ.get("KDBG") == "1"
    dint = lambda n, s, d: nc.dram_tensor(
        n, s, d, kind="ExternalOutput" if _dbg else "Internal").ap()

    xin = din("xin", [S, D])
    wq = din("wq", [D, D])
    wk = din("wk", [D, NKV * HD])
    wv = din("wv", [D, NKV * HD])
    wo = din("wo", [D, D])
    wfc = din("wfc", [HF, D])
    wpj = din("wpj", [D, HF])
    cosT_c = din("cosT_c", [128, S], BF16)
    sinsT_c = din("sinsT_c", [128, S], BF16)
    mcT_c = din("mcT_c", [128, 128], BF16)
    mfT_c = din("mfT_c", [128, 128], BF16)
    ident_c = din("ident_c", [128, 128], BF16)
    onesrb_c = din("onesrb_c", [1, 128], BF16)
    onesrf_c = din("onesrf_c", [1, 128], F32)
    onescf_c = din("onescf_c", [128, 1], F32)
    onescb_c = din("onescb_c", [128, 1], BF16)
    epsc_c = din("epsc_c", [128, 1], F32)
    y = nc.dram_tensor("y", [S, D], F32, kind="ExternalOutput").ap()

    xnT_hbm = dint("xnT_hbm", [NSS, 128, NKC * SS], BF16)
    ctxT_hbm = dint("ctxT_hbm", [NSS, 128, NH * SS], BF16)
    x1_hbm = dint("x1_hbm", [S, D], F32)
    h_hbm = dint("h_hbm", [NSS, 128, NHC * SS], BF16)
    sfc_hbm = dint("sfc_hbm", [HF, D], BF16)
    sp_hbm = dint("sp_hbm", [D, HF], BF16)

    from contextlib import ExitStack
    es = ExitStack()

    ARENA = 211968
    arena = es.enter_context(nc.sbuf_tensor("arena", [128, ARENA], U8))
    arena_addr = nc.lookup_mloc(arena).addr

    def place(name, shape, dtype, off):
        assert off % 32 == 0, name
        per = int(np.prod(shape[1:])) * mybir.dt.size(dtype)
        assert off + per <= ARENA, (name, off, per)
        h = nc.virtual_sbuf_tensor(name, list(shape), dtype)
        nc._tensor(h.name, h.shape, h.dtype, type="SB")
        mloc = nc.lookup_mloc(h)
        mloc.allocated = True
        mloc.addr = arena_addr + off
        return h

    w0t = [place(f"w0t{i}", [128, 2048], F32, 143360 + i * 8192) for i in range(2)]
    s0t = [place(f"s0t{i}", [128, 2048], BF16, 159744 + i * 4096) for i in range(2)]
    acc0 = place("acc0", [128, 160], F32, 167936)
    s_xf = [place(f"s_xf{i}", [128, D], F32, i * 8192) for i in range(2)]
    s_mrow = place("s_mrow", [1, 2048], BF16, 0)
    s_Pc = [place(f"s_Pc{i}", [128, 512], BF16, 4096 + i * 1024) for i in range(3)]
    s_mcols = [place(f"s_mcols{i}", [128, 32], BF16, 8192 + i * 64) for i in range(2)]
    s_negrow32 = place("s_negrow32", [1, 32], BF16, 8320)
    s_negrow = place("s_negrow", [1, 16], BF16, 8384)
    s_mbias = [place(f"s_mbias{i}", [128, 16], F32, 8448 + i * 64) for i in range(2)]
    s_lnr = [place(f"s_lnr{i}", [128, 1], F32, 8576 + i * 32) for i in range(2)]
    s_r = [place(f"s_r{i}", [128, 1], F32, 8640 + i * 32) for i in range(2)]
    s_qtmp = [place(f"s_qtmp{i}", [128, SS], BF16, 9216 + i * 1024) for i in range(2)]
    s_qshf = [place(f"s_qshf{i}", [128, SS], BF16, 11264 + i * 1024) for i in range(2)]
    s_t1 = place("s_t1", [128, SS], BF16, 13312)
    s_t2 = place("s_t2", [128, SS], BF16, 14336)
    s_wk = place("s_wk", [128, NKC * 512], BF16, 16384)
    s_wv = place("s_wv", [128, NKC * 512], BF16, 32768)
    s_QT = place("s_QT", [128, NH * SS], BF16, 16384)
    s_xnT = place("s_xnT", [128, NKC * SS], BF16, 32768)
    s_ctxT = place("s_ctxT", [128, NH * SS], BF16, 16384)
    s_K = place("s_K", [128, NKV * S], BF16, 49152)
    s_V = place("s_V", [128, NT * NKV * 128], BF16, 81920)
    s_stage = [place(f"s_stage{i}", [128, D], F32, 49152 + i * 8192) for i in range(2)]
    s_ht = [place(f"s_ht{i}", [128, NHC * 128], BF16, 180736 + i * 10240) for i in range(2)]
    s_yst = place("s_yst", [128, D], F32, 201216)
    s_wq = place("s_wq", [128, NKC * D], BF16, 115200)
    s_wo = place("s_wo", [128, NKC * D], BF16, 115200)
    s_xnsq1 = place("s_xnsq1", [128, D], BF16, 115200)
    s_st1 = place("s_st1", [128, 8], F32, 119296)
    s_ktmp = [place(f"s_ktmp{i}", [128, SS], BF16, 119424 + i * 1024) for i in range(2)]
    s_kshf = [place(f"s_kshf{i}", [128, SS], BF16, 121472 + i * 1024) for i in range(2)]
    s_kt1 = place("s_kt1", [128, SS], BF16, 123520)
    s_kt2 = place("s_kt2", [128, SS], BF16, 124544)
    s_xnT1 = place("s_xnT1", [128, NKC * SS], BF16, 126976)
    s_SfcT = place("s_SfcT", [128, NKC * HF], BF16, 16384)
    s_SpT = place("s_SpT", [128, NHC * D], BF16, 16384)
    s_cosT = place("s_cosT", [128, S], BF16, 180736)
    s_sinsT = place("s_sinsT", [128, S], BF16, 188928)
    s_xn2T = place("s_xn2T", [128, NKC * SS], BF16, 180736)
    s_id = place("s_id", [128, 128], BF16, 197120)
    s_mcT = place("s_mcT", [128, 128], BF16, 197376)
    s_mfT = place("s_mfT", [128, 128], BF16, 197632)
    s_onesrb = place("s_onesrb", [1, 128], BF16, 197888)
    s_onesrf = place("s_onesrf", [1, 128], F32, 198144)
    s_onescf = place("s_onescf", [128, 1], F32, 198656)
    s_eps = place("s_eps", [128, 1], F32, 198688)
    sc0 = place("sc0", [1, 16], F32, 198720)
    scB = place("scB", [128, 8], F32, 198784)
    s_ctx1 = [place(f"s_ctx1{i}", [128, 512], BF16, 198912 + i * 1024) for i in range(2)]
    s_rb = [place(f"s_rb{i}", [128, 512], BF16, 201216 + i * 1024) for i in range(2)]
    s_rrow = place("s_rrow", [1, 512], BF16, 203264)
    s_onescb = place("s_onescb", [128, 1], BF16, 204288)
    s_hst = [place(f"s_hst{i}", [128, 2048], BF16, 198912 + i * 4096) for i in range(2)]
    s_xnsq3 = place("s_xnsq3", [128, D], BF16, 207104)
    s_st3 = place("s_st3", [128, 8], F32, 211232)

    pSC = [es.enter_context(nc.psum_tensor(f"pSC{i}", [128, 1024], F32)) for i in range(2)]
    pCX = [es.enter_context(nc.psum_tensor(f"pCX{i}", [128, 512], F32)) for i in range(2)]
    pLNB = es.enter_context(nc.psum_tensor("pLNB", [128, 1024], BF16))
    pAUX = es.enter_context(nc.psum_tensor("pAUX", [128, 512], F32))

    sems = {}
    ops = {e: [] for e in ENGS}

    def C(name):
        if name not in sems:
            sems[name] = Ctr(es.enter_context(nc.semaphore(name)))
        return sems[name]

    def E(eng, fn):
        ops[eng].append(fn)

    def W(eng, ctr, val):
        if val <= 0:
            return
        E(eng, lambda e, s=ctr.h, v=val: e.wait_ge(s, v))

    def EI(eng, ctr, fn, k=1):
        ctr.inc(k)
        E(eng, lambda e, f=fn, s=ctr.h, kk=k: f(e).then_inc(s, kk))

    class Res:

        def __init__(self):
            self.reqs = []

        def acquire(self, eng):
            for c, v in self.reqs:
                W(eng, c, v)
            self.reqs = []

        def release(self, ctr):
            self.reqs.append((ctr, ctr.n))

    rSC = [Res(), Res()]
    rCX = [Res(), Res()]
    rLNB = Res()
    rAUX = Res()

    FEN = C("FEN")
    fence_round = [0]

    def fence():
        fence_round[0] += 1
        tgt = 5 * fence_round[0]
        for eng in ENGS:
            E(eng, lambda e, s=FEN.h: e.nop().then_inc(s, 1))
        FEN.inc(5)
        for eng in ENGS:
            W(eng, FEN, tgt)

    _PH = int(os.environ.get("KPH", "9"))

    def _finish():
        with nc.Block() as block:
            @block.sync
            def _(e):
                for f in ops["sync"]:
                    f(e)

            @block.vector
            def _(e):
                for f in ops["vector"]:
                    f(e)

            @block.scalar
            def _(e):
                for f in ops["scalar"]:
                    f(e)

            @block.tensor
            def _(e):
                for f in ops["tensor"]:
                    f(e)

            @block.gpsimd
            def _(e):
                for f in ops["gpsimd"]:
                    f(e)
        es.close()
        return nc

    dW = C("dW")
    for dst, src in [(s_cosT, cosT_c), (s_sinsT, sinsT_c), (s_mcT, mcT_c),
                     (s_mfT, mfT_c), (s_id, ident_c), (s_onesrb, onesrb_c),
                     (s_onesrf, onesrf_c), (s_onescf, onescf_c), (s_eps, epsc_c),
                     (s_onescb, onescb_c)]:
        EI("sync", dW, lambda e, d=dst, s=src: e.dma_start(out=d[:], in_=s), 16)
    NCONSTW = dW.n

    dW0 = [C("dW0a"), C("dW0b")]
    vAB = C("vAB")
    tPS = C("tPS")
    gTN = C("gTN")
    dS = [C("dSa"), C("dSb")]
    aC = C("aC")

    mats = [("fc", wfc, sfc_hbm, HF // 128, 1, 2048, HF * D, 0, 6),
            ("pj", wpj, sp_hbm, D // 128, 4, 1280, HF * D, 2, 7)]

    def p0_gen():
     for (mn, wsrc, sdst, nrt, ncc, cw, nel, cthr, cabs) in mats:
          ntile = nrt * ncc
          base_vab = vAB.n
          ti = 0
          for rt in range(nrt):
              for cc in range(ncc):
                  j = ti % 2
                  lq = "sync" if j == 0 else "scalar"
                  if ti >= 2:
                      W(lq, vAB, base_vab + ti - 1)
                  else:
                      W(lq, gTN, gTN.n)
                  EI(lq, dW0[j], lambda e, j=j, rt=rt, cc=cc, w=wsrc, cw=cw:
                     e.dma_start(out=w0t[j][:, 0:cw], in_=w[rt * 128:(rt + 1) * 128,
                                                           cc * cw:(cc + 1) * cw]), 16)
                  W("vector", dW0[j], dW0[j].n)
                  EI("vector", vAB, lambda e, j=j, ti=ti, cw=cw:
                     e.tensor_reduce(out=acc0[:, ti:ti + 1], in_=w0t[j][:, 0:cw],
                                     axis=AX, op=ALU.add, apply_absolute_value=True))
                  ti += 1
                  yield
          W("vector", vAB, vAB.n)
          EI("vector", vAB, lambda e, nt=ntile:
             e.tensor_reduce(out=acc0[:, 159:160], in_=acc0[:, 0:nt], axis=AX,
                             op=ALU.add))
          W("tensor", vAB, vAB.n)
          W("tensor", dW, NCONSTW)
          rAUX.acquire("tensor")
          EI("tensor", tPS, lambda e:
             e.matmul(out=pAUX[0:1, 20:21], lhsT=acc0[:, 159:160], rhs=s_onescf[:],
                      start=True, stop=True))
          W("vector", tPS, tPS.n)

          steps = [
              lambda e: e.tensor_copy(out=sc0[0:1, 0:1], in_=pAUX[0:1, 20:21]),
              lambda e, nel=nel: e.tensor_scalar(out=sc0[0:1, 1:2], in0=sc0[0:1, 0:1],
                                                 scalar1=1.0 / nel, scalar2=None,
                                                 op0=ALU.mult),
              lambda e: e.tensor_scalar(out=sc0[0:1, 2:3], in0=sc0[0:1, 1:2],
                                        scalar1=1e-5, scalar2=None, op0=ALU.max),
              lambda e: e.tensor_scalar(out=sc0[0:1, 3:4], in0=sc0[0:1, 2:3],
                                        scalar1=0.7, scalar2=None, op0=ALU.mult),
              lambda e: e.tensor_scalar(out=sc0[0:1, 4:5], in0=sc0[0:1, 3:4],
                                        scalar1=-1.0, scalar2=None, op0=ALU.mult),
              lambda e, cabs=cabs: e.tensor_copy(out=sc0[0:1, cabs:cabs + 1],
                                                 in_=sc0[0:1, 2:3]),
          ]
          for stp in steps:
              EI("vector", vAB, stp)
              W("vector", vAB, vAB.n)
          W("tensor", vAB, vAB.n)
          rAUX.acquire("tensor")
          EI("tensor", tPS, lambda e:
             e.matmul(out=pAUX[:, 21:22], lhsT=s_onesrf[:], rhs=sc0[0:1, 3:4],
                      start=True, stop=True))
          EI("tensor", tPS, lambda e:
             e.matmul(out=pAUX[:, 22:23], lhsT=s_onesrf[:], rhs=sc0[0:1, 4:5],
                      start=True, stop=True))
          W("vector", tPS, tPS.n)

          def evac_thr(e, cthr=cthr):
              e.tensor_copy(out=scB[:, cthr:cthr + 1], in_=pAUX[:, 21:22])
              return e.tensor_copy(out=scB[:, cthr + 1:cthr + 2], in_=pAUX[:, 22:23])
          EI("vector", vAB, evac_thr)
          rAUX.release(vAB)
          base_gtn = gTN.n
          base_ds = [dS[0].n, dS[1].n]
          vab_thr = vAB.n
          ti = 0
          for rt in range(nrt):
              for cc in range(ncc):
                  j = ti % 2
                  lq = "sync" if j == 0 else "scalar"
                  if ti >= 2:
                      W(lq, gTN, base_gtn + 3 * (ti - 1))
                  else:
                      W(lq, vAB, vab_thr)
                  EI(lq, dW0[j], lambda e, j=j, rt=rt, cc=cc, w=wsrc, cw=cw:
                     e.dma_start(out=w0t[j][:, 0:cw], in_=w[rt * 128:(rt + 1) * 128,
                                                           cc * cw:(cc + 1) * cw]), 16)
                  W("vector", dW0[j], dW0[j].n)
                  if ti >= 2:
                      W("vector", dS[j], base_ds[j] + 16 * (ti // 2))

                  EI("vector", gTN, lambda e, j=j, c=cthr, cw=cw:
                     e.tensor_scalar(out=s0t[j][:, 0:cw], in0=w0t[j][:, 0:cw],
                                     scalar1=scB[:, c:c + 1], scalar2=None,
                                     op0=ALU.is_gt))
                  EI("vector", gTN, lambda e, j=j, c=cthr, cw=cw:
                     e.tensor_scalar(out=w0t[j][:, 0:cw], in0=w0t[j][:, 0:cw],
                                     scalar1=scB[:, c + 1:c + 2], scalar2=None,
                                     op0=ALU.is_lt))
                  W("vector", gTN, gTN.n)
                  EI("vector", gTN, lambda e, j=j, cw=cw:
                     e.tensor_sub(out=s0t[j][:, 0:cw], in0=s0t[j][:, 0:cw],
                                  in1=w0t[j][:, 0:cw]))
                  W("gpsimd", gTN, gTN.n)
                  EI("gpsimd", dS[j], lambda e, j=j, rt=rt, cc=cc, sd=sdst, cw=cw:
                     e.dma_start(out=sd[rt * 128:(rt + 1) * 128,
                                        cc * cw:(cc + 1) * cw], in_=s0t[j][:, 0:cw]), 16)
                  ti += 1
                  yield

     EI("vector", vAB, lambda e: e.tensor_mul(out=sc0[0:1, 8:9], in0=sc0[0:1, 6:7],
                                              in1=sc0[0:1, 6:7]))
     W("vector", vAB, vAB.n)
     EI("vector", vAB, lambda e: e.tensor_mul(out=sc0[0:1, 8:9], in0=sc0[0:1, 8:9],
                                              in1=sc0[0:1, 7:8]))
     W("scalar", vAB, vAB.n)

     EI("scalar", aC, lambda e: e.activation(out=sc0[0:1, 9:10], in_=sc0[0:1, 8:9],
                                             func=AF.Ln))
     W("scalar", aC, aC.n)
     EI("scalar", aC, lambda e: e.activation(out=sc0[0:1, 10:11],
                                             in_=sc0[0:1, 9:10], func=AF.Exp,
                                             scale=0.5))
     W("tensor", aC, aC.n)
     rAUX.acquire("tensor")
     EI("tensor", tPS, lambda e:
        e.matmul(out=pAUX[:, 23:24], lhsT=s_onesrf[:], rhs=sc0[0:1, 10:11],
                 start=True, stop=True))
     W("vector", tPS, tPS.n)
     EI("vector", vAB, lambda e: e.tensor_copy(out=scB[:, 4:5], in_=pAUX[:, 23:24]))
     rAUX.release(vAB)

     if _dbg:
         o_scB = nc.dram_tensor("o_scB", [128, 8], F32, kind="ExternalOutput").ap()
         o_sc0 = nc.dram_tensor("o_sc0", [1, 16], F32, kind="ExternalOutput").ap()
         dDBG = C("dDBG")
         W("sync", vAB, vAB.n)
         EI("sync", dDBG, lambda e: e.dma_start(out=o_scB, in_=scB[:]), 16)
         EI("sync", dDBG, lambda e: e.dma_start(out=o_sc0, in_=sc0[:]), 16)
         W("sync", dDBG, 32)

     yield

    _p0 = p0_gen()

    def p0_pump(n):
        for _ in range(n):
            if next(_p0, "done") == "done":
                break

    if _PH <= 1:
        for _ in _p0:
            pass
        return _finish()

    dX = [C("dXa"), C("dXb")]
    aSQ = C("aSQ")
    vRED = C("vRED")
    vST = C("vST")
    aRS = C("aRS")
    vXN = C("vXN")
    tTR = C("tTR")
    vTE = C("vTE")

    def emit_ln_tile(src_hbm, tt, xnsq, stats, xnT_dst, dve_prewait=None):
        j = tt % 2
        if vXN.n >= 2:
            W("sync", vXN, vXN.n - 1)
        EI("sync", dX[j], lambda e, j=j, tt=tt, src=src_hbm:
           e.dma_start(out=s_xf[j][:], in_=src[tt * 128:(tt + 1) * 128, :]), 16)
        ldx = dX[j].n
        W("scalar", dX[j], ldx)
        W("scalar", tTR, tTR.n)
        EI("scalar", aSQ, lambda e, j=j, xnsq=xnsq, stats=stats:
           e.activation(out=xnsq[:], in_=s_xf[j][:], func=AF.Square,
                        accum_out=stats[:, 1:2]))
        W("vector", dX[j], ldx)
        EI("vector", vRED, lambda e, j=j, stats=stats:
           e.tensor_reduce(out=stats[:, 0:1], in_=s_xf[j][:], axis=AX, op=ALU.add))
        W("vector", vRED, vRED.n)
        W("vector", aSQ, aSQ.n)

        EI("vector", vST, lambda e, stats=stats:
           e.tensor_scalar(out=stats[:, 2:3], in0=stats[:, 0:1],
                           scalar1=-1.0 / D, scalar2=None, op0=ALU.mult))
        EI("vector", vST, lambda e, stats=stats:
           e.tensor_scalar(out=stats[:, 3:4], in0=stats[:, 1:2],
                           scalar1=1.0 / D, scalar2=None, op0=ALU.mult))
        W("vector", vST, vST.n)
        EI("vector", vST, lambda e, stats=stats:
           e.scalar_tensor_tensor(out=stats[:, 4:5], in0=stats[:, 2:3],
                                  scalar=stats[:, 2:3], in1=stats[:, 3:4],
                                  op0=ALU.mult, op1=ALU.subtract))
        W("scalar", vST, vST.n)

        def rstd_chain(e, stats=stats):
            e.activation(out=stats[:, 5:6], in_=stats[:, 4:5], func=AF.Ln,
                         bias=s_eps[:], scale=-1.0)
            return e.activation(out=stats[:, 6:7], in_=stats[:, 5:6], func=AF.Exp,
                                scale=-0.5)
        EI("scalar", aRS, rstd_chain)
        W("vector", aRS, aRS.n)
        if dve_prewait is not None:
            dve_prewait()
        EI("vector", vXN, lambda e, j=j, xnsq=xnsq, stats=stats:
           e.tensor_scalar(out=xnsq[:], in0=s_xf[j][:], scalar1=stats[:, 2:3],
                           scalar2=stats[:, 6:7], op0=ALU.add, op1=ALU.mult))
        r = tt % 4
        W("tensor", vXN, vXN.n)
        for f in range(2):
            rLNB.acquire("tensor")
            for kk in range(8):
                k = 8 * f + kk
                fn = lambda e, k=k, kk=kk, xnsq=xnsq: e.transpose(
                    out=pLNB[:, kk * 128:(kk + 1) * 128],
                    in_=xnsq[:, k * 128:(k + 1) * 128], identity=s_id[:])
                if kk == 7:
                    EI("tensor", tTR, fn)
                else:
                    E("tensor", fn)
            W("vector", tTR, tTR.n)

            def evac(e, f=f, r=r, dst=xnT_dst):
                o = dst[:, :].rearrange("p (k q) -> p k q", k=NKC, q=SS)
                o = o[:, 8 * f:8 * f + 8, r * 128:(r + 1) * 128]
                return e.tensor_copy(
                    out=o, in_=pLNB[:, 0:1024].rearrange("p (k q) -> p k q",
                                                         k=8, q=128))
            EI("vector", vTE, evac)
            rLNB.release(vTE)

    dWG = C("dWG")
    W("gpsimd", dS[0], dS[0].n)
    W("gpsimd", dS[1], dS[1].n)
    for k in range(NKC):
        EI("gpsimd", dWG, lambda e, k=k:
           e.dma_start(out=s_wk[:, k * 512:(k + 1) * 512],
                       in_=wk[k * 128:(k + 1) * 128, :]), 16)
    for k in range(NKC):
        EI("gpsimd", dWG, lambda e, k=k:
           e.dma_start(out=s_wv[:, k * 512:(k + 1) * 512],
                       in_=wv[k * 128:(k + 1) * 128, :]), 16)
    NWKV = dWG.n

    dXN_ST = C("dXN_ST")
    tKV = C("tKV")
    aKE = C("aKE")
    dKS = C("dKS")
    vKR = C("vKR")
    aVE = C("aVE")

    for s in range(NSS):
        for r in range(4):
            tt = 4 * s + r

            def dve_pre():
                if s >= 1:
                    W("vector", dXN_ST, 16 * s)
                    W("vector", tKV, 8 * s)
            emit_ln_tile(xin, tt, s_xnsq1, s_st1, s_xnT1,
                         dve_prewait=dve_pre if r == 0 else None)
            p0_pump(4)
        W("sync", vTE, vTE.n)
        EI("sync", dXN_ST, lambda e, s=s:
           e.dma_start(out=xnT_hbm[s], in_=s_xnT1[:]), 16)
        W("tensor", vTE, vTE.n)
        W("tensor", dWG, NWKV)
        for kvh in range(NKV):
            p = kvh % 2
            rSC[p].acquire("tensor")
            for k in range(NKC):
                fn = lambda e, k=k, kvh=kvh, p=p: e.matmul(
                    out=pSC[p][:, 0:512],
                    lhsT=s_wk[:, k * 512 + kvh * 128:k * 512 + (kvh + 1) * 128],
                    rhs=s_xnT1[:, k * SS:(k + 1) * SS],
                    start=(k == 0), stop=(k == NKC - 1))
                if k == NKC - 1:
                    EI("tensor", tKV, fn)
                else:
                    E("tensor", fn)
            j = kvh % 2
            W("scalar", tKV, tKV.n)
            if vKR.n >= 2:
                W("scalar", vKR, vKR.n - 1)
            EI("scalar", aKE, lambda e, p=p, j=j:
               e.copy(out=s_ktmp[j][:], in_=pSC[p][:, 0:512]))
            rSC[p].release(aKE)
            W("sync", aKE, aKE.n)
            if vKR.n >= 2:
                W("sync", vKR, vKR.n - 1)
            E("sync", lambda e, j=j, sh=dKS.h: (
                e.dma_start(out=s_kshf[j][0:64, :],
                            in_=s_ktmp[j][64:128, :]).then_inc(sh, 16),
                e.dma_start(out=s_kshf[j][64:128, :],
                            in_=s_ktmp[j][0:64, :]).then_inc(sh, 16)))
            dKS.inc(32)
            W("vector", aKE, aKE.n)
            W("vector", dKS, dKS.n)

            def rope_k(e, j=j, s=s, kvh=kvh):
                e.tensor_mul(out=s_kt1[:], in0=s_ktmp[j][:],
                             in1=s_cosT[:, s * SS:(s + 1) * SS])
                e.tensor_mul(out=s_kt2[:], in0=s_kshf[j][:],
                             in1=s_sinsT[:, s * SS:(s + 1) * SS])
                return e.tensor_add(
                    out=s_K[:, kvh * S + s * SS:kvh * S + (s + 1) * SS],
                    in0=s_kt1[:], in1=s_kt2[:])
            EI("vector", vKR, rope_k)
        for tb in range(4):
            p = tb % 2
            cpos = 4 * s + tb
            rSC[p].acquire("tensor")
            for k in range(NKC):
                fn = lambda e, k=k, tb=tb, p=p: e.matmul(
                    out=pSC[p][:, 0:512],
                    lhsT=s_xnT1[:, k * SS + tb * 128:k * SS + (tb + 1) * 128],
                    rhs=s_wv[:, k * 512:(k + 1) * 512],
                    start=(k == 0), stop=(k == NKC - 1))
                if k == NKC - 1:
                    EI("tensor", tKV, fn)
                else:
                    E("tensor", fn)
            W("scalar", tKV, tKV.n)

            def evac_v(e, p=p, cpos=cpos):
                return e.copy(out=s_V[:, cpos * 512:(cpos + 1) * 512],
                              in_=pSC[p][:, 0:512])
            EI("scalar", aVE, evac_v)
            rSC[p].release(aVE)
        p0_pump(10)

    for _ in _p0:
        pass

    fence()
    if _PH <= 2:
        return _finish()

    W("gpsimd", dS[0], dS[0].n)
    W("gpsimd", dS[1], dS[1].n)
    for k in range(NKC):
        EI("gpsimd", dWG, lambda e, k=k:
           e.dma_start(out=s_wq[:, k * D:(k + 1) * D],
                       in_=wq[k * 128:(k + 1) * 128, :]), 16)
    NWQ = dWG.n
    vMS = C("vMS")
    EI("vector", vMS, lambda e: (e.memset(s_mbias[0][:], 0.0),
                                 e.memset(s_mbias[1][:], 0.0),
                                 e.memset(s_Pc[0][:], 0.0),
                                 e.memset(s_Pc[1][:], 0.0),
                                 e.memset(s_Pc[2][:], 0.0))[-1])

    dXN_LD = C("dXN_LD")
    tQP = C("tQP")
    aQE = C("aQE")
    dQS = C("dQS")
    vQR = C("vQR")
    tSC = [C("tSC0"), C("tSC1")]
    vMX = [C("vMX0"), C("vMX1")]
    aEX = [C("aEX0"), C("aEX1")]
    gMK = [C("gMK0"), C("gMK1")]
    tPV = [C("tPV0"), C("tPV1")]
    aRC = [C("aRC0"), C("aRC1")]
    vCX = [C("vCX0"), C("vCX1")]
    dCTX = C("dCTX")
    dMB = C("dMB")
    vMB = C("vMB")
    tMB = C("tMB")
    aMB = C("aMB")
    vMBE = C("vMBE")

    tSC2 = C("tSC2")
    aEX2 = C("aEX2")
    gMK2 = C("gMK2")
    tPV2 = [C("tPV2a"), C("tPV2b")]
    aRC2 = [C("aRC2a"), C("aRC2b")]
    tRB2 = [C("tRB2a"), C("tRB2b")]
    vRB2 = [C("vRB2a"), C("vRB2b")]
    vCX2 = [C("vCX2a"), C("vCX2b")]
    vMX2 = C("vMX2")
    dCTX2 = C("dCTX2")
    dMB2 = C("dMB2")
    vMB2 = C("vMB2")
    aMB2 = C("aMB2")
    tMB2 = C("tMB2")
    vMBE2 = C("vMBE2")
    rSCB = [Res(), Res()]
    rCXB = [Res(), Res()]
    rSMB = [Res(), Res()]
    sp_guard = {0: [], 1: [], 2: []}
    scb = [lambda a, b: pSC[0][:, a:b], lambda a, b: pSC[0][:, 512 + a:512 + b]]
    ctxb = [lambda a, b: pSC[1][:, a:b], lambda a, b: pSC[1][:, 512 + a:512 + b]]
    sumb = [lambda a, b: pCX[0][0:1, a:b], lambda a, b: pCX[1][0:1, a:b]]
    rbb = [lambda a, b: pCX[0][:, a:b], lambda a, b: pCX[1][:, a:b]]

    def emit_group(s, h):
        g = 16 * s + h
        pg = g % 2
        kv = h // GQ
        c0 = max(0, 4 * s - 8)
        chunks = list(range(c0, 4 * s + 4))
        geo = []
        for c in chunks:
            rel = c - 4 * s
            q0 = max(0, rel * 128)
            q1 = min(512, rel * 128 + 1152)
            tri = None
            if 0 <= rel <= 3:
                tri = ("c", rel * 128)
            elif -8 <= rel <= -5:
                tri = ("f", rel * 128 + 1024)
            geo.append((c, q0, q1, tri))
        geo.sort(key=lambda x: 0 if x[0] == 4 * s else 1)
        nch = len(geo)
        exp_marks = []

        def emit_chunk_scores(ci):
            c, q0, q1, tri = geo[ci]
            bk = ci % 2
            rSCB[bk].acquire("tensor")
            if ci == 0:
                W("tensor", vQR, 16 * s + h + 1)
            EI("tensor", tSC2, lambda e, c=c, q0=q0, q1=q1, bk=bk:
               e.matmul(out=scb[bk](q0, q1),
                        lhsT=s_K[:, kv * S + c * 128:kv * S + (c + 1) * 128],
                        rhs=s_QT[:, h * SS + q0:h * SS + q1],
                        start=True, stop=True))
            W("scalar", tSC2, tSC2.n)
            if s >= 1:
                W("scalar", vMBE2, 2 * s)
            else:
                W("scalar", vMS, 1)
            sp = ci % 3
            for cc, vv in sp_guard[sp]:
                W("scalar", cc, vv)
            sp_guard[sp] = []
            EI("scalar", aEX2, lambda e, sp=sp, q0=q0, q1=q1, bk=bk:
               e.activation(out=s_Pc[sp][:, q0:q1], in_=scb[bk](q0, q1),
                            func=AF.Exp, bias=s_mbias[s % 2][:, h:h + 1],
                            scale=ISQ))
            rSCB[bk].release(aEX2)
            if tri is not None:
                W("gpsimd", aEX2, aEX2.n)
                kind, t0c = tri
                mk = s_mcT if kind == "c" else s_mfT
                EI("gpsimd", gMK2, lambda e, sp=sp, t0c=t0c, mk=mk:
                   e.tensor_mul(out=s_Pc[sp][:, t0c:t0c + 128],
                                in0=s_Pc[sp][:, t0c:t0c + 128], in1=mk[:]))
                exp_marks.append((ci, (gMK2, gMK2.n)))
            else:
                exp_marks.append((ci, (aEX2, aEX2.n)))

        def emit_chunk_pv(ci):
            c, q0, q1, tri = geo[ci]
            sp = ci % 3
            cc, vv = exp_marks[ci][1]
            W("tensor", cc, vv)
            if ci == 0:
                rCXB[pg].acquire("tensor")
                rSMB[pg].acquire("tensor")
            vb = (c * NKV + kv) * 128
            E("tensor", lambda e, sp=sp, q0=q0, q1=q1, vb=vb, ci=ci:
              e.matmul(out=ctxb[pg](q0, q1),
                       lhsT=s_V[:, vb:vb + 128],
                       rhs=s_Pc[sp][:, q0:q1],
                       start=(ci == 0), stop=(ci == nch - 1)))
            fn = lambda e, sp=sp, q0=q0, q1=q1, ci=ci: e.matmul(
                out=sumb[pg](q0, q1),
                lhsT=s_onescb[:],
                rhs=s_Pc[sp][:, q0:q1],
                start=(ci == 0), stop=(ci == nch - 1))
            EI("tensor", tPV2[pg], fn)
            sp_guard[sp].append((tPV2[pg], tPV2[pg].n))

        emit_chunk_scores(0)
        for ci in range(1, nch):
            emit_chunk_scores(ci)
            emit_chunk_pv(ci - 1)
        emit_chunk_pv(nch - 1)
        W("vector", exp_marks[-1][1][0], exp_marks[-1][1][1])
        if s >= 2:
            W("vector", dMB2, 16 * (s - 1))
        EI("vector", vMX2, lambda e, sp=(nch - 1) % 3:
           e.reduce_max(out=s_mcols[s % 2][:, h:h + 1], in_=s_Pc[sp][:, 0:512],
                        axis=AX))
        sp_guard[(nch - 1) % 3].append((vMX2, vMX2.n))
        W("scalar", tPV2[pg], tPV2[pg].n)
        W("scalar", tRB2[0], tRB2[0].n)
        W("scalar", tRB2[1], tRB2[1].n)
        EI("scalar", aRC2[pg], lambda e:
           e.activation(out=s_rrow[:], in_=sumb[pg](0, 512), func=AF.Ln))
        W("scalar", aRC2[pg], aRC2[pg].n)
        EI("scalar", aRC2[pg], lambda e:
           e.activation(out=s_rrow[:], in_=s_rrow[:], func=AF.Exp, scale=-1.0))
        W("tensor", aRC2[pg], aRC2[pg].n)
        EI("tensor", tRB2[pg], lambda e: e.matmul(
            out=rbb[pg](0, 512), lhsT=s_onesrb[:], rhs=s_rrow[0:1, 0:512],
            start=True, stop=True))
        W("vector", tRB2[pg], tRB2[pg].n)
        if g >= 2:
            W("vector", dCTX2, 16 * (g - 1))
        EI("vector", vRB2[pg], lambda e:
           e.tensor_copy(out=s_rb[pg][:], in_=rbb[pg](0, 512)))
        rSMB[pg].release(vRB2[pg])
        EI("vector", vCX2[pg], lambda e:
           e.tensor_mul(out=s_ctx1[g % 2][:], in0=ctxb[pg](0, 512),
                        in1=s_rb[pg][:]))
        rCXB[pg].release(vCX2[pg])
        W("sync", vCX2[pg], vCX2[pg].n)
        EI("sync", dCTX2, lambda e, s=s, h=h, g=g: e.dma_start(
            out=ctxT_hbm[s][:, h * SS:(h + 1) * SS], in_=s_ctx1[g % 2][:]), 16)

    def emit_mbias2(s):
        W("sync", vMX2, vMX2.n)
        EI("sync", dMB2, lambda e, s=s: e.dma_start(
            out=s_mrow[0:1, :].rearrange("o (p c) -> o p c", p=128, c=16),
            in_=s_mcols[s % 2][:, 0:16]), 16)
        W("vector", dMB2, dMB2.n)
        EI("vector", vMB2, lambda e: e.tensor_reduce(
            out=s_negrow[:], in_=s_mrow[0:1, :].rearrange(
                "o (p c) -> o c p", p=128, c=16), axis=AX, op=ALU.max))
        W("scalar", vMB2, vMB2.n)
        EI("scalar", aMB2, lambda e:
           e.activation(out=s_negrow[:], in_=s_negrow[:], func=AF.Ln))
        W("tensor", aMB2, aMB2.n)
        rAUX.acquire("tensor")
        EI("tensor", tMB2, lambda e: e.matmul(
            out=pAUX[:, 0:16], lhsT=s_onesrb[:], rhs=s_negrow[0:1, 0:16],
            start=True, stop=True))
        W("vector", tMB2, tMB2.n)
        W("vector", aEX2, aEX2.n)

        EI("vector", vMBE2, lambda e, s=s:
           e.tensor_copy(out=s_mbias[(s + 1) % 2][:], in_=s_mbias[s % 2][:]))
        W("vector", vMBE2, vMBE2.n)
        EI("vector", vMBE2, lambda e, s=s:
           e.tensor_sub(out=s_mbias[(s + 1) % 2][:],
                        in0=s_mbias[(s + 1) % 2][:], in1=pAUX[:, 0:16]))
        rAUX.release(vMBE2)

    for s in range(NSS):
        if s >= 1:
            W("sync", tQP, 16 * s)
        W("sync", dXN_ST, 16 * (s + 1))
        EI("sync", dXN_LD, lambda e, s=s:
           e.dma_start(out=s_xnT[:], in_=xnT_hbm[s]), 16)
        W("tensor", dXN_LD, dXN_LD.n)
        W("tensor", dWG, NWQ)
        for h in range(NH):
            p = h % 2
            (rSCB[0] if p == 0 else rCXB[0]).acquire("tensor")
            if s > 0 or h >= 2:
                W("tensor", aQE, aQE.n - 1)
            for k in range(NKC):
                fn = lambda e, k=k, h=h, p=p: e.matmul(
                    out=pSC[p][:, 0:512],
                    lhsT=s_wq[:, k * D + h * 128:k * D + (h + 1) * 128],
                    rhs=s_xnT[:, k * SS:(k + 1) * SS],
                    start=(k == 0), stop=(k == NKC - 1))
                if k == NKC - 1:
                    EI("tensor", tQP, fn)
                else:
                    E("tensor", fn)
            j = h % 2
            W("scalar", tQP, tQP.n)
            if vQR.n >= 2:
                W("scalar", vQR, vQR.n - 1)
            EI("scalar", aQE, lambda e, p=p, j=j:
               e.copy(out=s_qtmp[j][:], in_=pSC[p][:, 0:512]))
            (rSCB[0] if p == 0 else rCXB[0]).release(aQE)
            W("sync", aQE, aQE.n)
            if vQR.n >= 2:
                W("sync", vQR, vQR.n - 1)
            E("sync", lambda e, j=j, sh=dQS.h: (
                e.dma_start(out=s_qshf[j][0:64, :],
                            in_=s_qtmp[j][64:128, :]).then_inc(sh, 16),
                e.dma_start(out=s_qshf[j][64:128, :],
                            in_=s_qtmp[j][0:64, :]).then_inc(sh, 16)))
            dQS.inc(32)
            W("vector", aQE, aQE.n)
            W("vector", dQS, dQS.n)
            if h == 0 and s > 0:
                W("vector", tSC2, tSC2.n)

            def rope_q(e, j=j, s=s, h=h):
                e.tensor_mul(out=s_t1[:], in0=s_qtmp[j][:],
                             in1=s_cosT[:, s * SS:(s + 1) * SS])
                e.tensor_mul(out=s_t2[:], in0=s_qshf[j][:],
                             in1=s_sinsT[:, s * SS:(s + 1) * SS])
                return e.tensor_add(out=s_QT[:, h * SS:(h + 1) * SS],
                                    in0=s_t1[:], in1=s_t2[:])
            EI("vector", vQR, rope_q)
        W("tensor", aQE, aQE.n)
        for bk in range(2):
            rSCB[bk].acquire("tensor")
        for h in range(NH):
            emit_group(s, h)
        emit_mbias2(s)

    fence()
    if _PH <= 3:
        return _finish()

    for k in range(NKC):
        EI("gpsimd", dWG, lambda e, k=k:
           e.dma_start(out=s_wo[:, k * D:(k + 1) * D],
                       in_=wo[k * 128:(k + 1) * 128, :]), 16)
    NWO = dWG.n
    dCT_LD = C("dCT_LD")
    dX2 = C("dX2")
    dX1_ST = C("dX1_ST")
    tOU = [C("tOU0"), C("tOU1")]
    vOU = [C("vOU0"), C("vOU1")]
    pc2 = [0]
    base_tou = tOU[0].n
    for s in range(NSS):
        if s >= 1:
            W("sync", tOU[0], base_tou + 4 * s)
        EI("sync", dCT_LD, lambda e, s=s: e.dma_start(
            out=s_ctxT[:], in_=ctxT_hbm[s]), 16)
        for tb in range(4):
            t = 4 * s + tb
            if t >= 2:
                W("sync", vOU[0], vOU[0].n)
            EI("sync", dX2, lambda e, t=t:
               e.dma_start(out=s_xf[t % 2][:], in_=xin[t * 128:(t + 1) * 128, :]), 16)
            W("tensor", dCT_LD, 16 * (s + 1))
            W("tensor", dWG, NWO)
            W("tensor", vOU[0], vOU[0].n)
            pb = [(pSC[0], 0), (pSC[0], 512), (pSC[1], 0), (pSC[1], 512)]
            for h in range(NH):
                for n in range(4):
                    bt, bo = pb[n]
                    fn = lambda e, h=h, tb=tb, n=n, bt=bt, bo=bo: e.matmul(
                        out=bt[:, bo:bo + 512],
                        lhsT=s_ctxT[:, h * SS + tb * 128:h * SS + (tb + 1) * 128],
                        rhs=s_wo[:, h * D + n * 512:h * D + (n + 1) * 512],
                        start=(h == 0), stop=(h == NH - 1))
                    if h == NH - 1 and n == 3:
                        EI("tensor", tOU[0], fn)
                    else:
                        E("tensor", fn)
            W("vector", tOU[0], tOU[0].n)
            W("vector", dX2, 16 * (t + 1))
            if t >= 2:
                W("vector", dX1_ST, 16 * (t - 1))

            def evac2b(e, t=t, pb=pb):
                for n in range(4):
                    bt, bo = pb[n]
                    i = e.tensor_add(out=s_stage[t % 2][:, n * 512:(n + 1) * 512],
                                     in0=bt[:, bo:bo + 512],
                                     in1=s_xf[t % 2][:, n * 512:(n + 1) * 512])
                return i
            EI("vector", vOU[0], evac2b)
            W("sync", vOU[0], vOU[0].n)
            EI("sync", dX1_ST, lambda e, t=t:
               e.dma_start(out=x1_hbm[t * 128:(t + 1) * 128, :],
                           in_=s_stage[t % 2][:]), 16)

    fence()
    if _PH <= 4:
        return _finish()

    dSF = C("dSF")
    for k in range(NKC):
        EI("sync", dSF, lambda e, k=k: e.dma_start(
            out=s_SfcT[:, k * HF:(k + 1) * HF],
            in_=sfc_hbm[:, k * 128:(k + 1) * 128], transpose=True), 16)
    tH = [C("tH0"), C("tH1")]
    aH = [C("aH0"), C("aH1")]
    dH_ST = C("dH_ST")
    for s in range(NSS):
        for r in range(4):
            tt = 4 * s + r

            def dve_pre3():
                if s >= 1:
                    W("vector", tH[0], (NHC * s) // 2)
                    W("vector", tH[1], (NHC * s) // 2)
            emit_ln_tile(x1_hbm, tt, s_xnsq3, s_st3, s_xn2T,
                         dve_prewait=dve_pre3 if r == 0 else None)
        W("tensor", vTE, vTE.n)
        W("tensor", dSF, 16 * NKC)
        for hh in range(NHC):
            p = hh % 2
            rSC[p].acquire("tensor")
            for k in range(NKC):
                fn = lambda e, k=k, hh=hh, p=p: e.matmul(
                    out=pSC[p][:, 0:512],
                    lhsT=s_SfcT[:, k * HF + hh * 128:k * HF + (hh + 1) * 128],
                    rhs=s_xn2T[:, k * SS:(k + 1) * SS],
                    start=(k == 0), stop=(k == NKC - 1))
                if k == NKC - 1:
                    EI("tensor", tH[p], fn)
                else:
                    E("tensor", fn)
            jb = (hh // 4) % 2
            W("scalar", tH[p], tH[p].n)
            bidx = (NHC // 4) * s + hh // 4
            if hh % 4 == 0 and bidx >= 2:
                W("scalar", dH_ST, 16 * (bidx - 1))

            def relu_sq(e, p=p, jb=jb, hh=hh):
                e.activation(out=pAUX[:, 0:512], in_=pSC[p][:, 0:512],
                             func=AF.Relu, scale=scB[:, 4:5])
                return e.activation(
                    out=s_hst[jb][:, (hh % 4) * 512:(hh % 4 + 1) * 512],
                    in_=pAUX[:, 0:512], func=AF.Square)
            EI("scalar", aH[p], relu_sq)
            rSC[p].release(aH[p])
            if hh % 4 == 3:
                W("sync", aH[0], aH[0].n)
                W("sync", aH[1], aH[1].n)
                EI("sync", dH_ST, lambda e, s=s, hh=hh, jb=jb: e.dma_start(
                    out=h_hbm[s][:, (hh - 3) * 512:(hh + 1) * 512],
                    in_=s_hst[jb][:]), 16)

    fence()
    if _PH <= 5:
        return _finish()

    dSP = C("dSP")
    for jc in range(NHC):
        EI("sync", dSP, lambda e, jc=jc: e.dma_start(
            out=s_SpT[:, jc * D:(jc + 1) * D],
            in_=sp_hbm[:, jc * 128:(jc + 1) * 128], transpose=True), 16)
    dH_LD = C("dH_LD")
    dX1_LD = C("dX1_LD")
    dY = C("dY")
    pc3 = [0]
    for t in range(NT):
        s, tb = t // 4, t % 4
        if t >= 2:
            W("sync", vOU[0], vOU[0].n)

        def ht_in(e, s=s, tb=tb, j=t % 2):
            src = h_hbm[s].rearrange("p (c q) -> p c q", c=NHC, q=SS)
            src = src[:, :, tb * 128:(tb + 1) * 128]
            return e.dma_start(out=s_ht[j][:, :].rearrange(
                "p (c q) -> p c q", c=NHC, q=128), in_=src)
        EI("sync", dH_LD, ht_in, 16)
        EI("sync", dX1_LD, lambda e, t=t: e.dma_start(
            out=s_xf[t % 2][:], in_=x1_hbm[t * 128:(t + 1) * 128, :]), 16)
        W("tensor", dSP, 16 * NHC)
        W("tensor", dH_LD, 16 * (t + 1))
        W("tensor", vOU[0], vOU[0].n)
        pb = [(pSC[0], 0), (pSC[0], 512), (pSC[1], 0), (pSC[1], 512)]
        for jc in range(NHC):
            for n in range(4):
                bt, bo = pb[n]
                fn = lambda e, jc=jc, n=n, bt=bt, bo=bo, jj=t % 2: e.matmul(
                    out=bt[:, bo:bo + 512],
                    lhsT=s_ht[jj][:, jc * 128:(jc + 1) * 128],
                    rhs=s_SpT[:, jc * D + n * 512:jc * D + (n + 1) * 512],
                    start=(jc == 0), stop=(jc == NHC - 1))
                if jc == NHC - 1 and n == 3:
                    EI("tensor", tOU[0], fn)
                else:
                    E("tensor", fn)
        W("vector", tOU[0], tOU[0].n)
        W("vector", dX1_LD, 16 * (t + 1))
        if t >= 1:
            W("vector", dY, 16 * t)

        def evac3b(e, t=t, pb=pb):
            for n in range(4):
                bt, bo = pb[n]
                i = e.tensor_add(out=s_yst[:, n * 512:(n + 1) * 512],
                                 in0=bt[:, bo:bo + 512],
                                 in1=s_xf[t % 2][:, n * 512:(n + 1) * 512])
            return i
        EI("vector", vOU[0], evac3b)
        W("sync", vOU[0], vOU[0].n)
        EI("sync", dY, lambda e, t=t: e.dma_start(
            out=y[t * 128:(t + 1) * 128, :], in_=s_yst[:]), 16)
    W("sync", dY, 16 * NT)
    return _finish()


def _consts():
    inv = 1.0 / (10000.0 ** (np.arange(0, HD, 2, dtype=np.float32) / HD))
    t = np.arange(S, dtype=np.float32)
    f = np.concatenate([inv, inv])
    ang = f[:, None] * t[None, :]
    cosT = np.cos(ang).astype(bfnp)
    sins = np.sin(ang).astype(np.float32)
    sins[:HD // 2] *= -1.0
    sinsT = sins.astype(bfnp)
    jj = np.arange(128)
    mcT = (jj[:, None] <= jj[None, :]).astype(bfnp)
    mfT = (jj[:, None] >= jj[None, :]).astype(bfnp)
    return dict(
        cosT_c=cosT, sinsT_c=sinsT, mcT_c=mcT, mfT_c=mfT,
        ident_c=np.eye(128, dtype=bfnp),
        onescb_c=np.ones((128, 1), dtype=bfnp),
        onesrb_c=np.ones((1, 128), dtype=bfnp),
        onesrf_c=np.ones((1, 128), dtype=np.float32),
        onescf_c=np.ones((128, 1), dtype=np.float32),
        epsc_c=np.full((128, 1), 1e-5, dtype=np.float32),
    )


def kernel(L0_attn_hidden_states, L0_attn_w_q, L0_attn_w_k, L0_attn_w_v,
           L0_attn_w_o, L0_ffn_w_fc, L0_ffn_w_proj):
    x = np.ascontiguousarray(np.asarray(L0_attn_hidden_states, dtype=np.float32))
    base = dict(
        wq=np.ascontiguousarray(L0_attn_w_q, dtype=np.float32),
        wk=np.ascontiguousarray(L0_attn_w_k, dtype=np.float32),
        wv=np.ascontiguousarray(L0_attn_w_v, dtype=np.float32),
        wo=np.ascontiguousarray(L0_attn_w_o, dtype=np.float32),
        wfc=np.ascontiguousarray(L0_ffn_w_fc, dtype=np.float32),
        wpj=np.ascontiguousarray(L0_ffn_w_proj, dtype=np.float32),
        **_consts(),
    )
    nc = build_nc()
    in_maps = [dict(xin=np.ascontiguousarray(x[b]), **base) for b in range(8)]
    res = run_bass_kernel_spmd(nc, in_maps, core_ids=list(range(8)))
    out = np.stack([np.asarray(r["y"], dtype=np.float32) for r in res.results])
    return out


# version 09
# speedup vs baseline: 1.0728x; 1.0062x over previous
import math
import os
import numpy as np
import ml_dtypes
import concourse.bass as bass
import concourse.mybir as mybir
import concourse.bass_utils as _bu
from concourse.bass_utils import run_bass_kernel_spmd

if not getattr(_bu, "_ldwopt_patched", False):
    _bu._ldwopt_patched = True
    _orig_run_command = _bu.run_command

    def _run_command_ldwopt(cmd, **kw):
        cmd = ["--enable-ldw-opt=true" if c == "--enable-ldw-opt=false" else c
               for c in cmd]
        return _orig_run_command(cmd, **kw)

    _bu.run_command = _run_command_ldwopt

F32 = mybir.dt.float32
BF16 = mybir.dt.bfloat16
U8 = mybir.dt.uint8
AX = mybir.AxisListType.X
AF = mybir.ActivationFunctionType
ALU = mybir.AluOpType
bfnp = ml_dtypes.bfloat16

S, D, NH, NKV, HD, WIN, HF = 4096, 2048, 16, 4, 128, 1024, 5120
NT, SS, NSS = 32, 512, 8
NKC = D // 128
NHC = HF // 128
ISQ = 1.0 / math.sqrt(HD)
GQ = NH // NKV

ENGS = ("sync", "vector", "scalar", "tensor", "gpsimd")


class Ctr:
    __slots__ = ("h", "n")

    def __init__(self, h):
        self.h = h
        self.n = 0

    def inc(self, k=1):
        self.n += k
        return self.n


def build_nc():
    nc = bass.Bass("TRN2", target_bir_lowering=False,
                   detect_race_conditions=False)
    din = lambda n, s, d=F32: nc.dram_tensor(n, s, d, kind="ExternalInput").ap()
    _dbg = os.environ.get("KDBG") == "1"
    dint = lambda n, s, d: nc.dram_tensor(
        n, s, d, kind="ExternalOutput" if _dbg else "Internal").ap()

    xin = din("xin", [S, D])
    wq = din("wq", [D, D])
    wk = din("wk", [D, NKV * HD])
    wv = din("wv", [D, NKV * HD])
    wo = din("wo", [D, D])
    wfc = din("wfc", [HF, D])
    wpj = din("wpj", [D, HF])
    cosT_c = din("cosT_c", [128, S], BF16)
    sinsT_c = din("sinsT_c", [128, S], BF16)
    mcT_c = din("mcT_c", [128, 128], BF16)
    mfT_c = din("mfT_c", [128, 128], BF16)
    ident_c = din("ident_c", [128, 128], BF16)
    onesrb_c = din("onesrb_c", [1, 128], BF16)
    onesrf_c = din("onesrf_c", [1, 128], F32)
    onescf_c = din("onescf_c", [128, 1], F32)
    onescb_c = din("onescb_c", [128, 1], BF16)
    epsc_c = din("epsc_c", [128, 1], F32)
    y = nc.dram_tensor("y", [S, D], F32, kind="ExternalOutput").ap()

    xnT_hbm = dint("xnT_hbm", [NSS, 128, NKC * SS], BF16)
    ctxT_hbm = dint("ctxT_hbm", [NSS, 128, NH * SS], BF16)
    x1_hbm = dint("x1_hbm", [S, D], F32)
    h_hbm = dint("h_hbm", [NSS, 128, NHC * SS], BF16)
    sfc_hbm = dint("sfc_hbm", [HF, D], BF16)
    sp_hbm = dint("sp_hbm", [D, HF], BF16)

    from contextlib import ExitStack
    es = ExitStack()

    ARENA = 211968
    arena = es.enter_context(nc.sbuf_tensor("arena", [128, ARENA], U8))
    arena_addr = nc.lookup_mloc(arena).addr

    def place(name, shape, dtype, off):
        assert off % 32 == 0, name
        per = int(np.prod(shape[1:])) * mybir.dt.size(dtype)
        assert off + per <= ARENA, (name, off, per)
        h = nc.virtual_sbuf_tensor(name, list(shape), dtype)
        nc._tensor(h.name, h.shape, h.dtype, type="SB")
        mloc = nc.lookup_mloc(h)
        mloc.allocated = True
        mloc.addr = arena_addr + off
        return h

    w0t = [place(f"w0t{i}", [128, 2048], F32, 143360 + i * 8192) for i in range(2)]
    s0t = [place(f"s0t{i}", [128, 2048], BF16, 159744 + i * 4096) for i in range(2)]
    acc0 = place("acc0", [128, 160], F32, 167936)
    s_xf = [place(f"s_xf{i}", [128, D], F32, i * 8192) for i in range(2)]
    s_mrow = place("s_mrow", [1, 2048], BF16, 0)
    s_Pc = [place(f"s_Pc{i}", [128, 512], BF16, 4096 + i * 1024) for i in range(3)]
    s_mcols = [place(f"s_mcols{i}", [128, 32], BF16, 8192 + i * 64) for i in range(2)]
    s_negrow32 = place("s_negrow32", [1, 32], BF16, 8320)
    s_negrow = place("s_negrow", [1, 16], BF16, 8384)
    s_mbias = [place(f"s_mbias{i}", [128, 16], F32, 8448 + i * 64) for i in range(2)]
    s_lnr = [place(f"s_lnr{i}", [128, 1], F32, 8576 + i * 32) for i in range(2)]
    s_r = [place(f"s_r{i}", [128, 1], F32, 8640 + i * 32) for i in range(2)]
    s_qtmp = [place(f"s_qtmp{i}", [128, SS], BF16, 9216 + i * 1024) for i in range(2)]
    s_qshf = [place(f"s_qshf{i}", [128, SS], BF16, 11264 + i * 1024) for i in range(2)]
    s_t1 = place("s_t1", [128, SS], BF16, 13312)
    s_t2 = place("s_t2", [128, SS], BF16, 14336)
    s_wk = place("s_wk", [128, NKC * 512], BF16, 16384)
    s_wv = place("s_wv", [128, NKC * 512], BF16, 32768)
    s_QT = place("s_QT", [128, NH * SS], BF16, 16384)
    s_xnT = place("s_xnT", [128, NKC * SS], BF16, 32768)
    s_ctxT = place("s_ctxT", [128, NH * SS], BF16, 16384)
    s_K = place("s_K", [128, NKV * S], BF16, 49152)
    s_V = place("s_V", [128, NT * NKV * 128], BF16, 81920)
    s_stage = [place(f"s_stage{i}", [128, D], F32, 49152 + i * 8192) for i in range(2)]
    s_ht = [place(f"s_ht{i}", [128, NHC * 128], BF16, 180736 + i * 10240) for i in range(2)]
    s_yst = place("s_yst", [128, D], F32, 201216)
    s_wq = place("s_wq", [128, NKC * D], BF16, 115200)
    s_wo = place("s_wo", [128, NKC * D], BF16, 115200)
    s_xnsq1 = place("s_xnsq1", [128, D], BF16, 115200)
    s_st1 = place("s_st1", [128, 8], F32, 119296)
    s_ktmp = [place(f"s_ktmp{i}", [128, SS], BF16, 119424 + i * 1024) for i in range(2)]
    s_kshf = [place(f"s_kshf{i}", [128, SS], BF16, 121472 + i * 1024) for i in range(2)]
    s_kt1 = place("s_kt1", [128, SS], BF16, 123520)
    s_kt2 = place("s_kt2", [128, SS], BF16, 124544)
    s_xnT1 = place("s_xnT1", [128, NKC * SS], BF16, 126976)
    s_SfcT = place("s_SfcT", [128, NKC * HF], BF16, 16384)
    s_SpT = place("s_SpT", [128, NHC * D], BF16, 16384)
    s_cosT = place("s_cosT", [128, S], BF16, 180736)
    s_sinsT = place("s_sinsT", [128, S], BF16, 188928)
    s_xn2T = place("s_xn2T", [128, NKC * SS], BF16, 180736)
    s_id = place("s_id", [128, 128], BF16, 197120)
    s_mcT = place("s_mcT", [128, 128], BF16, 197376)
    s_mfT = place("s_mfT", [128, 128], BF16, 197632)
    s_onesrb = place("s_onesrb", [1, 128], BF16, 197888)
    s_onesrf = place("s_onesrf", [1, 128], F32, 198144)
    s_onescf = place("s_onescf", [128, 1], F32, 198656)
    s_eps = place("s_eps", [128, 1], F32, 198688)
    sc0 = place("sc0", [1, 16], F32, 198720)
    scB = place("scB", [128, 8], F32, 198784)
    s_ctx1 = [place(f"s_ctx1{i}", [128, 512], BF16, 198912 + i * 1024) for i in range(2)]
    s_rb = [place(f"s_rb{i}", [128, 512], BF16, 201216 + i * 1024) for i in range(2)]
    s_rrow = place("s_rrow", [1, 512], BF16, 203264)
    s_onescb = place("s_onescb", [128, 1], BF16, 204288)
    s_hst = [place(f"s_hst{i}", [128, 2048], BF16, 198912 + i * 4096) for i in range(2)]
    s_xnsq3 = place("s_xnsq3", [128, D], BF16, 207104)
    s_st3 = place("s_st3", [128, 8], F32, 211232)

    pSC = [es.enter_context(nc.psum_tensor(f"pSC{i}", [128, 1024], F32)) for i in range(2)]
    pCX = [es.enter_context(nc.psum_tensor(f"pCX{i}", [128, 512], F32)) for i in range(2)]
    pLNB = es.enter_context(nc.psum_tensor("pLNB", [128, 1024], BF16))
    pAUX = es.enter_context(nc.psum_tensor("pAUX", [128, 512], F32))

    sems = {}
    ops = {e: [] for e in ENGS}

    def C(name):
        if name not in sems:
            sems[name] = Ctr(es.enter_context(nc.semaphore(name)))
        return sems[name]

    def E(eng, fn):
        ops[eng].append(fn)

    def W(eng, ctr, val):
        if val <= 0:
            return
        E(eng, lambda e, s=ctr.h, v=val: e.wait_ge(s, v))

    def EI(eng, ctr, fn, k=1):
        ctr.inc(k)
        E(eng, lambda e, f=fn, s=ctr.h, kk=k: f(e).then_inc(s, kk))

    class Res:

        def __init__(self):
            self.reqs = []

        def acquire(self, eng):
            for c, v in self.reqs:
                W(eng, c, v)
            self.reqs = []

        def release(self, ctr):
            self.reqs.append((ctr, ctr.n))

    rSC = [Res(), Res()]
    rCX = [Res(), Res()]
    rLNB = Res()
    rAUX = Res()

    FEN = C("FEN")
    fence_round = [0]

    def fence():
        fence_round[0] += 1
        tgt = 5 * fence_round[0]
        for eng in ENGS:
            E(eng, lambda e, s=FEN.h: e.nop().then_inc(s, 1))
        FEN.inc(5)
        for eng in ENGS:
            W(eng, FEN, tgt)

    _PH = int(os.environ.get("KPH", "9"))

    def _finish():
        with nc.Block() as block:
            @block.sync
            def _(e):
                for f in ops["sync"]:
                    f(e)

            @block.vector
            def _(e):
                for f in ops["vector"]:
                    f(e)

            @block.scalar
            def _(e):
                for f in ops["scalar"]:
                    f(e)

            @block.tensor
            def _(e):
                for f in ops["tensor"]:
                    f(e)

            @block.gpsimd
            def _(e):
                for f in ops["gpsimd"]:
                    f(e)
        es.close()
        return nc

    dW = C("dW")
    for dst, src in [(s_cosT, cosT_c), (s_sinsT, sinsT_c), (s_mcT, mcT_c),
                     (s_mfT, mfT_c), (s_id, ident_c), (s_onesrb, onesrb_c),
                     (s_onesrf, onesrf_c), (s_onescf, onescf_c), (s_eps, epsc_c),
                     (s_onescb, onescb_c)]:
        EI("sync", dW, lambda e, d=dst, s=src: e.dma_start(out=d[:], in_=s), 16)
    NCONSTW = dW.n

    dW0 = [C("dW0a"), C("dW0b")]
    vAB = C("vAB")
    tPS = C("tPS")
    gTN = C("gTN")
    dS = [C("dSa"), C("dSb")]
    aC = C("aC")

    mats = [("fc", wfc, sfc_hbm, HF // 128, 1, 2048, HF * D, 0, 6),
            ("pj", wpj, sp_hbm, D // 128, 4, 1280, HF * D, 2, 7)]

    def p0_gen():
     for (mn, wsrc, sdst, nrt, ncc, cw, nel, cthr, cabs) in mats:
          ntile = nrt * ncc
          base_vab = vAB.n
          ti = 0
          for rt in range(nrt):
              for cc in range(ncc):
                  j = ti % 2
                  lq = "sync" if j == 0 else "scalar"
                  if ti >= 2:
                      W(lq, vAB, base_vab + ti - 1)
                  else:
                      W(lq, gTN, gTN.n)
                  EI(lq, dW0[j], lambda e, j=j, rt=rt, cc=cc, w=wsrc, cw=cw:
                     e.dma_start(out=w0t[j][:, 0:cw], in_=w[rt * 128:(rt + 1) * 128,
                                                           cc * cw:(cc + 1) * cw]), 16)
                  W("vector", dW0[j], dW0[j].n)
                  EI("vector", vAB, lambda e, j=j, ti=ti, cw=cw:
                     e.tensor_reduce(out=acc0[:, ti:ti + 1], in_=w0t[j][:, 0:cw],
                                     axis=AX, op=ALU.add, apply_absolute_value=True))
                  ti += 1
                  yield
          W("vector", vAB, vAB.n)
          EI("vector", vAB, lambda e, nt=ntile:
             e.tensor_reduce(out=acc0[:, 159:160], in_=acc0[:, 0:nt], axis=AX,
                             op=ALU.add))
          W("tensor", vAB, vAB.n)
          W("tensor", dW, NCONSTW)
          rAUX.acquire("tensor")
          EI("tensor", tPS, lambda e:
             e.matmul(out=pAUX[0:1, 20:21], lhsT=acc0[:, 159:160], rhs=s_onescf[:],
                      start=True, stop=True))
          W("vector", tPS, tPS.n)

          steps = [
              lambda e: e.tensor_copy(out=sc0[0:1, 0:1], in_=pAUX[0:1, 20:21]),
              lambda e, nel=nel: e.tensor_scalar(out=sc0[0:1, 1:2], in0=sc0[0:1, 0:1],
                                                 scalar1=1.0 / nel, scalar2=None,
                                                 op0=ALU.mult),
              lambda e: e.tensor_scalar(out=sc0[0:1, 2:3], in0=sc0[0:1, 1:2],
                                        scalar1=1e-5, scalar2=None, op0=ALU.max),
              lambda e: e.tensor_scalar(out=sc0[0:1, 3:4], in0=sc0[0:1, 2:3],
                                        scalar1=0.7, scalar2=None, op0=ALU.mult),
              lambda e: e.tensor_scalar(out=sc0[0:1, 4:5], in0=sc0[0:1, 3:4],
                                        scalar1=-1.0, scalar2=None, op0=ALU.mult),
              lambda e, cabs=cabs: e.tensor_copy(out=sc0[0:1, cabs:cabs + 1],
                                                 in_=sc0[0:1, 2:3]),
          ]
          for stp in steps:
              EI("vector", vAB, stp)
              W("vector", vAB, vAB.n)
          W("tensor", vAB, vAB.n)
          rAUX.acquire("tensor")
          EI("tensor", tPS, lambda e:
             e.matmul(out=pAUX[:, 21:22], lhsT=s_onesrf[:], rhs=sc0[0:1, 3:4],
                      start=True, stop=True))
          EI("tensor", tPS, lambda e:
             e.matmul(out=pAUX[:, 22:23], lhsT=s_onesrf[:], rhs=sc0[0:1, 4:5],
                      start=True, stop=True))
          W("vector", tPS, tPS.n)

          def evac_thr(e, cthr=cthr):
              e.tensor_copy(out=scB[:, cthr:cthr + 1], in_=pAUX[:, 21:22])
              return e.tensor_copy(out=scB[:, cthr + 1:cthr + 2], in_=pAUX[:, 22:23])
          EI("vector", vAB, evac_thr)
          rAUX.release(vAB)
          base_gtn = gTN.n
          base_ds = [dS[0].n, dS[1].n]
          vab_thr = vAB.n
          ti = 0
          for rt in range(nrt):
              for cc in range(ncc):
                  j = ti % 2
                  lq = "sync" if j == 0 else "scalar"
                  if ti >= 2:
                      W(lq, gTN, base_gtn + 3 * (ti - 1))
                  else:
                      W(lq, vAB, vab_thr)
                  EI(lq, dW0[j], lambda e, j=j, rt=rt, cc=cc, w=wsrc, cw=cw:
                     e.dma_start(out=w0t[j][:, 0:cw], in_=w[rt * 128:(rt + 1) * 128,
                                                           cc * cw:(cc + 1) * cw]), 16)
                  W("vector", dW0[j], dW0[j].n)
                  if ti >= 2:
                      W("vector", dS[j], base_ds[j] + 16 * (ti // 2))

                  EI("vector", gTN, lambda e, j=j, c=cthr, cw=cw:
                     e.tensor_scalar(out=s0t[j][:, 0:cw], in0=w0t[j][:, 0:cw],
                                     scalar1=scB[:, c:c + 1], scalar2=None,
                                     op0=ALU.is_gt))
                  EI("vector", gTN, lambda e, j=j, c=cthr, cw=cw:
                     e.tensor_scalar(out=w0t[j][:, 0:cw], in0=w0t[j][:, 0:cw],
                                     scalar1=scB[:, c + 1:c + 2], scalar2=None,
                                     op0=ALU.is_lt))
                  W("vector", gTN, gTN.n)
                  EI("vector", gTN, lambda e, j=j, cw=cw:
                     e.tensor_sub(out=s0t[j][:, 0:cw], in0=s0t[j][:, 0:cw],
                                  in1=w0t[j][:, 0:cw]))
                  W("gpsimd", gTN, gTN.n)
                  EI("gpsimd", dS[j], lambda e, j=j, rt=rt, cc=cc, sd=sdst, cw=cw:
                     e.dma_start(out=sd[rt * 128:(rt + 1) * 128,
                                        cc * cw:(cc + 1) * cw], in_=s0t[j][:, 0:cw]), 16)
                  ti += 1
                  yield

     EI("vector", vAB, lambda e: e.tensor_mul(out=sc0[0:1, 8:9], in0=sc0[0:1, 6:7],
                                              in1=sc0[0:1, 6:7]))
     W("vector", vAB, vAB.n)
     EI("vector", vAB, lambda e: e.tensor_mul(out=sc0[0:1, 8:9], in0=sc0[0:1, 8:9],
                                              in1=sc0[0:1, 7:8]))
     W("scalar", vAB, vAB.n)

     EI("scalar", aC, lambda e: e.activation(out=sc0[0:1, 9:10], in_=sc0[0:1, 8:9],
                                             func=AF.Ln))
     W("scalar", aC, aC.n)
     EI("scalar", aC, lambda e: e.activation(out=sc0[0:1, 10:11],
                                             in_=sc0[0:1, 9:10], func=AF.Exp,
                                             scale=0.5))
     W("tensor", aC, aC.n)
     rAUX.acquire("tensor")
     EI("tensor", tPS, lambda e:
        e.matmul(out=pAUX[:, 23:24], lhsT=s_onesrf[:], rhs=sc0[0:1, 10:11],
                 start=True, stop=True))
     W("vector", tPS, tPS.n)
     EI("vector", vAB, lambda e: e.tensor_copy(out=scB[:, 4:5], in_=pAUX[:, 23:24]))
     rAUX.release(vAB)

     if _dbg:
         o_scB = nc.dram_tensor("o_scB", [128, 8], F32, kind="ExternalOutput").ap()
         o_sc0 = nc.dram_tensor("o_sc0", [1, 16], F32, kind="ExternalOutput").ap()
         dDBG = C("dDBG")
         W("sync", vAB, vAB.n)
         EI("sync", dDBG, lambda e: e.dma_start(out=o_scB, in_=scB[:]), 16)
         EI("sync", dDBG, lambda e: e.dma_start(out=o_sc0, in_=sc0[:]), 16)
         W("sync", dDBG, 32)

     yield

    _p0 = p0_gen()
    _OVL = int(os.environ.get("KOVL", "1"))

    def p0_pump(n):
        for _ in range(n):
            if next(_p0, "done") == "done":
                break

    if not _OVL:
        for _ in _p0:
            pass

    if _PH <= 1:
        for _ in _p0:
            pass
        return _finish()

    dX = [C("dXa"), C("dXb")]
    aSQ = C("aSQ")
    vRED = C("vRED")
    vST = C("vST")
    aRS = C("aRS")
    vXN = C("vXN")
    tTR = C("tTR")
    vTE = C("vTE")

    def emit_ln_tile(src_hbm, tt, xnsq, stats, xnT_dst, dve_prewait=None):
        j = tt % 2
        if vXN.n >= 2:
            W("sync", vXN, vXN.n - 1)
        EI("sync", dX[j], lambda e, j=j, tt=tt, src=src_hbm:
           e.dma_start(out=s_xf[j][:], in_=src[tt * 128:(tt + 1) * 128, :]), 16)
        ldx = dX[j].n
        W("scalar", dX[j], ldx)
        W("scalar", tTR, tTR.n)
        EI("scalar", aSQ, lambda e, j=j, xnsq=xnsq, stats=stats:
           e.activation(out=xnsq[:], in_=s_xf[j][:], func=AF.Square,
                        accum_out=stats[:, 1:2]))
        W("vector", dX[j], ldx)
        EI("vector", vRED, lambda e, j=j, stats=stats:
           e.tensor_reduce(out=stats[:, 0:1], in_=s_xf[j][:], axis=AX, op=ALU.add))
        W("vector", vRED, vRED.n)
        W("vector", aSQ, aSQ.n)

        EI("vector", vST, lambda e, stats=stats:
           e.tensor_scalar(out=stats[:, 2:3], in0=stats[:, 0:1],
                           scalar1=-1.0 / D, scalar2=None, op0=ALU.mult))
        EI("vector", vST, lambda e, stats=stats:
           e.tensor_scalar(out=stats[:, 3:4], in0=stats[:, 1:2],
                           scalar1=1.0 / D, scalar2=None, op0=ALU.mult))
        W("vector", vST, vST.n)
        EI("vector", vST, lambda e, stats=stats:
           e.scalar_tensor_tensor(out=stats[:, 4:5], in0=stats[:, 2:3],
                                  scalar=stats[:, 2:3], in1=stats[:, 3:4],
                                  op0=ALU.mult, op1=ALU.subtract))
        W("scalar", vST, vST.n)

        def rstd_chain(e, stats=stats):
            e.activation(out=stats[:, 5:6], in_=stats[:, 4:5], func=AF.Ln,
                         bias=s_eps[:], scale=-1.0)
            return e.activation(out=stats[:, 6:7], in_=stats[:, 5:6], func=AF.Exp,
                                scale=-0.5)
        EI("scalar", aRS, rstd_chain)
        W("vector", aRS, aRS.n)
        if dve_prewait is not None:
            dve_prewait()
        EI("vector", vXN, lambda e, j=j, xnsq=xnsq, stats=stats:
           e.tensor_scalar(out=xnsq[:], in0=s_xf[j][:], scalar1=stats[:, 2:3],
                           scalar2=stats[:, 6:7], op0=ALU.add, op1=ALU.mult))
        r = tt % 4
        W("tensor", vXN, vXN.n)
        for f in range(2):
            rLNB.acquire("tensor")
            for kk in range(8):
                k = 8 * f + kk
                fn = lambda e, k=k, kk=kk, xnsq=xnsq: e.transpose(
                    out=pLNB[:, kk * 128:(kk + 1) * 128],
                    in_=xnsq[:, k * 128:(k + 1) * 128], identity=s_id[:])
                if kk == 7:
                    EI("tensor", tTR, fn)
                else:
                    E("tensor", fn)
            W("vector", tTR, tTR.n)

            def evac(e, f=f, r=r, dst=xnT_dst):
                o = dst[:, :].rearrange("p (k q) -> p k q", k=NKC, q=SS)
                o = o[:, 8 * f:8 * f + 8, r * 128:(r + 1) * 128]
                return e.tensor_copy(
                    out=o, in_=pLNB[:, 0:1024].rearrange("p (k q) -> p k q",
                                                         k=8, q=128))
            EI("vector", vTE, evac)
            rLNB.release(vTE)

    dWG = C("dWG")
    W("gpsimd", dS[0], dS[0].n)
    W("gpsimd", dS[1], dS[1].n)
    for k in range(NKC):
        EI("gpsimd", dWG, lambda e, k=k:
           e.dma_start(out=s_wk[:, k * 512:(k + 1) * 512],
                       in_=wk[k * 128:(k + 1) * 128, :]), 16)
    for k in range(NKC):
        EI("gpsimd", dWG, lambda e, k=k:
           e.dma_start(out=s_wv[:, k * 512:(k + 1) * 512],
                       in_=wv[k * 128:(k + 1) * 128, :]), 16)
    NWKV = dWG.n

    dXN_ST = C("dXN_ST")
    tKV = C("tKV")
    aKE = C("aKE")
    dKS = C("dKS")
    vKR = C("vKR")
    aVE = C("aVE")

    for s in range(NSS):
        for r in range(4):
            tt = 4 * s + r

            def dve_pre():
                if s >= 1:
                    W("vector", dXN_ST, 16 * s)
                    W("vector", tKV, 8 * s)
            emit_ln_tile(xin, tt, s_xnsq1, s_st1, s_xnT1,
                         dve_prewait=dve_pre if r == 0 else None)
            p0_pump(4)
        W("sync", vTE, vTE.n)
        EI("sync", dXN_ST, lambda e, s=s:
           e.dma_start(out=xnT_hbm[s], in_=s_xnT1[:]), 16)
        W("tensor", vTE, vTE.n)
        W("tensor", dWG, NWKV)
        for kvh in range(NKV):
            p = kvh % 2
            rSC[p].acquire("tensor")
            for k in range(NKC):
                fn = lambda e, k=k, kvh=kvh, p=p: e.matmul(
                    out=pSC[p][:, 0:512],
                    lhsT=s_wk[:, k * 512 + kvh * 128:k * 512 + (kvh + 1) * 128],
                    rhs=s_xnT1[:, k * SS:(k + 1) * SS],
                    start=(k == 0), stop=(k == NKC - 1))
                if k == NKC - 1:
                    EI("tensor", tKV, fn)
                else:
                    E("tensor", fn)
            j = kvh % 2
            W("scalar", tKV, tKV.n)
            if vKR.n >= 2:
                W("scalar", vKR, vKR.n - 1)
            EI("scalar", aKE, lambda e, p=p, j=j:
               e.copy(out=s_ktmp[j][:], in_=pSC[p][:, 0:512]))
            rSC[p].release(aKE)
            W("sync", aKE, aKE.n)
            if vKR.n >= 2:
                W("sync", vKR, vKR.n - 1)
            E("sync", lambda e, j=j, sh=dKS.h: (
                e.dma_start(out=s_kshf[j][0:64, :],
                            in_=s_ktmp[j][64:128, :]).then_inc(sh, 16),
                e.dma_start(out=s_kshf[j][64:128, :],
                            in_=s_ktmp[j][0:64, :]).then_inc(sh, 16)))
            dKS.inc(32)
            W("vector", aKE, aKE.n)
            W("vector", dKS, dKS.n)

            def rope_k(e, j=j, s=s, kvh=kvh):
                e.tensor_mul(out=s_kt1[:], in0=s_ktmp[j][:],
                             in1=s_cosT[:, s * SS:(s + 1) * SS])
                e.tensor_mul(out=s_kt2[:], in0=s_kshf[j][:],
                             in1=s_sinsT[:, s * SS:(s + 1) * SS])
                return e.tensor_add(
                    out=s_K[:, kvh * S + s * SS:kvh * S + (s + 1) * SS],
                    in0=s_kt1[:], in1=s_kt2[:])
            EI("vector", vKR, rope_k)
        for tb in range(4):
            p = tb % 2
            cpos = 4 * s + tb
            rSC[p].acquire("tensor")
            for k in range(NKC):
                fn = lambda e, k=k, tb=tb, p=p: e.matmul(
                    out=pSC[p][:, 0:512],
                    lhsT=s_xnT1[:, k * SS + tb * 128:k * SS + (tb + 1) * 128],
                    rhs=s_wv[:, k * 512:(k + 1) * 512],
                    start=(k == 0), stop=(k == NKC - 1))
                if k == NKC - 1:
                    EI("tensor", tKV, fn)
                else:
                    E("tensor", fn)
            W("scalar", tKV, tKV.n)

            def evac_v(e, p=p, cpos=cpos):
                return e.copy(out=s_V[:, cpos * 512:(cpos + 1) * 512],
                              in_=pSC[p][:, 0:512])
            EI("scalar", aVE, evac_v)
            rSC[p].release(aVE)
        p0_pump(10)

    for _ in _p0:
        pass

    fence()
    if _PH <= 2:
        return _finish()

    W("gpsimd", dS[0], dS[0].n)
    W("gpsimd", dS[1], dS[1].n)
    for k in range(NKC):
        EI("gpsimd", dWG, lambda e, k=k:
           e.dma_start(out=s_wq[:, k * D:(k + 1) * D],
                       in_=wq[k * 128:(k + 1) * 128, :]), 16)
    NWQ = dWG.n
    vMS = C("vMS")
    EI("vector", vMS, lambda e: (e.memset(s_mbias[0][:], 0.0),
                                 e.memset(s_mbias[1][:], 0.0),
                                 e.memset(s_Pc[0][:], 0.0),
                                 e.memset(s_Pc[1][:], 0.0),
                                 e.memset(s_Pc[2][:], 0.0))[-1])

    dXN_LD = C("dXN_LD")
    tQP = C("tQP")
    aQE = C("aQE")
    dQS = C("dQS")
    vQR = C("vQR")
    tSC = [C("tSC0"), C("tSC1")]
    vMX = [C("vMX0"), C("vMX1")]
    aEX = [C("aEX0"), C("aEX1")]
    gMK = [C("gMK0"), C("gMK1")]
    tPV = [C("tPV0"), C("tPV1")]
    aRC = [C("aRC0"), C("aRC1")]
    vCX = [C("vCX0"), C("vCX1")]
    dCTX = C("dCTX")
    dMB = C("dMB")
    vMB = C("vMB")
    tMB = C("tMB")
    aMB = C("aMB")
    vMBE = C("vMBE")

    tSC2 = C("tSC2")
    aEX2 = C("aEX2")
    gMK2 = C("gMK2")
    tPV2 = [C("tPV2a"), C("tPV2b")]
    aRC2 = [C("aRC2a"), C("aRC2b")]
    tRB2 = [C("tRB2a"), C("tRB2b")]
    vRB2 = [C("vRB2a"), C("vRB2b")]
    vCX2 = [C("vCX2a"), C("vCX2b")]
    vMX2 = C("vMX2")
    dCTX2 = C("dCTX2")
    dMB2 = C("dMB2")
    vMB2 = C("vMB2")
    aMB2 = C("aMB2")
    tMB2 = C("tMB2")
    vMBE2 = C("vMBE2")
    rSCB = [Res(), Res()]
    rCXB = [Res(), Res()]
    rSMB = [Res(), Res()]
    sp_guard = {0: [], 1: [], 2: []}
    scb = [lambda a, b: pSC[0][:, a:b], lambda a, b: pSC[0][:, 512 + a:512 + b]]
    ctxb = [lambda a, b: pSC[1][:, a:b], lambda a, b: pSC[1][:, 512 + a:512 + b]]
    sumb = [lambda a, b: pCX[0][0:1, a:b], lambda a, b: pCX[1][0:1, a:b]]
    rbb = [lambda a, b: pCX[0][:, a:b], lambda a, b: pCX[1][:, a:b]]

    def emit_group(s, h):
        g = 16 * s + h
        pg = g % 2
        kv = h // GQ
        c0 = max(0, 4 * s - 8)
        chunks = list(range(c0, 4 * s + 4))
        geo = []
        for c in chunks:
            rel = c - 4 * s
            q0 = max(0, rel * 128)
            q1 = min(512, rel * 128 + 1152)
            tri = None
            if 0 <= rel <= 3:
                tri = ("c", rel * 128)
            elif -8 <= rel <= -5:
                tri = ("f", rel * 128 + 1024)
            geo.append((c, q0, q1, tri))
        geo.sort(key=lambda x: 0 if x[0] == 4 * s else 1)
        nch = len(geo)
        exp_marks = []

        def emit_chunk_scores(ci):
            c, q0, q1, tri = geo[ci]
            bk = ci % 2
            rSCB[bk].acquire("tensor")
            if ci == 0:
                W("tensor", vQR, 16 * s + h + 1)
            EI("tensor", tSC2, lambda e, c=c, q0=q0, q1=q1, bk=bk:
               e.matmul(out=scb[bk](q0, q1),
                        lhsT=s_K[:, kv * S + c * 128:kv * S + (c + 1) * 128],
                        rhs=s_QT[:, h * SS + q0:h * SS + q1],
                        start=True, stop=True))
            W("scalar", tSC2, tSC2.n)
            if s >= 1:
                W("scalar", vMBE2, 2 * s)
            else:
                W("scalar", vMS, 1)
            sp = ci % 3
            for cc, vv in sp_guard[sp]:
                W("scalar", cc, vv)
            sp_guard[sp] = []
            EI("scalar", aEX2, lambda e, sp=sp, q0=q0, q1=q1, bk=bk:
               e.activation(out=s_Pc[sp][:, q0:q1], in_=scb[bk](q0, q1),
                            func=AF.Exp, bias=s_mbias[s % 2][:, h:h + 1],
                            scale=ISQ))
            rSCB[bk].release(aEX2)
            if tri is not None:
                W("gpsimd", aEX2, aEX2.n)
                kind, t0c = tri
                mk = s_mcT if kind == "c" else s_mfT
                EI("gpsimd", gMK2, lambda e, sp=sp, t0c=t0c, mk=mk:
                   e.tensor_mul(out=s_Pc[sp][:, t0c:t0c + 128],
                                in0=s_Pc[sp][:, t0c:t0c + 128], in1=mk[:]))
                exp_marks.append((ci, (gMK2, gMK2.n)))
            else:
                exp_marks.append((ci, (aEX2, aEX2.n)))

        def emit_chunk_pv(ci):
            c, q0, q1, tri = geo[ci]
            sp = ci % 3
            cc, vv = exp_marks[ci][1]
            W("tensor", cc, vv)
            if ci == 0:
                rCXB[pg].acquire("tensor")
                rSMB[pg].acquire("tensor")
            vb = (c * NKV + kv) * 128
            E("tensor", lambda e, sp=sp, q0=q0, q1=q1, vb=vb, ci=ci:
              e.matmul(out=ctxb[pg](q0, q1),
                       lhsT=s_V[:, vb:vb + 128],
                       rhs=s_Pc[sp][:, q0:q1],
                       start=(ci == 0), stop=(ci == nch - 1)))
            fn = lambda e, sp=sp, q0=q0, q1=q1, ci=ci: e.matmul(
                out=sumb[pg](q0, q1),
                lhsT=s_onescb[:],
                rhs=s_Pc[sp][:, q0:q1],
                start=(ci == 0), stop=(ci == nch - 1))
            EI("tensor", tPV2[pg], fn)
            sp_guard[sp].append((tPV2[pg], tPV2[pg].n))

        emit_chunk_scores(0)
        for ci in range(1, nch):
            emit_chunk_scores(ci)
            emit_chunk_pv(ci - 1)
        emit_chunk_pv(nch - 1)
        W("vector", exp_marks[-1][1][0], exp_marks[-1][1][1])
        if s >= 2:
            W("vector", dMB2, 16 * (s - 1))
        EI("vector", vMX2, lambda e, sp=(nch - 1) % 3:
           e.reduce_max(out=s_mcols[s % 2][:, h:h + 1], in_=s_Pc[sp][:, 0:512],
                        axis=AX))
        sp_guard[(nch - 1) % 3].append((vMX2, vMX2.n))
        W("scalar", tPV2[pg], tPV2[pg].n)
        W("scalar", tRB2[0], tRB2[0].n)
        W("scalar", tRB2[1], tRB2[1].n)
        EI("scalar", aRC2[pg], lambda e:
           e.activation(out=s_rrow[:], in_=sumb[pg](0, 512), func=AF.Ln))
        W("scalar", aRC2[pg], aRC2[pg].n)
        EI("scalar", aRC2[pg], lambda e:
           e.activation(out=s_rrow[:], in_=s_rrow[:], func=AF.Exp, scale=-1.0))
        W("tensor", aRC2[pg], aRC2[pg].n)
        EI("tensor", tRB2[pg], lambda e: e.matmul(
            out=rbb[pg](0, 512), lhsT=s_onesrb[:], rhs=s_rrow[0:1, 0:512],
            start=True, stop=True))
        W("vector", tRB2[pg], tRB2[pg].n)
        if g >= 2:
            W("vector", dCTX2, 16 * (g - 1))
        EI("vector", vRB2[pg], lambda e:
           e.tensor_copy(out=s_rb[pg][:], in_=rbb[pg](0, 512)))
        rSMB[pg].release(vRB2[pg])
        EI("vector", vCX2[pg], lambda e:
           e.tensor_mul(out=s_ctx1[g % 2][:], in0=ctxb[pg](0, 512),
                        in1=s_rb[pg][:]))
        rCXB[pg].release(vCX2[pg])
        W("sync", vCX2[pg], vCX2[pg].n)
        EI("sync", dCTX2, lambda e, s=s, h=h, g=g: e.dma_start(
            out=ctxT_hbm[s][:, h * SS:(h + 1) * SS], in_=s_ctx1[g % 2][:]), 16)

    def emit_mbias2(s):
        W("sync", vMX2, vMX2.n)
        EI("sync", dMB2, lambda e, s=s: e.dma_start(
            out=s_mrow[0:1, :].rearrange("o (p c) -> o p c", p=128, c=16),
            in_=s_mcols[s % 2][:, 0:16]), 16)
        W("vector", dMB2, dMB2.n)
        EI("vector", vMB2, lambda e: e.tensor_reduce(
            out=s_negrow[:], in_=s_mrow[0:1, :].rearrange(
                "o (p c) -> o c p", p=128, c=16), axis=AX, op=ALU.max))
        W("scalar", vMB2, vMB2.n)
        EI("scalar", aMB2, lambda e:
           e.activation(out=s_negrow[:], in_=s_negrow[:], func=AF.Ln))
        W("tensor", aMB2, aMB2.n)
        rAUX.acquire("tensor")
        EI("tensor", tMB2, lambda e: e.matmul(
            out=pAUX[:, 0:16], lhsT=s_onesrb[:], rhs=s_negrow[0:1, 0:16],
            start=True, stop=True))
        W("vector", tMB2, tMB2.n)
        W("vector", aEX2, aEX2.n)

        EI("vector", vMBE2, lambda e, s=s:
           e.tensor_copy(out=s_mbias[(s + 1) % 2][:], in_=s_mbias[s % 2][:]))
        W("vector", vMBE2, vMBE2.n)
        EI("vector", vMBE2, lambda e, s=s:
           e.tensor_sub(out=s_mbias[(s + 1) % 2][:],
                        in0=s_mbias[(s + 1) % 2][:], in1=pAUX[:, 0:16]))
        rAUX.release(vMBE2)

    for s in range(NSS):
        if s >= 1:
            W("sync", tQP, 16 * s)
        W("sync", dXN_ST, 16 * (s + 1))
        EI("sync", dXN_LD, lambda e, s=s:
           e.dma_start(out=s_xnT[:], in_=xnT_hbm[s]), 16)
        W("tensor", dXN_LD, dXN_LD.n)
        W("tensor", dWG, NWQ)
        for h in range(NH):
            p = h % 2
            (rSCB[0] if p == 0 else rCXB[0]).acquire("tensor")
            if s > 0 or h >= 2:
                W("tensor", aQE, aQE.n - 1)
            for k in range(NKC):
                fn = lambda e, k=k, h=h, p=p: e.matmul(
                    out=pSC[p][:, 0:512],
                    lhsT=s_wq[:, k * D + h * 128:k * D + (h + 1) * 128],
                    rhs=s_xnT[:, k * SS:(k + 1) * SS],
                    start=(k == 0), stop=(k == NKC - 1))
                if k == NKC - 1:
                    EI("tensor", tQP, fn)
                else:
                    E("tensor", fn)
            j = h % 2
            W("scalar", tQP, tQP.n)
            if vQR.n >= 2:
                W("scalar", vQR, vQR.n - 1)
            EI("scalar", aQE, lambda e, p=p, j=j:
               e.copy(out=s_qtmp[j][:], in_=pSC[p][:, 0:512]))
            (rSCB[0] if p == 0 else rCXB[0]).release(aQE)
            W("sync", aQE, aQE.n)
            if vQR.n >= 2:
                W("sync", vQR, vQR.n - 1)
            E("sync", lambda e, j=j, sh=dQS.h: (
                e.dma_start(out=s_qshf[j][0:64, :],
                            in_=s_qtmp[j][64:128, :]).then_inc(sh, 16),
                e.dma_start(out=s_qshf[j][64:128, :],
                            in_=s_qtmp[j][0:64, :]).then_inc(sh, 16)))
            dQS.inc(32)
            W("vector", aQE, aQE.n)
            W("vector", dQS, dQS.n)
            if h == 0 and s > 0:
                W("vector", tSC2, tSC2.n)

            def rope_q(e, j=j, s=s, h=h):
                e.tensor_mul(out=s_t1[:], in0=s_qtmp[j][:],
                             in1=s_cosT[:, s * SS:(s + 1) * SS])
                e.tensor_mul(out=s_t2[:], in0=s_qshf[j][:],
                             in1=s_sinsT[:, s * SS:(s + 1) * SS])
                return e.tensor_add(out=s_QT[:, h * SS:(h + 1) * SS],
                                    in0=s_t1[:], in1=s_t2[:])
            EI("vector", vQR, rope_q)
        W("tensor", aQE, aQE.n)
        for bk in range(2):
            rSCB[bk].acquire("tensor")
        for h in range(NH):
            emit_group(s, h)
        emit_mbias2(s)

    fence()
    if _PH <= 3:
        return _finish()

    for k in range(NKC):
        EI("gpsimd", dWG, lambda e, k=k:
           e.dma_start(out=s_wo[:, k * D:(k + 1) * D],
                       in_=wo[k * 128:(k + 1) * 128, :]), 16)
    NWO = dWG.n
    dCT_LD = C("dCT_LD")
    dX2 = C("dX2")
    dX1_ST = C("dX1_ST")
    tOU = [C("tOU0"), C("tOU1")]
    vOU = [C("vOU0"), C("vOU1")]
    pc2 = [0]
    base_tou = tOU[0].n
    for s in range(NSS):
        if s >= 1:
            W("sync", tOU[0], base_tou + 4 * s)
        EI("sync", dCT_LD, lambda e, s=s: e.dma_start(
            out=s_ctxT[:], in_=ctxT_hbm[s]), 16)
        for tb in range(4):
            t = 4 * s + tb
            if t >= 2:
                W("sync", vOU[0], vOU[0].n)
            EI("sync", dX2, lambda e, t=t:
               e.dma_start(out=s_xf[t % 2][:], in_=xin[t * 128:(t + 1) * 128, :]), 16)
            W("tensor", dCT_LD, 16 * (s + 1))
            W("tensor", dWG, NWO)
            W("tensor", vOU[0], vOU[0].n)
            pb = [(pSC[0], 0), (pSC[0], 512), (pSC[1], 0), (pSC[1], 512)]
            for h in range(NH):
                for n in range(4):
                    bt, bo = pb[n]
                    fn = lambda e, h=h, tb=tb, n=n, bt=bt, bo=bo: e.matmul(
                        out=bt[:, bo:bo + 512],
                        lhsT=s_ctxT[:, h * SS + tb * 128:h * SS + (tb + 1) * 128],
                        rhs=s_wo[:, h * D + n * 512:h * D + (n + 1) * 512],
                        start=(h == 0), stop=(h == NH - 1))
                    if h == NH - 1 and n == 3:
                        EI("tensor", tOU[0], fn)
                    else:
                        E("tensor", fn)
            W("vector", tOU[0], tOU[0].n)
            W("vector", dX2, 16 * (t + 1))
            if t >= 2:
                W("vector", dX1_ST, 16 * (t - 1))

            def evac2b(e, t=t, pb=pb):
                for n in range(4):
                    bt, bo = pb[n]
                    i = e.tensor_add(out=s_stage[t % 2][:, n * 512:(n + 1) * 512],
                                     in0=bt[:, bo:bo + 512],
                                     in1=s_xf[t % 2][:, n * 512:(n + 1) * 512])
                return i
            EI("vector", vOU[0], evac2b)
            W("sync", vOU[0], vOU[0].n)
            EI("sync", dX1_ST, lambda e, t=t:
               e.dma_start(out=x1_hbm[t * 128:(t + 1) * 128, :],
                           in_=s_stage[t % 2][:]), 16)

    fence()
    if _PH <= 4:
        return _finish()

    dSF = C("dSF")
    for k in range(NKC):
        EI("sync", dSF, lambda e, k=k: e.dma_start(
            out=s_SfcT[:, k * HF:(k + 1) * HF],
            in_=sfc_hbm[:, k * 128:(k + 1) * 128], transpose=True), 16)
    tH = [C("tH0"), C("tH1")]
    aH = [C("aH0"), C("aH1")]
    dH_ST = C("dH_ST")
    for s in range(NSS):
        for r in range(4):
            tt = 4 * s + r

            def dve_pre3():
                if s >= 1:
                    W("vector", tH[0], (NHC * s) // 2)
                    W("vector", tH[1], (NHC * s) // 2)
            emit_ln_tile(x1_hbm, tt, s_xnsq3, s_st3, s_xn2T,
                         dve_prewait=dve_pre3 if r == 0 else None)
        W("tensor", vTE, vTE.n)
        W("tensor", dSF, 16 * NKC)
        for hh in range(NHC):
            p = hh % 2
            rSC[p].acquire("tensor")
            for k in range(NKC):
                fn = lambda e, k=k, hh=hh, p=p: e.matmul(
                    out=pSC[p][:, 0:512],
                    lhsT=s_SfcT[:, k * HF + hh * 128:k * HF + (hh + 1) * 128],
                    rhs=s_xn2T[:, k * SS:(k + 1) * SS],
                    start=(k == 0), stop=(k == NKC - 1))
                if k == NKC - 1:
                    EI("tensor", tH[p], fn)
                else:
                    E("tensor", fn)
            jb = (hh // 4) % 2
            W("scalar", tH[p], tH[p].n)
            bidx = (NHC // 4) * s + hh // 4
            if hh % 4 == 0 and bidx >= 2:
                W("scalar", dH_ST, 16 * (bidx - 1))

            def relu_sq(e, p=p, jb=jb, hh=hh):
                e.activation(out=pAUX[:, 0:512], in_=pSC[p][:, 0:512],
                             func=AF.Relu, scale=scB[:, 4:5])
                return e.activation(
                    out=s_hst[jb][:, (hh % 4) * 512:(hh % 4 + 1) * 512],
                    in_=pAUX[:, 0:512], func=AF.Square)
            EI("scalar", aH[p], relu_sq)
            rSC[p].release(aH[p])
            if hh % 4 == 3:
                W("sync", aH[0], aH[0].n)
                W("sync", aH[1], aH[1].n)
                EI("sync", dH_ST, lambda e, s=s, hh=hh, jb=jb: e.dma_start(
                    out=h_hbm[s][:, (hh - 3) * 512:(hh + 1) * 512],
                    in_=s_hst[jb][:]), 16)

    fence()
    if _PH <= 5:
        return _finish()

    dSP = C("dSP")
    for jc in range(NHC):
        EI("sync", dSP, lambda e, jc=jc: e.dma_start(
            out=s_SpT[:, jc * D:(jc + 1) * D],
            in_=sp_hbm[:, jc * 128:(jc + 1) * 128], transpose=True), 16)
    dH_LD = C("dH_LD")
    dX1_LD = C("dX1_LD")
    dY = C("dY")
    pc3 = [0]
    for t in range(NT):
        s, tb = t // 4, t % 4
        if t >= 2:
            W("sync", vOU[0], vOU[0].n)

        def ht_in(e, s=s, tb=tb, j=t % 2):
            src = h_hbm[s].rearrange("p (c q) -> p c q", c=NHC, q=SS)
            src = src[:, :, tb * 128:(tb + 1) * 128]
            return e.dma_start(out=s_ht[j][:, :].rearrange(
                "p (c q) -> p c q", c=NHC, q=128), in_=src)
        EI("sync", dH_LD, ht_in, 16)
        EI("sync", dX1_LD, lambda e, t=t: e.dma_start(
            out=s_xf[t % 2][:], in_=x1_hbm[t * 128:(t + 1) * 128, :]), 16)
        W("tensor", dSP, 16 * NHC)
        W("tensor", dH_LD, 16 * (t + 1))
        W("tensor", vOU[0], vOU[0].n)
        pb = [(pSC[0], 0), (pSC[0], 512), (pSC[1], 0), (pSC[1], 512)]
        for jc in range(NHC):
            for n in range(4):
                bt, bo = pb[n]
                fn = lambda e, jc=jc, n=n, bt=bt, bo=bo, jj=t % 2: e.matmul(
                    out=bt[:, bo:bo + 512],
                    lhsT=s_ht[jj][:, jc * 128:(jc + 1) * 128],
                    rhs=s_SpT[:, jc * D + n * 512:jc * D + (n + 1) * 512],
                    start=(jc == 0), stop=(jc == NHC - 1))
                if jc == NHC - 1 and n == 3:
                    EI("tensor", tOU[0], fn)
                else:
                    E("tensor", fn)
        W("vector", tOU[0], tOU[0].n)
        W("vector", dX1_LD, 16 * (t + 1))
        if t >= 1:
            W("vector", dY, 16 * t)

        def evac3b(e, t=t, pb=pb):
            for n in range(4):
                bt, bo = pb[n]
                i = e.tensor_add(out=s_yst[:, n * 512:(n + 1) * 512],
                                 in0=bt[:, bo:bo + 512],
                                 in1=s_xf[t % 2][:, n * 512:(n + 1) * 512])
            return i
        EI("vector", vOU[0], evac3b)
        W("sync", vOU[0], vOU[0].n)
        EI("sync", dY, lambda e, t=t: e.dma_start(
            out=y[t * 128:(t + 1) * 128, :], in_=s_yst[:]), 16)
    W("sync", dY, 16 * NT)
    return _finish()


def _consts():
    inv = 1.0 / (10000.0 ** (np.arange(0, HD, 2, dtype=np.float32) / HD))
    t = np.arange(S, dtype=np.float32)
    f = np.concatenate([inv, inv])
    ang = f[:, None] * t[None, :]
    cosT = np.cos(ang).astype(bfnp)
    sins = np.sin(ang).astype(np.float32)
    sins[:HD // 2] *= -1.0
    sinsT = sins.astype(bfnp)
    jj = np.arange(128)
    mcT = (jj[:, None] <= jj[None, :]).astype(bfnp)
    mfT = (jj[:, None] >= jj[None, :]).astype(bfnp)
    return dict(
        cosT_c=cosT, sinsT_c=sinsT, mcT_c=mcT, mfT_c=mfT,
        ident_c=np.eye(128, dtype=bfnp),
        onescb_c=np.ones((128, 1), dtype=bfnp),
        onesrb_c=np.ones((1, 128), dtype=bfnp),
        onesrf_c=np.ones((1, 128), dtype=np.float32),
        onescf_c=np.ones((128, 1), dtype=np.float32),
        epsc_c=np.full((128, 1), 1e-5, dtype=np.float32),
    )


def kernel(L0_attn_hidden_states, L0_attn_w_q, L0_attn_w_k, L0_attn_w_v,
           L0_attn_w_o, L0_ffn_w_fc, L0_ffn_w_proj):
    x = np.ascontiguousarray(np.asarray(L0_attn_hidden_states, dtype=np.float32))
    base = dict(
        wq=np.ascontiguousarray(L0_attn_w_q, dtype=np.float32),
        wk=np.ascontiguousarray(L0_attn_w_k, dtype=np.float32),
        wv=np.ascontiguousarray(L0_attn_w_v, dtype=np.float32),
        wo=np.ascontiguousarray(L0_attn_w_o, dtype=np.float32),
        wfc=np.ascontiguousarray(L0_ffn_w_fc, dtype=np.float32),
        wpj=np.ascontiguousarray(L0_ffn_w_proj, dtype=np.float32),
        **_consts(),
    )
    nc = build_nc()
    in_maps = [dict(xin=np.ascontiguousarray(x[b]), **base) for b in range(8)]
    res = run_bass_kernel_spmd(nc, in_maps, core_ids=list(range(8)))
    out = np.stack([np.asarray(r["y"], dtype=np.float32) for r in res.results])
    return out


# version 10 (final)
# speedup vs baseline: 1.0975x; 1.0230x over previous
import math
import os
import numpy as np
import ml_dtypes
import concourse.bass as bass
import concourse.mybir as mybir
import concourse.bass_utils as _bu
from concourse.bass_utils import run_bass_kernel_spmd

import os as _os
if _os.environ.get("KLDW", "1") == "1" and not getattr(_bu, "_ldwopt_patched", False):
    _bu._ldwopt_patched = True
    _orig_run_command = _bu.run_command

    def _run_command_ldwopt(cmd, **kw):
        cmd = ["--enable-ldw-opt=true" if c == "--enable-ldw-opt=false" else c
               for c in cmd]
        return _orig_run_command(cmd, **kw)

    _bu.run_command = _run_command_ldwopt

F32 = mybir.dt.float32
BF16 = mybir.dt.bfloat16
U8 = mybir.dt.uint8
AX = mybir.AxisListType.X
AF = mybir.ActivationFunctionType
ALU = mybir.AluOpType
bfnp = ml_dtypes.bfloat16

S, D, NH, NKV, HD, WIN, HF = 4096, 2048, 16, 4, 128, 1024, 5120
NT, SS, NSS = 32, 512, 8
NKC = D // 128
NHC = HF // 128
ISQ = 1.0 / math.sqrt(HD)
GQ = NH // NKV

ENGS = ("sync", "vector", "scalar", "tensor", "gpsimd")


class Ctr:
    __slots__ = ("h", "n")

    def __init__(self, h):
        self.h = h
        self.n = 0

    def inc(self, k=1):
        self.n += k
        return self.n


def build_nc():
    nc = bass.Bass("TRN2", target_bir_lowering=False,
                   detect_race_conditions=False)
    din = lambda n, s, d=F32: nc.dram_tensor(n, s, d, kind="ExternalInput").ap()
    _dbg = os.environ.get("KDBG") == "1"
    dint = lambda n, s, d: nc.dram_tensor(
        n, s, d, kind="ExternalOutput" if _dbg else "Internal").ap()

    xin = din("xin", [S, D])
    wq = din("wq", [D, D])
    wk = din("wk", [D, NKV * HD])
    wv = din("wv", [D, NKV * HD])
    wo = din("wo", [D, D])
    wfc = din("wfc", [HF, D])
    wpj = din("wpj", [D, HF])
    cosT_c = din("cosT_c", [128, S], BF16)
    sinsT_c = din("sinsT_c", [128, S], BF16)
    mcT_c = din("mcT_c", [128, 128], BF16)
    mfT_c = din("mfT_c", [128, 128], BF16)
    ident_c = din("ident_c", [128, 128], BF16)
    onesrb_c = din("onesrb_c", [1, 128], BF16)
    onesrf_c = din("onesrf_c", [1, 128], F32)
    onescf_c = din("onescf_c", [128, 1], F32)
    onescb_c = din("onescb_c", [128, 1], BF16)
    epsc_c = din("epsc_c", [128, 1], F32)
    y = nc.dram_tensor("y", [S, D], F32, kind="ExternalOutput").ap()

    xnT_hbm = dint("xnT_hbm", [NSS, 128, NKC * SS], BF16)
    ctxT_hbm = dint("ctxT_hbm", [NSS, 128, NH * SS], BF16)
    x1_hbm = dint("x1_hbm", [S, D], F32)
    h_hbm = dint("h_hbm", [NSS, 128, NHC * SS], BF16)
    sfc_hbm = dint("sfc_hbm", [HF, D], BF16)
    sp_hbm = dint("sp_hbm", [D, HF], BF16)

    from contextlib import ExitStack
    es = ExitStack()

    ARENA = 211968
    arena = es.enter_context(nc.sbuf_tensor("arena", [128, ARENA], U8))
    arena_addr = nc.lookup_mloc(arena).addr

    def place(name, shape, dtype, off):
        assert off % 32 == 0, name
        per = int(np.prod(shape[1:])) * mybir.dt.size(dtype)
        assert off + per <= ARENA, (name, off, per)
        h = nc.virtual_sbuf_tensor(name, list(shape), dtype)
        nc._tensor(h.name, h.shape, h.dtype, type="SB")
        mloc = nc.lookup_mloc(h)
        mloc.allocated = True
        mloc.addr = arena_addr + off
        return h

    w0t = [place(f"w0t{i}", [128, 2048], F32, 143360 + i * 8192) for i in range(2)]
    s0t = [place(f"s0t{i}", [128, 2048], BF16, 159744 + i * 4096) for i in range(2)]
    acc0 = place("acc0", [128, 160], F32, 167936)
    s_xf = [place(f"s_xf{i}", [128, D], F32, i * 8192) for i in range(2)]
    s_mrow = place("s_mrow", [1, 2048], BF16, 0)
    _pc_offs = [4096, 5120, 6144, 7168, 15360, 204320, 205344, 206368, 207392,
                208416, 209440, 210464]
    s_Pc = [place(f"s_Pc{i}", [128, 512], BF16, o) for i, o in enumerate(_pc_offs)]
    s_mcols = [place(f"s_mcols{i}", [128, 32], BF16, 8192 + i * 64) for i in range(2)]
    s_negrow32 = place("s_negrow32", [1, 32], BF16, 8320)
    s_negrow = place("s_negrow", [1, 16], BF16, 8384)
    s_mbias = [place(f"s_mbias{i}", [128, 16], F32, 8448 + i * 64) for i in range(2)]
    s_lnr = [place(f"s_lnr{i}", [128, 1], F32, 8576 + i * 32) for i in range(2)]
    s_r = [place(f"s_r{i}", [128, 1], F32, 8640 + i * 32) for i in range(2)]
    s_qtmp = [place(f"s_qtmp{i}", [128, SS], BF16, 9216 + i * 1024) for i in range(2)]
    s_qshf = [place(f"s_qshf{i}", [128, SS], BF16, 11264 + i * 1024) for i in range(2)]
    s_t1 = place("s_t1", [128, SS], BF16, 13312)
    s_t2 = place("s_t2", [128, SS], BF16, 14336)
    s_wk = place("s_wk", [128, NKC * 512], BF16, 16384)
    s_wv = place("s_wv", [128, NKC * 512], BF16, 32768)
    s_QT = place("s_QT", [128, NH * SS], BF16, 16384)
    s_xnT = place("s_xnT", [128, NKC * SS], BF16, 32768)
    s_ctxT = place("s_ctxT", [128, NH * SS], BF16, 16384)
    s_K = place("s_K", [128, NKV * S], BF16, 49152)
    s_V = place("s_V", [128, NT * NKV * 128], BF16, 81920)
    s_stage = [place(f"s_stage{i}", [128, D], F32, 49152 + i * 8192) for i in range(2)]
    s_ht = [place(f"s_ht{i}", [128, NHC * 128], BF16, 180736 + i * 10240) for i in range(2)]
    s_yst = place("s_yst", [128, D], F32, 201216)
    s_wq = place("s_wq", [128, NKC * D], BF16, 115200)
    s_wo = place("s_wo", [128, NKC * D], BF16, 115200)
    s_xnsq1 = place("s_xnsq1", [128, D], BF16, 115200)
    s_st1 = place("s_st1", [128, 8], F32, 119296)
    s_ktmp = [place(f"s_ktmp{i}", [128, SS], BF16, 119424 + i * 1024) for i in range(2)]
    s_kshf = [place(f"s_kshf{i}", [128, SS], BF16, 121472 + i * 1024) for i in range(2)]
    s_kt1 = place("s_kt1", [128, SS], BF16, 123520)
    s_kt2 = place("s_kt2", [128, SS], BF16, 124544)
    s_xnT1 = place("s_xnT1", [128, NKC * SS], BF16, 126976)
    s_SfcT = place("s_SfcT", [128, NKC * HF], BF16, 16384)
    s_SpT = place("s_SpT", [128, NHC * D], BF16, 16384)
    s_cosT = place("s_cosT", [128, S], BF16, 180736)
    s_sinsT = place("s_sinsT", [128, S], BF16, 188928)
    s_xn2T = place("s_xn2T", [128, NKC * SS], BF16, 180736)
    s_id = place("s_id", [128, 128], BF16, 197120)
    s_mcT = place("s_mcT", [128, 128], BF16, 197376)
    s_mfT = place("s_mfT", [128, 128], BF16, 197632)
    s_onesrb = place("s_onesrb", [1, 128], BF16, 197888)
    s_onesrf = place("s_onesrf", [1, 128], F32, 198144)
    s_onescf = place("s_onescf", [128, 1], F32, 198656)
    s_eps = place("s_eps", [128, 1], F32, 198688)
    sc0 = place("sc0", [1, 16], F32, 198720)
    scB = place("scB", [128, 8], F32, 198784)
    s_ctx1 = [place(f"s_ctx1{i}", [128, 512], BF16, 198912 + i * 1024) for i in range(2)]
    s_rb = [place(f"s_rb{i}", [128, 512], BF16, 201216 + i * 1024) for i in range(2)]
    s_rrow = place("s_rrow", [1, 512], BF16, 203264)
    s_onescb = place("s_onescb", [128, 1], BF16, 204288)
    s_hst = [place(f"s_hst{i}", [128, 2048], BF16, 198912 + i * 4096) for i in range(2)]
    s_xnsq3 = place("s_xnsq3", [128, D], BF16, 207104)
    s_st3 = place("s_st3", [128, 8], F32, 211232)

    pSC = [es.enter_context(nc.psum_tensor(f"pSC{i}", [128, 1024], F32)) for i in range(2)]
    pCX = [es.enter_context(nc.psum_tensor(f"pCX{i}", [128, 512], F32)) for i in range(2)]
    pLNB = es.enter_context(nc.psum_tensor("pLNB", [128, 1024], BF16))
    pAUX = es.enter_context(nc.psum_tensor("pAUX", [128, 512], F32))

    sems = {}
    ops = {e: [] for e in ENGS}

    def C(name):
        if name not in sems:
            sems[name] = Ctr(es.enter_context(nc.semaphore(name)))
        return sems[name]

    def E(eng, fn):
        ops[eng].append(fn)

    def W(eng, ctr, val):
        if val <= 0:
            return
        E(eng, lambda e, s=ctr.h, v=val: e.wait_ge(s, v))

    def EI(eng, ctr, fn, k=1):
        ctr.inc(k)
        E(eng, lambda e, f=fn, s=ctr.h, kk=k: f(e).then_inc(s, kk))

    class Res:

        def __init__(self):
            self.reqs = []

        def acquire(self, eng):
            for c, v in self.reqs:
                W(eng, c, v)
            self.reqs = []

        def release(self, ctr):
            self.reqs.append((ctr, ctr.n))

    rSC = [Res(), Res()]
    rCX = [Res(), Res()]
    rLNB = Res()
    rAUX = Res()

    FEN = C("FEN")
    fence_round = [0]

    def fence():
        fence_round[0] += 1
        tgt = 5 * fence_round[0]
        for eng in ENGS:
            E(eng, lambda e, s=FEN.h: e.nop().then_inc(s, 1))
        FEN.inc(5)
        for eng in ENGS:
            W(eng, FEN, tgt)

    _PH = int(os.environ.get("KPH", "9"))

    def _finish():
        with nc.Block() as block:
            @block.sync
            def _(e):
                for f in ops["sync"]:
                    f(e)

            @block.vector
            def _(e):
                for f in ops["vector"]:
                    f(e)

            @block.scalar
            def _(e):
                for f in ops["scalar"]:
                    f(e)

            @block.tensor
            def _(e):
                for f in ops["tensor"]:
                    f(e)

            @block.gpsimd
            def _(e):
                for f in ops["gpsimd"]:
                    f(e)
        es.close()
        return nc

    dW = C("dW")
    for dst, src in [(s_cosT, cosT_c), (s_sinsT, sinsT_c), (s_mcT, mcT_c),
                     (s_mfT, mfT_c), (s_id, ident_c), (s_onesrb, onesrb_c),
                     (s_onesrf, onesrf_c), (s_onescf, onescf_c), (s_eps, epsc_c),
                     (s_onescb, onescb_c)]:
        EI("sync", dW, lambda e, d=dst, s=src: e.dma_start(out=d[:], in_=s), 16)
    NCONSTW = dW.n

    dW0 = [C("dW0a"), C("dW0b")]
    vAB = C("vAB")
    tPS = C("tPS")
    gTN = C("gTN")
    dS = [C("dSa"), C("dSb")]
    aC = C("aC")

    mats = [("fc", wfc, sfc_hbm, HF // 128, 1, 2048, HF * D, 0, 6),
            ("pj", wpj, sp_hbm, D // 128, 4, 1280, HF * D, 2, 7)]

    def p0_gen():
     for (mn, wsrc, sdst, nrt, ncc, cw, nel, cthr, cabs) in mats:
          ntile = nrt * ncc
          base_vab = vAB.n
          ti = 0
          for rt in range(nrt):
              for cc in range(ncc):
                  j = ti % 2
                  lq = "sync" if j == 0 else "scalar"
                  if ti >= 2:
                      W(lq, vAB, base_vab + ti - 1)
                  else:
                      W(lq, gTN, gTN.n)
                  EI(lq, dW0[j], lambda e, j=j, rt=rt, cc=cc, w=wsrc, cw=cw:
                     e.dma_start(out=w0t[j][:, 0:cw], in_=w[rt * 128:(rt + 1) * 128,
                                                           cc * cw:(cc + 1) * cw]), 16)
                  W("vector", dW0[j], dW0[j].n)
                  EI("vector", vAB, lambda e, j=j, ti=ti, cw=cw:
                     e.tensor_reduce(out=acc0[:, ti:ti + 1], in_=w0t[j][:, 0:cw],
                                     axis=AX, op=ALU.add, apply_absolute_value=True))
                  ti += 1
                  yield
          W("vector", vAB, vAB.n)
          EI("vector", vAB, lambda e, nt=ntile:
             e.tensor_reduce(out=acc0[:, 159:160], in_=acc0[:, 0:nt], axis=AX,
                             op=ALU.add))
          W("tensor", vAB, vAB.n)
          W("tensor", dW, NCONSTW)
          rAUX.acquire("tensor")
          EI("tensor", tPS, lambda e:
             e.matmul(out=pAUX[0:1, 20:21], lhsT=acc0[:, 159:160], rhs=s_onescf[:],
                      start=True, stop=True))
          W("vector", tPS, tPS.n)

          steps = [
              lambda e: e.tensor_copy(out=sc0[0:1, 0:1], in_=pAUX[0:1, 20:21]),
              lambda e, nel=nel: e.tensor_scalar(out=sc0[0:1, 1:2], in0=sc0[0:1, 0:1],
                                                 scalar1=1.0 / nel, scalar2=None,
                                                 op0=ALU.mult),
              lambda e: e.tensor_scalar(out=sc0[0:1, 2:3], in0=sc0[0:1, 1:2],
                                        scalar1=1e-5, scalar2=None, op0=ALU.max),
              lambda e: e.tensor_scalar(out=sc0[0:1, 3:4], in0=sc0[0:1, 2:3],
                                        scalar1=0.7, scalar2=None, op0=ALU.mult),
              lambda e: e.tensor_scalar(out=sc0[0:1, 4:5], in0=sc0[0:1, 3:4],
                                        scalar1=-1.0, scalar2=None, op0=ALU.mult),
              lambda e, cabs=cabs: e.tensor_copy(out=sc0[0:1, cabs:cabs + 1],
                                                 in_=sc0[0:1, 2:3]),
          ]
          for stp in steps:
              EI("vector", vAB, stp)
              W("vector", vAB, vAB.n)
          W("tensor", vAB, vAB.n)
          rAUX.acquire("tensor")
          EI("tensor", tPS, lambda e:
             e.matmul(out=pAUX[:, 21:22], lhsT=s_onesrf[:], rhs=sc0[0:1, 3:4],
                      start=True, stop=True))
          EI("tensor", tPS, lambda e:
             e.matmul(out=pAUX[:, 22:23], lhsT=s_onesrf[:], rhs=sc0[0:1, 4:5],
                      start=True, stop=True))
          W("vector", tPS, tPS.n)

          def evac_thr(e, cthr=cthr):
              e.tensor_copy(out=scB[:, cthr:cthr + 1], in_=pAUX[:, 21:22])
              return e.tensor_copy(out=scB[:, cthr + 1:cthr + 2], in_=pAUX[:, 22:23])
          EI("vector", vAB, evac_thr)
          rAUX.release(vAB)
          base_gtn = gTN.n
          base_ds = [dS[0].n, dS[1].n]
          vab_thr = vAB.n
          ti = 0
          for rt in range(nrt):
              for cc in range(ncc):
                  j = ti % 2
                  lq = "sync" if j == 0 else "scalar"
                  if ti >= 2:
                      W(lq, gTN, base_gtn + 3 * (ti - 1))
                  else:
                      W(lq, vAB, vab_thr)
                  EI(lq, dW0[j], lambda e, j=j, rt=rt, cc=cc, w=wsrc, cw=cw:
                     e.dma_start(out=w0t[j][:, 0:cw], in_=w[rt * 128:(rt + 1) * 128,
                                                           cc * cw:(cc + 1) * cw]), 16)
                  W("vector", dW0[j], dW0[j].n)
                  if ti >= 2:
                      W("vector", dS[j], base_ds[j] + 16 * (ti // 2))

                  EI("vector", gTN, lambda e, j=j, c=cthr, cw=cw:
                     e.tensor_scalar(out=s0t[j][:, 0:cw], in0=w0t[j][:, 0:cw],
                                     scalar1=scB[:, c:c + 1], scalar2=None,
                                     op0=ALU.is_gt))
                  EI("vector", gTN, lambda e, j=j, c=cthr, cw=cw:
                     e.tensor_scalar(out=w0t[j][:, 0:cw], in0=w0t[j][:, 0:cw],
                                     scalar1=scB[:, c + 1:c + 2], scalar2=None,
                                     op0=ALU.is_lt))
                  W("vector", gTN, gTN.n)
                  EI("vector", gTN, lambda e, j=j, cw=cw:
                     e.tensor_sub(out=s0t[j][:, 0:cw], in0=s0t[j][:, 0:cw],
                                  in1=w0t[j][:, 0:cw]))
                  W("gpsimd", gTN, gTN.n)
                  EI("gpsimd", dS[j], lambda e, j=j, rt=rt, cc=cc, sd=sdst, cw=cw:
                     e.dma_start(out=sd[rt * 128:(rt + 1) * 128,
                                        cc * cw:(cc + 1) * cw], in_=s0t[j][:, 0:cw]), 16)
                  ti += 1
                  yield

     EI("vector", vAB, lambda e: e.tensor_mul(out=sc0[0:1, 8:9], in0=sc0[0:1, 6:7],
                                              in1=sc0[0:1, 6:7]))
     W("vector", vAB, vAB.n)
     EI("vector", vAB, lambda e: e.tensor_mul(out=sc0[0:1, 8:9], in0=sc0[0:1, 8:9],
                                              in1=sc0[0:1, 7:8]))
     W("scalar", vAB, vAB.n)

     EI("scalar", aC, lambda e: e.activation(out=sc0[0:1, 9:10], in_=sc0[0:1, 8:9],
                                             func=AF.Ln))
     W("scalar", aC, aC.n)
     EI("scalar", aC, lambda e: e.activation(out=sc0[0:1, 10:11],
                                             in_=sc0[0:1, 9:10], func=AF.Exp,
                                             scale=0.5))
     W("tensor", aC, aC.n)
     rAUX.acquire("tensor")
     EI("tensor", tPS, lambda e:
        e.matmul(out=pAUX[:, 23:24], lhsT=s_onesrf[:], rhs=sc0[0:1, 10:11],
                 start=True, stop=True))
     W("vector", tPS, tPS.n)
     EI("vector", vAB, lambda e: e.tensor_copy(out=scB[:, 4:5], in_=pAUX[:, 23:24]))
     rAUX.release(vAB)

     if _dbg:
         o_scB = nc.dram_tensor("o_scB", [128, 8], F32, kind="ExternalOutput").ap()
         o_sc0 = nc.dram_tensor("o_sc0", [1, 16], F32, kind="ExternalOutput").ap()
         dDBG = C("dDBG")
         W("sync", vAB, vAB.n)
         EI("sync", dDBG, lambda e: e.dma_start(out=o_scB, in_=scB[:]), 16)
         EI("sync", dDBG, lambda e: e.dma_start(out=o_sc0, in_=sc0[:]), 16)
         W("sync", dDBG, 32)

     yield

    _p0 = p0_gen()
    _OVL = int(os.environ.get("KOVL", "1"))

    def p0_pump(n):
        for _ in range(n):
            if next(_p0, "done") == "done":
                break

    if not _OVL:
        for _ in _p0:
            pass

    if _PH <= 1:
        for _ in _p0:
            pass
        return _finish()

    dX = [C("dXa"), C("dXb")]
    aSQ = C("aSQ")
    vRED = C("vRED")
    vST = C("vST")
    aRS = C("aRS")
    vXN = C("vXN")
    tTR = C("tTR")
    vTE = C("vTE")

    def emit_ln_tile(src_hbm, tt, xnsq, stats, xnT_dst, dve_prewait=None):
        j = tt % 2
        if vXN.n >= 2:
            W("sync", vXN, vXN.n - 1)
        EI("sync", dX[j], lambda e, j=j, tt=tt, src=src_hbm:
           e.dma_start(out=s_xf[j][:], in_=src[tt * 128:(tt + 1) * 128, :]), 16)
        ldx = dX[j].n
        W("scalar", dX[j], ldx)
        W("scalar", tTR, tTR.n)
        EI("scalar", aSQ, lambda e, j=j, xnsq=xnsq, stats=stats:
           e.activation(out=xnsq[:], in_=s_xf[j][:], func=AF.Square,
                        accum_out=stats[:, 1:2]))
        W("vector", dX[j], ldx)
        EI("vector", vRED, lambda e, j=j, stats=stats:
           e.tensor_reduce(out=stats[:, 0:1], in_=s_xf[j][:], axis=AX, op=ALU.add))
        W("vector", vRED, vRED.n)
        W("vector", aSQ, aSQ.n)

        EI("vector", vST, lambda e, stats=stats:
           e.tensor_scalar(out=stats[:, 2:3], in0=stats[:, 0:1],
                           scalar1=-1.0 / D, scalar2=None, op0=ALU.mult))
        EI("vector", vST, lambda e, stats=stats:
           e.tensor_scalar(out=stats[:, 3:4], in0=stats[:, 1:2],
                           scalar1=1.0 / D, scalar2=None, op0=ALU.mult))
        W("vector", vST, vST.n)
        EI("vector", vST, lambda e, stats=stats:
           e.scalar_tensor_tensor(out=stats[:, 4:5], in0=stats[:, 2:3],
                                  scalar=stats[:, 2:3], in1=stats[:, 3:4],
                                  op0=ALU.mult, op1=ALU.subtract))
        W("scalar", vST, vST.n)

        def rstd_chain(e, stats=stats):
            e.activation(out=stats[:, 5:6], in_=stats[:, 4:5], func=AF.Ln,
                         bias=s_eps[:], scale=-1.0)
            return e.activation(out=stats[:, 6:7], in_=stats[:, 5:6], func=AF.Exp,
                                scale=-0.5)
        EI("scalar", aRS, rstd_chain)
        W("vector", aRS, aRS.n)
        if dve_prewait is not None:
            dve_prewait()
        EI("vector", vXN, lambda e, j=j, xnsq=xnsq, stats=stats:
           e.tensor_scalar(out=xnsq[:], in0=s_xf[j][:], scalar1=stats[:, 2:3],
                           scalar2=stats[:, 6:7], op0=ALU.add, op1=ALU.mult))
        r = tt % 4
        W("tensor", vXN, vXN.n)
        for f in range(2):
            rLNB.acquire("tensor")
            for kk in range(8):
                k = 8 * f + kk
                fn = lambda e, k=k, kk=kk, xnsq=xnsq: e.transpose(
                    out=pLNB[:, kk * 128:(kk + 1) * 128],
                    in_=xnsq[:, k * 128:(k + 1) * 128], identity=s_id[:])
                if kk == 7:
                    EI("tensor", tTR, fn)
                else:
                    E("tensor", fn)
            W("vector", tTR, tTR.n)

            def evac(e, f=f, r=r, dst=xnT_dst):
                o = dst[:, :].rearrange("p (k q) -> p k q", k=NKC, q=SS)
                o = o[:, 8 * f:8 * f + 8, r * 128:(r + 1) * 128]
                return e.tensor_copy(
                    out=o, in_=pLNB[:, 0:1024].rearrange("p (k q) -> p k q",
                                                         k=8, q=128))
            EI("vector", vTE, evac)
            rLNB.release(vTE)

    dWG = C("dWG")
    W("gpsimd", dS[0], dS[0].n)
    W("gpsimd", dS[1], dS[1].n)
    for k in range(NKC):
        EI("gpsimd", dWG, lambda e, k=k:
           e.dma_start(out=s_wk[:, k * 512:(k + 1) * 512],
                       in_=wk[k * 128:(k + 1) * 128, :]), 16)
    for k in range(NKC):
        EI("gpsimd", dWG, lambda e, k=k:
           e.dma_start(out=s_wv[:, k * 512:(k + 1) * 512],
                       in_=wv[k * 128:(k + 1) * 128, :]), 16)
    NWKV = dWG.n

    dXN_ST = C("dXN_ST")
    tKV = C("tKV")
    aKE = C("aKE")
    dKS = C("dKS")
    vKR = C("vKR")
    aVE = C("aVE")

    for s in range(NSS):
        for r in range(4):
            tt = 4 * s + r

            def dve_pre():
                if s >= 1:
                    W("vector", dXN_ST, 16 * s)
                    W("vector", tKV, 8 * s)
            emit_ln_tile(xin, tt, s_xnsq1, s_st1, s_xnT1,
                         dve_prewait=dve_pre if r == 0 else None)
            p0_pump(4)
        W("sync", vTE, vTE.n)
        EI("sync", dXN_ST, lambda e, s=s:
           e.dma_start(out=xnT_hbm[s], in_=s_xnT1[:]), 16)
        W("tensor", vTE, vTE.n)
        W("tensor", dWG, NWKV)
        for kvh in range(NKV):
            p = kvh % 2
            rSC[p].acquire("tensor")
            for k in range(NKC):
                fn = lambda e, k=k, kvh=kvh, p=p: e.matmul(
                    out=pSC[p][:, 0:512],
                    lhsT=s_wk[:, k * 512 + kvh * 128:k * 512 + (kvh + 1) * 128],
                    rhs=s_xnT1[:, k * SS:(k + 1) * SS],
                    start=(k == 0), stop=(k == NKC - 1))
                if k == NKC - 1:
                    EI("tensor", tKV, fn)
                else:
                    E("tensor", fn)
            j = kvh % 2
            W("scalar", tKV, tKV.n)
            if vKR.n >= 2:
                W("scalar", vKR, vKR.n - 1)
            EI("scalar", aKE, lambda e, p=p, j=j:
               e.copy(out=s_ktmp[j][:], in_=pSC[p][:, 0:512]))
            rSC[p].release(aKE)
            W("sync", aKE, aKE.n)
            if vKR.n >= 2:
                W("sync", vKR, vKR.n - 1)
            E("sync", lambda e, j=j, sh=dKS.h: (
                e.dma_start(out=s_kshf[j][0:64, :],
                            in_=s_ktmp[j][64:128, :]).then_inc(sh, 16),
                e.dma_start(out=s_kshf[j][64:128, :],
                            in_=s_ktmp[j][0:64, :]).then_inc(sh, 16)))
            dKS.inc(32)
            W("vector", aKE, aKE.n)
            W("vector", dKS, dKS.n)

            def rope_k(e, j=j, s=s, kvh=kvh):
                e.tensor_mul(out=s_kt1[:], in0=s_ktmp[j][:],
                             in1=s_cosT[:, s * SS:(s + 1) * SS])
                e.tensor_mul(out=s_kt2[:], in0=s_kshf[j][:],
                             in1=s_sinsT[:, s * SS:(s + 1) * SS])
                return e.tensor_add(
                    out=s_K[:, kvh * S + s * SS:kvh * S + (s + 1) * SS],
                    in0=s_kt1[:], in1=s_kt2[:])
            EI("vector", vKR, rope_k)
        for tb in range(4):
            p = tb % 2
            cpos = 4 * s + tb
            rSC[p].acquire("tensor")
            for k in range(NKC):
                fn = lambda e, k=k, tb=tb, p=p: e.matmul(
                    out=pSC[p][:, 0:512],
                    lhsT=s_xnT1[:, k * SS + tb * 128:k * SS + (tb + 1) * 128],
                    rhs=s_wv[:, k * 512:(k + 1) * 512],
                    start=(k == 0), stop=(k == NKC - 1))
                if k == NKC - 1:
                    EI("tensor", tKV, fn)
                else:
                    E("tensor", fn)
            W("scalar", tKV, tKV.n)

            def evac_v(e, p=p, cpos=cpos):
                return e.copy(out=s_V[:, cpos * 512:(cpos + 1) * 512],
                              in_=pSC[p][:, 0:512])
            EI("scalar", aVE, evac_v)
            rSC[p].release(aVE)
        p0_pump(10)

    for _ in _p0:
        pass

    fence()
    if _PH <= 2:
        return _finish()

    W("gpsimd", dS[0], dS[0].n)
    W("gpsimd", dS[1], dS[1].n)
    for k in range(NKC):
        EI("gpsimd", dWG, lambda e, k=k:
           e.dma_start(out=s_wq[:, k * D:(k + 1) * D],
                       in_=wq[k * 128:(k + 1) * 128, :]), 16)
    NWQ = dWG.n
    vMS = C("vMS")
    EI("vector", vMS, lambda e: (e.memset(s_mbias[0][:], 0.0),
                                 e.memset(s_mbias[1][:], 0.0),
                                 *[e.memset(s_Pc[i][:], 0.0) for i in range(12)])[-1])

    dXN_LD = C("dXN_LD")
    tQP = C("tQP")
    aQE = C("aQE")
    dQS = C("dQS")
    vQR = C("vQR")
    tSC = [C("tSC0"), C("tSC1")]
    vMX = [C("vMX0"), C("vMX1")]
    aEX = [C("aEX0"), C("aEX1")]
    gMK = [C("gMK0"), C("gMK1")]
    tPV = [C("tPV0"), C("tPV1")]
    aRC = [C("aRC0"), C("aRC1")]
    vCX = [C("vCX0"), C("vCX1")]
    dCTX = C("dCTX")
    dMB = C("dMB")
    vMB = C("vMB")
    tMB = C("tMB")
    aMB = C("aMB")
    vMBE = C("vMBE")

    tSC2 = C("tSC2")
    aEX2 = C("aEX2")
    gMK2 = C("gMK2")
    tPV2 = [C("tPV2a"), C("tPV2b")]
    aRC2 = [C("aRC2a"), C("aRC2b")]
    tRB2 = [C("tRB2a"), C("tRB2b")]
    vRB2 = [C("vRB2a"), C("vRB2b")]
    vCX2 = [C("vCX2a"), C("vCX2b")]
    vMX2 = C("vMX2")
    dCTX2 = C("dCTX2")
    dMB2 = C("dMB2")
    vMB2 = C("vMB2")
    aMB2 = C("aMB2")
    tMB2 = C("tMB2")
    vMBE2 = C("vMBE2")
    rSCB = [Res(), Res()]
    rCXB = [Res(), Res()]
    rSMB = [Res(), Res()]
    sp_guard = {i: [] for i in range(12)}
    scb = [lambda a, b: pSC[0][:, a:b], lambda a, b: pSC[0][:, 512 + a:512 + b]]
    ctxb = [lambda a, b: pSC[1][:, a:b], lambda a, b: pSC[1][:, 512 + a:512 + b]]
    sumb = [lambda a, b: pCX[0][0:1, a:b], lambda a, b: pCX[1][0:1, a:b]]
    rbb = [lambda a, b: pCX[0][:, a:b], lambda a, b: pCX[1][:, a:b]]

    def emit_group(s, h):
        g = 16 * s + h
        pg = g % 2
        kv = h // GQ
        c0 = max(0, 4 * s - 8)
        chunks = list(range(c0, 4 * s + 4))
        geo = []
        for c in chunks:
            rel = c - 4 * s
            q0 = max(0, rel * 128)
            q1 = min(512, rel * 128 + 1152)
            tri = None
            if 0 <= rel <= 3:
                tri = ("c", rel * 128)
            elif -8 <= rel <= -5:
                tri = ("f", rel * 128 + 1024)
            geo.append((c, q0, q1, tri))
        geo.sort(key=lambda x: 0 if x[0] == 4 * s else 1)
        nch = len(geo)
        exp_marks = []

        def emit_chunk_scores(ci):
            c, q0, q1, tri = geo[ci]
            bk = ci % 2
            rSCB[bk].acquire("tensor")
            if ci == 0:
                W("tensor", vQR, 16 * s + h + 1)
            EI("tensor", tSC2, lambda e, c=c, q0=q0, q1=q1, bk=bk:
               e.matmul(out=scb[bk](q0, q1),
                        lhsT=s_K[:, kv * S + c * 128:kv * S + (c + 1) * 128],
                        rhs=s_QT[:, h * SS + q0:h * SS + q1],
                        start=True, stop=True))
            W("scalar", tSC2, tSC2.n)
            if s >= 1:
                W("scalar", vMBE2, 2 * s)
            else:
                W("scalar", vMS, 1)
            sp = ci
            for cc, vv in sp_guard[sp]:
                W("scalar", cc, vv)
            sp_guard[sp] = []
            EI("scalar", aEX2, lambda e, sp=sp, q0=q0, q1=q1, bk=bk:
               e.activation(out=s_Pc[sp][:, q0:q1], in_=scb[bk](q0, q1),
                            func=AF.Exp, bias=s_mbias[s % 2][:, h:h + 1],
                            scale=ISQ))
            rSCB[bk].release(aEX2)
            if tri is not None:
                W("gpsimd", aEX2, aEX2.n)
                kind, t0c = tri
                mk = s_mcT if kind == "c" else s_mfT
                EI("gpsimd", gMK2, lambda e, sp=sp, t0c=t0c, mk=mk:
                   e.tensor_mul(out=s_Pc[sp][:, t0c:t0c + 128],
                                in0=s_Pc[sp][:, t0c:t0c + 128], in1=mk[:]))
                exp_marks.append((ci, (gMK2, gMK2.n)))
            else:
                exp_marks.append((ci, (aEX2, aEX2.n)))

        for ci in range(nch):
            emit_chunk_scores(ci)
        rCXB[pg].acquire("tensor")
        for ci in range(nch):
            c, q0, q1, tri = geo[ci]
            cc, vv = exp_marks[ci][1]
            W("tensor", cc, vv)
            vb = (c * NKV + kv) * 128
            fn = lambda e, sp=ci, q0=q0, q1=q1, vb=vb, ci=ci: e.matmul(
                out=ctxb[pg](q0, q1),
                lhsT=s_V[:, vb:vb + 128],
                rhs=s_Pc[sp][:, q0:q1],
                start=(ci == 0), stop=(ci == nch - 1))
            if ci == nch - 1:
                EI("tensor", tPV2[pg], fn)
            else:
                E("tensor", fn)
        rSMB[pg].acquire("tensor")
        for ci in range(nch):
            c, q0, q1, tri = geo[ci]
            fn = lambda e, sp=ci, q0=q0, q1=q1, ci=ci: e.matmul(
                out=sumb[pg](q0, q1),
                lhsT=s_onescb[:],
                rhs=s_Pc[sp][:, q0:q1],
                start=(ci == 0), stop=(ci == nch - 1))
            if ci == nch - 1:
                EI("tensor", tPV2[pg], fn)
            else:
                E("tensor", fn)
            sp_guard[ci].append((tPV2[pg], tPV2[pg].n))
        W("vector", exp_marks[-1][1][0], exp_marks[-1][1][1])
        if s >= 2:
            W("vector", dMB2, 16 * (s - 1))
        EI("vector", vMX2, lambda e, sp=(nch - 1) % 3:
           e.reduce_max(out=s_mcols[s % 2][:, h:h + 1], in_=s_Pc[sp][:, 0:512],
                        axis=AX))
        sp_guard[(nch - 1) % 3].append((vMX2, vMX2.n))
        W("scalar", tPV2[pg], tPV2[pg].n)
        W("scalar", tRB2[0], tRB2[0].n)
        W("scalar", tRB2[1], tRB2[1].n)
        EI("scalar", aRC2[pg], lambda e:
           e.activation(out=s_rrow[:], in_=sumb[pg](0, 512), func=AF.Ln))
        W("scalar", aRC2[pg], aRC2[pg].n)
        EI("scalar", aRC2[pg], lambda e:
           e.activation(out=s_rrow[:], in_=s_rrow[:], func=AF.Exp, scale=-1.0))
        W("tensor", aRC2[pg], aRC2[pg].n)
        EI("tensor", tRB2[pg], lambda e: e.matmul(
            out=rbb[pg](0, 512), lhsT=s_onesrb[:], rhs=s_rrow[0:1, 0:512],
            start=True, stop=True))
        W("vector", tRB2[pg], tRB2[pg].n)
        if g >= 2:
            W("vector", dCTX2, 16 * (g - 1))
        EI("vector", vRB2[pg], lambda e:
           e.tensor_copy(out=s_rb[pg][:], in_=rbb[pg](0, 512)))
        rSMB[pg].release(vRB2[pg])
        EI("vector", vCX2[pg], lambda e:
           e.tensor_mul(out=s_ctx1[g % 2][:], in0=ctxb[pg](0, 512),
                        in1=s_rb[pg][:]))
        rCXB[pg].release(vCX2[pg])
        W("sync", vCX2[pg], vCX2[pg].n)
        EI("sync", dCTX2, lambda e, s=s, h=h, g=g: e.dma_start(
            out=ctxT_hbm[s][:, h * SS:(h + 1) * SS], in_=s_ctx1[g % 2][:]), 16)

    def emit_mbias2(s):
        W("sync", vMX2, vMX2.n)
        EI("sync", dMB2, lambda e, s=s: e.dma_start(
            out=s_mrow[0:1, :].rearrange("o (p c) -> o p c", p=128, c=16),
            in_=s_mcols[s % 2][:, 0:16]), 16)
        W("vector", dMB2, dMB2.n)
        EI("vector", vMB2, lambda e: e.tensor_reduce(
            out=s_negrow[:], in_=s_mrow[0:1, :].rearrange(
                "o (p c) -> o c p", p=128, c=16), axis=AX, op=ALU.max))
        W("scalar", vMB2, vMB2.n)
        EI("scalar", aMB2, lambda e:
           e.activation(out=s_negrow[:], in_=s_negrow[:], func=AF.Ln))
        W("tensor", aMB2, aMB2.n)
        rAUX.acquire("tensor")
        EI("tensor", tMB2, lambda e: e.matmul(
            out=pAUX[:, 0:16], lhsT=s_onesrb[:], rhs=s_negrow[0:1, 0:16],
            start=True, stop=True))
        W("vector", tMB2, tMB2.n)
        W("vector", aEX2, aEX2.n)

        EI("vector", vMBE2, lambda e, s=s:
           e.tensor_copy(out=s_mbias[(s + 1) % 2][:], in_=s_mbias[s % 2][:]))
        W("vector", vMBE2, vMBE2.n)
        EI("vector", vMBE2, lambda e, s=s:
           e.tensor_sub(out=s_mbias[(s + 1) % 2][:],
                        in0=s_mbias[(s + 1) % 2][:], in1=pAUX[:, 0:16]))
        rAUX.release(vMBE2)

    for s in range(NSS):
        if s >= 1:
            W("sync", tQP, 16 * s)
        W("sync", dXN_ST, 16 * (s + 1))
        EI("sync", dXN_LD, lambda e, s=s:
           e.dma_start(out=s_xnT[:], in_=xnT_hbm[s]), 16)
        W("tensor", dXN_LD, dXN_LD.n)
        W("tensor", dWG, NWQ)
        for h in range(NH):
            p = h % 2
            (rSCB[0] if p == 0 else rCXB[0]).acquire("tensor")
            if s > 0 or h >= 2:
                W("tensor", aQE, aQE.n - 1)
            for k in range(NKC):
                fn = lambda e, k=k, h=h, p=p: e.matmul(
                    out=pSC[p][:, 0:512],
                    lhsT=s_wq[:, k * D + h * 128:k * D + (h + 1) * 128],
                    rhs=s_xnT[:, k * SS:(k + 1) * SS],
                    start=(k == 0), stop=(k == NKC - 1))
                if k == NKC - 1:
                    EI("tensor", tQP, fn)
                else:
                    E("tensor", fn)
            j = h % 2
            W("scalar", tQP, tQP.n)
            if vQR.n >= 2:
                W("scalar", vQR, vQR.n - 1)
            EI("scalar", aQE, lambda e, p=p, j=j:
               e.copy(out=s_qtmp[j][:], in_=pSC[p][:, 0:512]))
            (rSCB[0] if p == 0 else rCXB[0]).release(aQE)
            W("sync", aQE, aQE.n)
            if vQR.n >= 2:
                W("sync", vQR, vQR.n - 1)
            E("sync", lambda e, j=j, sh=dQS.h: (
                e.dma_start(out=s_qshf[j][0:64, :],
                            in_=s_qtmp[j][64:128, :]).then_inc(sh, 16),
                e.dma_start(out=s_qshf[j][64:128, :],
                            in_=s_qtmp[j][0:64, :]).then_inc(sh, 16)))
            dQS.inc(32)
            W("vector", aQE, aQE.n)
            W("vector", dQS, dQS.n)
            if h == 0 and s > 0:
                W("vector", tSC2, tSC2.n)

            def rope_q(e, j=j, s=s, h=h):
                e.tensor_mul(out=s_t1[:], in0=s_qtmp[j][:],
                             in1=s_cosT[:, s * SS:(s + 1) * SS])
                e.tensor_mul(out=s_t2[:], in0=s_qshf[j][:],
                             in1=s_sinsT[:, s * SS:(s + 1) * SS])
                return e.tensor_add(out=s_QT[:, h * SS:(h + 1) * SS],
                                    in0=s_t1[:], in1=s_t2[:])
            EI("vector", vQR, rope_q)
        W("tensor", aQE, aQE.n)
        for bk in range(2):
            rSCB[bk].acquire("tensor")
        for h in range(NH):
            emit_group(s, h)
        emit_mbias2(s)

    fence()
    if _PH <= 3:
        return _finish()

    for k in range(NKC):
        EI("gpsimd", dWG, lambda e, k=k:
           e.dma_start(out=s_wo[:, k * D:(k + 1) * D],
                       in_=wo[k * 128:(k + 1) * 128, :]), 16)
    NWO = dWG.n
    dCT_LD = C("dCT_LD")
    dX2 = C("dX2")
    dX1_ST = C("dX1_ST")
    tOU = [C("tOU0"), C("tOU1")]
    vOU = [C("vOU0"), C("vOU1")]
    pc2 = [0]
    base_tou = tOU[0].n
    for s in range(NSS):
        if s >= 1:
            W("sync", tOU[0], base_tou + 4 * s)
        EI("sync", dCT_LD, lambda e, s=s: e.dma_start(
            out=s_ctxT[:], in_=ctxT_hbm[s]), 16)
        for tb in range(4):
            t = 4 * s + tb
            if t >= 2:
                W("sync", vOU[0], vOU[0].n)
            EI("sync", dX2, lambda e, t=t:
               e.dma_start(out=s_xf[t % 2][:], in_=xin[t * 128:(t + 1) * 128, :]), 16)
            W("tensor", dCT_LD, 16 * (s + 1))
            W("tensor", dWG, NWO)
            W("tensor", vOU[0], vOU[0].n)
            pb = [(pSC[0], 0), (pSC[0], 512), (pSC[1], 0), (pSC[1], 512)]
            for h in range(NH):
                for n in range(4):
                    bt, bo = pb[n]
                    fn = lambda e, h=h, tb=tb, n=n, bt=bt, bo=bo: e.matmul(
                        out=bt[:, bo:bo + 512],
                        lhsT=s_ctxT[:, h * SS + tb * 128:h * SS + (tb + 1) * 128],
                        rhs=s_wo[:, h * D + n * 512:h * D + (n + 1) * 512],
                        start=(h == 0), stop=(h == NH - 1))
                    if h == NH - 1 and n == 3:
                        EI("tensor", tOU[0], fn)
                    else:
                        E("tensor", fn)
            W("vector", tOU[0], tOU[0].n)
            W("vector", dX2, 16 * (t + 1))
            if t >= 2:
                W("vector", dX1_ST, 16 * (t - 1))

            def evac2b(e, t=t, pb=pb):
                for n in range(4):
                    bt, bo = pb[n]
                    i = e.tensor_add(out=s_stage[t % 2][:, n * 512:(n + 1) * 512],
                                     in0=bt[:, bo:bo + 512],
                                     in1=s_xf[t % 2][:, n * 512:(n + 1) * 512])
                return i
            EI("vector", vOU[0], evac2b)
            W("sync", vOU[0], vOU[0].n)
            EI("sync", dX1_ST, lambda e, t=t:
               e.dma_start(out=x1_hbm[t * 128:(t + 1) * 128, :],
                           in_=s_stage[t % 2][:]), 16)

    fence()
    if _PH <= 4:
        return _finish()

    dSF = C("dSF")
    for k in range(NKC):
        EI("sync", dSF, lambda e, k=k: e.dma_start(
            out=s_SfcT[:, k * HF:(k + 1) * HF],
            in_=sfc_hbm[:, k * 128:(k + 1) * 128], transpose=True), 16)
    tH = [C("tH0"), C("tH1")]
    aH = [C("aH0"), C("aH1")]
    dH_ST = C("dH_ST")
    for s in range(NSS):
        for r in range(4):
            tt = 4 * s + r

            def dve_pre3():
                if s >= 1:
                    W("vector", tH[0], (NHC * s) // 2)
                    W("vector", tH[1], (NHC * s) // 2)
            emit_ln_tile(x1_hbm, tt, s_xnsq3, s_st3, s_xn2T,
                         dve_prewait=dve_pre3 if r == 0 else None)
        W("tensor", vTE, vTE.n)
        W("tensor", dSF, 16 * NKC)
        for hh in range(NHC):
            p = hh % 2
            rSC[p].acquire("tensor")
            for k in range(NKC):
                fn = lambda e, k=k, hh=hh, p=p: e.matmul(
                    out=pSC[p][:, 0:512],
                    lhsT=s_SfcT[:, k * HF + hh * 128:k * HF + (hh + 1) * 128],
                    rhs=s_xn2T[:, k * SS:(k + 1) * SS],
                    start=(k == 0), stop=(k == NKC - 1))
                if k == NKC - 1:
                    EI("tensor", tH[p], fn)
                else:
                    E("tensor", fn)
            jb = (hh // 4) % 2
            W("scalar", tH[p], tH[p].n)
            bidx = (NHC // 4) * s + hh // 4
            if hh % 4 == 0 and bidx >= 2:
                W("scalar", dH_ST, 16 * (bidx - 1))

            def relu_sq(e, p=p, jb=jb, hh=hh):
                e.activation(out=pAUX[:, 0:512], in_=pSC[p][:, 0:512],
                             func=AF.Relu, scale=scB[:, 4:5])
                return e.activation(
                    out=s_hst[jb][:, (hh % 4) * 512:(hh % 4 + 1) * 512],
                    in_=pAUX[:, 0:512], func=AF.Square)
            EI("scalar", aH[p], relu_sq)
            rSC[p].release(aH[p])
            if hh % 4 == 3:
                W("sync", aH[0], aH[0].n)
                W("sync", aH[1], aH[1].n)
                EI("sync", dH_ST, lambda e, s=s, hh=hh, jb=jb: e.dma_start(
                    out=h_hbm[s][:, (hh - 3) * 512:(hh + 1) * 512],
                    in_=s_hst[jb][:]), 16)

    fence()
    if _PH <= 5:
        return _finish()

    dSP = C("dSP")
    for jc in range(NHC):
        EI("sync", dSP, lambda e, jc=jc: e.dma_start(
            out=s_SpT[:, jc * D:(jc + 1) * D],
            in_=sp_hbm[:, jc * 128:(jc + 1) * 128], transpose=True), 16)
    dH_LD = C("dH_LD")
    dX1_LD = C("dX1_LD")
    dY = C("dY")
    pc3 = [0]
    for t in range(NT):
        s, tb = t // 4, t % 4
        if t >= 2:
            W("sync", vOU[0], vOU[0].n)

        def ht_in(e, s=s, tb=tb, j=t % 2):
            src = h_hbm[s].rearrange("p (c q) -> p c q", c=NHC, q=SS)
            src = src[:, :, tb * 128:(tb + 1) * 128]
            return e.dma_start(out=s_ht[j][:, :].rearrange(
                "p (c q) -> p c q", c=NHC, q=128), in_=src)
        EI("sync", dH_LD, ht_in, 16)
        EI("sync", dX1_LD, lambda e, t=t: e.dma_start(
            out=s_xf[t % 2][:], in_=x1_hbm[t * 128:(t + 1) * 128, :]), 16)
        W("tensor", dSP, 16 * NHC)
        W("tensor", dH_LD, 16 * (t + 1))
        W("tensor", vOU[0], vOU[0].n)
        pb = [(pSC[0], 0), (pSC[0], 512), (pSC[1], 0), (pSC[1], 512)]
        for jc in range(NHC):
            for n in range(4):
                bt, bo = pb[n]
                fn = lambda e, jc=jc, n=n, bt=bt, bo=bo, jj=t % 2: e.matmul(
                    out=bt[:, bo:bo + 512],
                    lhsT=s_ht[jj][:, jc * 128:(jc + 1) * 128],
                    rhs=s_SpT[:, jc * D + n * 512:jc * D + (n + 1) * 512],
                    start=(jc == 0), stop=(jc == NHC - 1))
                if jc == NHC - 1 and n == 3:
                    EI("tensor", tOU[0], fn)
                else:
                    E("tensor", fn)
        W("vector", tOU[0], tOU[0].n)
        W("vector", dX1_LD, 16 * (t + 1))
        if t >= 1:
            W("vector", dY, 16 * t)

        def evac3b(e, t=t, pb=pb):
            for n in range(4):
                bt, bo = pb[n]
                i = e.tensor_add(out=s_yst[:, n * 512:(n + 1) * 512],
                                 in0=bt[:, bo:bo + 512],
                                 in1=s_xf[t % 2][:, n * 512:(n + 1) * 512])
            return i
        EI("vector", vOU[0], evac3b)
        W("sync", vOU[0], vOU[0].n)
        EI("sync", dY, lambda e, t=t: e.dma_start(
            out=y[t * 128:(t + 1) * 128, :], in_=s_yst[:]), 16)
    W("sync", dY, 16 * NT)
    return _finish()


def _consts():
    inv = 1.0 / (10000.0 ** (np.arange(0, HD, 2, dtype=np.float32) / HD))
    t = np.arange(S, dtype=np.float32)
    f = np.concatenate([inv, inv])
    ang = f[:, None] * t[None, :]
    cosT = np.cos(ang).astype(bfnp)
    sins = np.sin(ang).astype(np.float32)
    sins[:HD // 2] *= -1.0
    sinsT = sins.astype(bfnp)
    jj = np.arange(128)
    mcT = (jj[:, None] <= jj[None, :]).astype(bfnp)
    mfT = (jj[:, None] >= jj[None, :]).astype(bfnp)
    return dict(
        cosT_c=cosT, sinsT_c=sinsT, mcT_c=mcT, mfT_c=mfT,
        ident_c=np.eye(128, dtype=bfnp),
        onescb_c=np.ones((128, 1), dtype=bfnp),
        onesrb_c=np.ones((1, 128), dtype=bfnp),
        onesrf_c=np.ones((1, 128), dtype=np.float32),
        onescf_c=np.ones((128, 1), dtype=np.float32),
        epsc_c=np.full((128, 1), 1e-5, dtype=np.float32),
    )


def kernel(L0_attn_hidden_states, L0_attn_w_q, L0_attn_w_k, L0_attn_w_v,
           L0_attn_w_o, L0_ffn_w_fc, L0_ffn_w_proj):
    x = np.ascontiguousarray(np.asarray(L0_attn_hidden_states, dtype=np.float32))
    base = dict(
        wq=np.ascontiguousarray(L0_attn_w_q, dtype=np.float32),
        wk=np.ascontiguousarray(L0_attn_w_k, dtype=np.float32),
        wv=np.ascontiguousarray(L0_attn_w_v, dtype=np.float32),
        wo=np.ascontiguousarray(L0_attn_w_o, dtype=np.float32),
        wfc=np.ascontiguousarray(L0_ffn_w_fc, dtype=np.float32),
        wpj=np.ascontiguousarray(L0_ffn_w_proj, dtype=np.float32),
        **_consts(),
    )
    nc = build_nc()
    in_maps = [dict(xin=np.ascontiguousarray(x[b]), **base) for b in range(8)]
    res = run_bass_kernel_spmd(nc, in_maps, core_ids=list(range(8)))
    out = np.stack([np.asarray(r["y"], dtype=np.float32) for r in res.results])
    return out
